# version 22
# baseline (speedup 1.0000x reference)
import math
from contextlib import ExitStack

import numpy as np
import ml_dtypes

import concourse.bass as bass
import concourse.mybir as mybir
from concourse.bass_utils import run_bass_kernel_spmd

F32 = mybir.dt.float32
BF16 = mybir.dt.bfloat16
I32 = mybir.dt.int32
AF = mybir.ActivationFunctionType
ALU = mybir.AluOpType
AX = mybir.AxisListType

RS96 = 1.0 / math.sqrt(96.0)
EPS = 1e-5
TWO_PI = 2.0 * math.pi


class CFG:
    def __init__(self, T, DM, CQ, INNER, NHL, n_pairs, SLOT, NS, MEGA=None):
        self.T, self.DM, self.CQ, self.INNER = T, DM, CQ, INNER
        self.NHL, self.n_pairs, self.SLOT, self.NS = NHL, n_pairs, SLOT, NS
        self.DC = 256
        self.KC = DM // 128
        self.CQC = CQ // 128
        self.NKT = T // 128
        self.NSLOT = T // SLOT
        self.OWN = T // 2
        self.YD = NHL * 64
        self.YC = self.YD // 128
        self.QRC = NHL * 32 // 128
        self.IC = INNER // 128
        self.MEGA = MEGA if MEGA is not None else min(512, self.OWN)
        self.NMEGA = self.OWN // self.MEGA
        assert NHL % 2 == 0 and self.YC >= 1 and self.QRC >= 1
        assert SLOT % 128 == 0 and NS <= 512 and T % SLOT == 0 and T % NS == 0


FULL = CFG(T=4096, DM=1024, CQ=4096, INNER=4096, NHL=8, n_pairs=4, SLOT=512, NS=512,
           MEGA=256)
MINI = CFG(T=512, DM=512, CQ=512, INNER=512, NHL=4, n_pairs=1, SLOT=128, NS=128,
           MEGA=128)


class Em:
    STREAMS = ("t", "v", "s", "g", "qd", "gd", "cc")

    def __init__(self, nc, es):
        self.nc = nc
        self.es = es
        self.sem = {k: es.enter_context(nc.semaphore("sem_" + k)) for k in self.STREAMS}
        self.cnt = {k: 0 for k in self.STREAMS}
        self.waited = {}

    def dma_on(self, name, instr):
        if name not in self.sem:
            self.sem[name] = self.es.enter_context(self.nc.semaphore("sem_" + name))
            self.cnt[name] = 0
        return self.do(name, instr, 16)

    def tok(self, s):
        return (s, self.cnt[s])

    def _wait(self, ename, eng, s, c):
        if c <= 0:
            return
        key = (ename, s)
        if self.waited.get(key, 0) >= c:
            return
        eng.wait_ge(self.sem[s], c)
        self.waited[key] = c

    def wait(self, ename, eng, *tokens):
        need = {}
        for t in tokens:
            if t is None:
                continue
            s, cv = t
            if cv > need.get(s, 0):
                need[s] = cv
        for s in sorted(need):
            self._wait(ename, eng, s, need[s])

    def pre(self, ename, eng, *tokens, selfsync=True):
        if selfsync and ename in ("v", "s", "g"):
            self._wait(ename, eng, ename, self.cnt[ename])
        self.wait(ename, eng, *tokens)

    def do(self, s, instr, n=1):
        instr.then_inc(self.sem[s], n)
        self.cnt[s] += n
        return (s, self.cnt[s])

    def dma(self, s, instr):
        return self.do(s, instr, 16)

    def all_tokens(self):
        return [(s, cv) for s, cv in self.cnt.items() if cv > 0]


def barrier(em, block):
    toks = em.all_tokens()
    for name, dec in (("t", block.tensor), ("v", block.vector), ("s", block.scalar),
                      ("g", block.gpsimd), ("qd", block.sync)):
        def body(eng, _t=tuple(toks), _n=name):
            em.wait(_n, eng, *_t)
        dec(body)


def build_nc(cfg, dbg=False):
    c = cfg
    nc = bass.Bass("TRN2", target_bir_lowering=False)

    x_seq = nc.dram_tensor("x_seq", [c.T, c.DM], F32, kind="ExternalInput").ap()
    x_own = nc.dram_tensor("x_own", [c.OWN, c.DM], F32, kind="ExternalInput").ap()
    wdq = nc.dram_tensor("wdq", [c.DM, c.CQ], F32, kind="ExternalInput").ap()
    wuq = nc.dram_tensor("wuq", [c.CQ, c.YD], F32, kind="ExternalInput").ap()
    wqr = nc.dram_tensor("wqr", [c.CQ, c.NHL * 32], F32, kind="ExternalInput").ap()
    wdkv = nc.dram_tensor("wdkv", [c.DM, c.DC], F32, kind="ExternalInput").ap()
    wuk = nc.dram_tensor("wuk", [c.DC, c.YD], F32, kind="ExternalInput").ap()
    wuv = nc.dram_tensor("wuv", [c.DC, c.YD], F32, kind="ExternalInput").ap()
    wkr = nc.dram_tensor("wkr", [c.DM, 32], F32, kind="ExternalInput").ap()
    wo = nc.dram_tensor("wo", [c.YD, c.DM], F32, kind="ExternalInput").ap()
    wg = nc.dram_tensor("wg", [c.INNER, c.DM], F32, kind="ExternalInput").ap()
    wu = nc.dram_tensor("wu", [c.INNER, c.DM], F32, kind="ExternalInput").ap()
    wd = nc.dram_tensor("wd", [c.DM, c.INNER], F32, kind="ExternalInput").ap()
    out = nc.dram_tensor("out", [c.OWN, c.DM], F32, kind="ExternalOutput").ap()
    if dbg:
        d_lnx = nc.dram_tensor("d_lnx", [c.DM, c.T], BF16, kind="ExternalOutput").ap()
        d_qcs = nc.dram_tensor("d_qcs", [c.YD, c.T], BF16, kind="ExternalOutput").ap()
        d_qrs = nc.dram_tensor("d_qrs", [c.NHL * 32, c.T], BF16, kind="ExternalOutput").ap()
        d_aop = nc.dram_tensor("d_aop", [2, c.DM, c.OWN], F32, kind="ExternalOutput").ap()
        d_aog = nc.dram_tensor("d_aog", [c.DM, c.OWN], F32, kind="ExternalOutput").ap()
        d_maxk = nc.dram_tensor("d_maxk", [c.NHL, 1], F32, kind="ExternalOutput").ap()
        d_cq = nc.dram_tensor("d_cq", [c.CQ, 512], BF16, kind="ExternalOutput").ap()
        d_scale = nc.dram_tensor("d_scale", [32, c.T], F32, kind="ExternalOutput").ap()
        d_ini = nc.dram_tensor("d_ini", [32, 2050], F32, kind="ExternalOutput").ap()
        d_sv = nc.dram_tensor("d_sv", [128, 6], F32, kind="ExternalOutput").ap()
        d_wgt = nc.dram_tensor("d_wgt", [c.DM, 512], BF16, kind="ExternalOutput").ap()
        d_wdt = nc.dram_tensor("d_wdt", [512, c.DM], BF16, kind="ExternalOutput").ap()
        d_hh = nc.dram_tensor("d_hh", [128, c.INNER], F32, kind="ExternalOutput").ap()
        d_r1 = nc.dram_tensor("d_r1", [128, c.DM], F32, kind="ExternalOutput").ap()
        d_al = nc.dram_tensor("d_al", [128, 3 * (c.MEGA // 128)], F32, kind="ExternalOutput").ap()
        d_scp4 = nc.dram_tensor("d_scp4", [128, c.T], F32, kind="ExternalOutput").ap()

    lnx_s = nc.dram_tensor("lnx_s", [c.DM, c.T], BF16, kind="Internal").ap()
    cq_s = nc.dram_tensor("cq_s", [c.CQ, c.T], BF16, kind="Internal").ap()
    qs_s = nc.dram_tensor("qs_s", [c.NHL * 128, c.T], BF16, kind="Internal").ap()
    aop_s = nc.dram_tensor("aop_s", [2, c.DM, c.OWN], F32, kind="Internal").ap()
    aog_s = nc.dram_tensor("aog_s", [c.DM, c.OWN], F32, kind="Internal").ap()
    wdq_bf = nc.dram_tensor("wdq_bf", [c.DM, c.CQ], BF16, kind="Internal").ap()
    wuq_bf = nc.dram_tensor("wuq_bf", [c.CQ, c.YD], BF16, kind="Internal").ap()
    wqr_bf = nc.dram_tensor("wqr_bf", [c.CQ, c.NHL * 32], BF16, kind="Internal").ap()
    wo_bf = nc.dram_tensor("wo_bf", [c.YD, c.DM], BF16, kind="Internal").ap()
    wdkv_bf = nc.dram_tensor("wdkv_bf", [c.DM, c.DC], BF16, kind="Internal").ap()
    wuk_bf = nc.dram_tensor("wuk_bf", [c.DC, c.YD], BF16, kind="Internal").ap()
    wuv_bf = nc.dram_tensor("wuv_bf", [c.DC, c.YD], BF16, kind="Internal").ap()
    wkr_bf = nc.dram_tensor("wkr_bf", [c.DM, 32], BF16, kind="Internal").ap()
    wgt_s = nc.dram_tensor("wgt_s", [c.DM, c.INNER], BF16, kind="Internal").ap()
    wut_s = nc.dram_tensor("wut_s", [c.DM, c.INNER], BF16, kind="Internal").ap()
    wdt_s = nc.dram_tensor("wdt_s", [c.INNER, c.DM], BF16, kind="Internal").ap()

    lnx_sv = lnx_s.rearrange("(c p) t -> p c t", p=128)
    cq_sv = cq_s.rearrange("(o p) t -> p o t", p=128)
    qs_sv = qs_s.rearrange("(o p) t -> p o t", p=128)
    groups = [[2 * p, 2 * p + 1] for p in range(c.n_pairs)]

    es = ExitStack()
    sb = lambda n, sh, dt, **kw: es.enter_context(nc.sbuf_tensor(n, sh, dt, **kw))
    ps = lambda n, sh, dt: es.enter_context(nc.psum_tensor(n, sh, dt))
    with es:
        em = Em(nc, es)
        pb = [ps(f"pb{i}", [128, 512], F32) for i in range(6)]
        ptp = ps("ptp", [128, 1024], BF16)
        ptp2 = ps("ptp2", [128, 1024], BF16)
        PT = 6

        ident = sb("ident", [128, 128], BF16)
        tri = sb("tri", [128, 128], BF16)
        ones_f = sb("ones_f", [128, 128], F32)
        ident_f = sb("ident_f", [128, 128], F32)
        sv_m = sb("sv_m", [128, 3], F32)
        mv_m = sb("mv_m", [128, 3], F32)
        att_es = ExitStack()
        sba = lambda n, sh, dt, **kw: att_es.enter_context(nc.sbuf_tensor(n, sh, dt, **kw))
        ehot = sba("ehot", [c.NHL, c.NHL * 128], BF16)
        ehot_v = ehot[:].rearrange("p (h f) -> p h f", f=128)
        hkh = sba("hkh", [128, c.NHL * c.NHL], BF16)
        maxk = sba("maxk", [c.NHL, 1], F32)
        Kh = [sba(f"Kh{i}", [128, c.T], BF16) for i in range(c.NHL)]
        Vp = sba("Vp", [128, c.NKT * c.NHL * 65], BF16)
        Vp_v = Vp[:].rearrange("p (k h d) -> p k h d", h=c.NHL, d=65)
        scp_c = sba("scp_c", [32, c.T], F32)
        QW = 2048
        quant_es = ExitStack()
        sbq = lambda n, sh, dt: quant_es.enter_context(nc.sbuf_tensor(n, sh, dt))
        wnat3 = [sbq(f"wnat3{i}", [128, QW], F32) for i in range(1)]
        zf3 = sbq("zf3", [128, 1024], BF16)
        wabs3 = sbq("wabs3", [128, 4], F32)
        wsc3 = sbq("wsc3", [1, 2], F32)
        wstg3 = sbq("wstg3", [128, 1024], BF16)
        scale_es = ExitStack()
        scale_t = scale_es.enter_context(nc.sbuf_tensor("scale_t", [32, c.T], F32))
        wn2_es = ExitStack()
        wnat3.append(wn2_es.enter_context(nc.sbuf_tensor("wnat3x", [128, QW], F32)))

        block = es.enter_context(nc.Block())

        ini_es = ExitStack()
        sbi = lambda n, sh, dt: ini_es.enter_context(nc.sbuf_tensor(n, sh, dt))
        io_pf = sbi("io_pf", [128, 128], I32)
        io_j = sbi("io_j", [32, 1], I32)
        io_pos = sbi("io_pos", [1, 512], I32)
        pos_f = sbi("pos_f", [1, 512], F32)
        jf = sbi("jf", [32, 2], F32)
        invf = sbi("invf", [32, 1], F32)
        emb = sbi("emb", [32, 512], F32)
        rr = sbi("rr", [32, 512], F32)
        u32 = sbi("u32", [32, 512], I32)
        uf = sbi("uf", [32, 512], F32)
        sint = sbi("sint", [32, 512], F32)
        cpl = sbi("cpl", [32, 512], F32)
        pi4 = sbi("pi4", [32, 1], F32)
        io_e = sbi("io_e", [c.NHL, 128], I32)

        def ini_g(g):
            g.iota(io_pf[:], pattern=[[-1, 128]], base=0, channel_multiplier=1)
            g.iota(io_j[:], pattern=[[0, 1]], base=0, channel_multiplier=1)
            em.do("g", g.iota(io_e[:], pattern=[[0, 128]], base=0, channel_multiplier=1))
        block.gpsimd(ini_g)
        t_iota = em.tok("g")

        def ini_v(v):
            em.pre("v", v, t_iota)
            em.do("v", v.tensor_scalar(ident[:], io_pf[:], 0, None, ALU.is_equal))
            em.pre("v", v)
            em.do("v", v.memset(pi4[:], math.pi / 4.0))
            em.pre("v", v)
            em.do("v", v.tensor_scalar(tri[:], io_pf[:], 0, None, ALU.is_le))
            em.pre("v", v)
            em.do("v", v.tensor_scalar(ident_f[:], io_pf[:], 0, None, ALU.is_equal))
            em.pre("v", v)
            em.do("v", v.memset(ones_f[:], 1.0))
            em.pre("v", v)
            em.do("v", v.tensor_copy(jf[:, 0:1], io_j[:]))
            em.pre("v", v)
            em.do("v", v.tensor_scalar(jf[:, 1:2], jf[:, 0:1], 16.0, None, ALU.is_ge))
            em.pre("v", v)
            em.do("v", v.scalar_tensor_tensor(jf[:, 0:1], jf[:, 1:2], -16.0, jf[:, 0:1],
                                              ALU.mult, ALU.add))
            for h in range(c.NHL):
                em.pre("v", v)
                em.do("v", v.tensor_scalar(ehot_v[:, h, :], io_e[:], h, None, ALU.is_equal))
            em.pre("v", v)
            em.do("v", v.memset(maxk[:], 0.0))
            em.pre("v", v)
            em.do("v", v.memset(hkh[:], 0.0))
            for h in range(c.NHL):
                em.pre("v", v)
                em.do("v", v.memset(hkh[0:96, h * c.NHL + h: h * c.NHL + h + 1], 1.0))
            for h in range(c.NHL):
                em.pre("v", v)
                em.do("v", v.memset(Kh[h][:], 0.0))
        block.vector(ini_v)
        t_vini = em.tok("v")

        def ini_s(s):
            em.pre("s", s, t_vini)
            em.do("s", s.activation(invf[:], jf[:, 0:1], AF.Exp,
                                    scale=-math.log(10000.0) / 16.0))
        block.scalar(ini_s)
        t_invf = em.tok("s")

        wsl = min(512, c.T)
        nsl = c.T // wsl
        tok_posf_rd = None
        tok_iop_rd = None
        for i in range(nsl):
            def ini_gp(g, i=i):
                em.pre("g", g, tok_iop_rd)
                em.do("g", g.iota(io_pos[0:1, 0:wsl], pattern=[[1, wsl]], base=i * wsl,
                                  channel_multiplier=0))
            block.gpsimd(ini_gp)
            t_iop = em.tok("g")

            def ini_vp(v, i=i):
                em.pre("v", v, t_iop, tok_posf_rd)
                em.do("v", v.tensor_copy(pos_f[0:1, 0:wsl], io_pos[0:1, 0:wsl]))
            block.vector(ini_vp)
            tok_iop_rd = em.tok("v")
            t_pf = em.tok("v")

            def ini_tp(t, i=i):
                em.wait("t", t, t_invf, t_vini, t_pf)
                em.do("t", t.matmul(pb[0][0:32, 0:wsl], ones_f[0:1, 0:32],
                                    pos_f[0:1, 0:wsl], start=True, stop=True))
            block.tensor(ini_tp)
            tok_posf_rd = em.tok("t")
            t_posbc = em.tok("t")

            def ini_v3(v, i=i):
                em.pre("v", v, t_posbc)
                em.do("v", v.tensor_scalar(emb[:, 0:wsl], pb[0][0:32, 0:wsl],
                                           invf[:], None, ALU.mult))
                em.pre("v", v)
                em.do("v", v.tensor_scalar(u32[:, 0:wsl], emb[:, 0:wsl], 1.0 / TWO_PI, 0.125,
                                           ALU.mult, ALU.add))
                em.pre("v", v)
                em.do("v", v.tensor_copy(uf[:, 0:wsl], u32[:, 0:wsl]))
                em.pre("v", v)
                em.do("v", v.scalar_tensor_tensor(rr[:, 0:wsl], emb[:, 0:wsl], 1.0 / TWO_PI,
                                                  uf[:, 0:wsl], ALU.mult, ALU.subtract))
                em.pre("v", v)
                em.do("v", v.tensor_scalar(cpl[:, 0:wsl], rr[:, 0:wsl], 0.375, None, ALU.is_gt))
                em.pre("v", v)
                em.do("v", v.tensor_scalar(rr[:, 0:wsl], rr[:, 0:wsl], -0.625, None, ALU.is_lt))
                em.pre("v", v)
                em.do("v", v.scalar_tensor_tensor(cpl[:, 0:wsl], rr[:, 0:wsl], -1.0,
                                                  cpl[:, 0:wsl], ALU.mult, ALU.add))
                em.pre("v", v)
                em.do("v", v.tensor_tensor(uf[:, 0:wsl], uf[:, 0:wsl], cpl[:, 0:wsl], ALU.add))
                em.pre("v", v)
                em.do("v", v.scalar_tensor_tensor(rr[:, 0:wsl], uf[:, 0:wsl], -TWO_PI,
                                                  emb[:, 0:wsl], ALU.mult, ALU.add))
            block.vector(ini_v3)
            t_rr = em.tok("v")

            def ini_s2(s, i=i):
                em.pre("s", s, t_rr)
                em.do("s", s.activation(sint[:, 0:wsl], rr[:, 0:wsl], AF.Sin,
                                        bias=pi4[:], scale=1.0))
            block.scalar(ini_s2)
            t_sin = em.tok("s")

            def ini_v4(v, i=i):
                em.pre("v", v, t_sin)
                em.do("v", v.tensor_scalar(scale_t[:, i * wsl:(i + 1) * wsl], sint[:, 0:wsl],
                                           math.sqrt(2.0), None, ALU.mult))
                em.pre("v", v)
                em.do("v", v.tensor_scalar(scp_c[:, i * wsl:(i + 1) * wsl], sint[:, 0:wsl],
                                           math.sqrt(2.0) * RS96, None, ALU.mult))
            block.vector(ini_v4)
            if dbg and i == nsl - 1:
                def ini_dbg(g, i=i):
                    em.wait("gd", g, em.tok("v"))
                    em.dma("gd", g.dma_start(out=d_ini[:, 0:512], in_=emb[:, 0:512]))
                    em.dma("gd", g.dma_start(out=d_ini[:, 512:1024], in_=rr[:, 0:512]))
                    em.dma("gd", g.dma_start(out=d_ini[:, 1024:1536], in_=uf[:, 0:512]))
                    em.dma("gd", g.dma_start(out=d_ini[:, 1536:2048], in_=sint[:, 0:512]))
                    em.dma("gd", g.dma_start(out=d_ini[:, 2048:2049], in_=invf[:], allow_slow_non_contiguous=True))
                    em.dma("gd", g.dma_start(out=d_ini[:, 2049:2050], in_=jf[:, 0:1], allow_slow_non_contiguous=True))
                block.gpsimd(ini_dbg)
        t_scp = em.tok("v")

        def ini_g2(g):
            em.wait("gd", g, t_scp)
            em.pre("g", g)
            em.do("g", g.memset(Vp_v[:, :, :, 64:65], 1.0))
            em.dma_on("wbs", g.dma_start(out=wdkv_bf, in_=wdkv))
            em.dma_on("wbs", g.dma_start(out=wuk_bf, in_=wuk))
            em.dma_on("wbs", g.dma_start(out=wuv_bf, in_=wuv))
            em.dma_on("wbs", g.dma_start(out=wkr_bf, in_=wkr))
            em.dma_on("wbl", g.dma_start(out=wdq_bf, in_=wdq))
            em.dma_on("wbl", g.dma_start(out=wuq_bf, in_=wuq))
            em.dma_on("wbl", g.dma_start(out=wqr_bf, in_=wqr))
            em.dma_on("wbl", g.dma_start(out=wo_bf, in_=wo))
        block.gpsimd(ini_g2)
        t_wbs = em.tok("wbs")
        t_wbl = em.tok("wbl")
        t_scp4 = em.tok("gd")
        t_vones = em.tok("g")

        barrier(em, block)
        ini_es.close()

        q_state = {"wn": [None, None], "wn_use": [None, None], "zf_rd": None,
                   "zb_rd": [None, None], "stg": [None] * 4, "pb3": None,
                   "i": 0}
        wgt_v3 = wgt_s.rearrange("(j p) r -> p j r", p=128)
        wut_v3 = wut_s.rearrange("(j p) r -> p j r", p=128)
        wdt_v3 = wdt_s.rearrange("(j p) r -> p j r", p=128)

        def _mk_units():
            units = []
            specs = [(wg, wgt_v3, c.INNER, c.DM, 0), (wu, wut_v3, c.INNER, c.DM, 1),
                     (wd, wdt_v3, c.DM, c.INNER, 2)]

            def u_clear():
                def f():
                    def b(v):
                        em.pre("v", v)
                        em.do("v", v.memset(wabs3[:, 0:3], 0.0))
                    block.vector(b)
                return (None, f)

            def u_p1(wsrc, r0, nr, cc0, ncol, sl, mi):
                def fl():
                    def b_l(sy):
                        em.wait("qd", sy, q_state["wn_use"][sl])
                        q_state["wn"][sl] = em.dma_on(f"qn{sl}", sy.dma_start(
                            out=wnat3[sl][:, 0:nr * ncol].rearrange("p (j f) -> p j f", j=nr),
                            in_=wsrc[r0 * 128:(r0 + nr) * 128, cc0:cc0 + ncol].rearrange(
                                "(j p) f -> p j f", p=128)))
                    block.sync(b_l)

                def f():
                    def b_a(v):
                        em.pre("v", v, q_state["wn"][sl])
                        em.do("v", v.reduce_sum(wabs3[:, 3:4], wnat3[sl][:, 0:nr * ncol],
                                                axis=AX.X, apply_absolute_value=True))
                        em.pre("v", v)
                        q_state["wn_use"][sl] = em.do("v", v.tensor_tensor(
                            wabs3[:, mi:mi + 1], wabs3[:, mi:mi + 1], wabs3[:, 3:4],
                            ALU.add))
                    block.vector(b_a)
                return (fl, f)

            def u_fin(mi, RC):
                def f():
                    def b_s(t):
                        em.wait("t", t, em.tok("v"), q_state["pb3"])
                        em.do("t", t.matmul(pb[4][0:1, 0:1], wabs3[:, mi:mi + 1],
                                            ones_f[:, 0:1], start=True, stop=True))
                    block.tensor(b_s)
                    t_sm = em.tok("t")

                    def b_m(v):
                        em.pre("v", v, t_sm)
                        em.do("v", v.tensor_scalar(wsc3[0:1, 0:1], pb[4][0:1, 0:1],
                                                   1.0 / RC, 1e-5, ALU.mult, ALU.max))
                        em.pre("v", v)
                        em.do("v", v.reciprocal(wsc3[0:1, 1:2], wsc3[0:1, 0:1]))
                    block.vector(b_m)
                    t_mn = em.tok("v")

                    def b_b(t):
                        em.wait("t", t, t_mn)
                        t.matmul(pb[4][:, 0:1], ones_f[0:1, :], wsc3[0:1, 1:2],
                                 start=True, stop=True)
                        em.do("t", t.matmul(pb[4][:, 1:2], ones_f[0:1, :], wsc3[0:1, 0:1],
                                            start=True, stop=True))
                    block.tensor(b_b)
                    t_b1 = em.tok("t")

                    def b_c(v):
                        em.pre("v", v, t_b1)
                        em.do("v", v.tensor_copy(sv_m[:, mi:mi + 1], pb[4][:, 0:1]))
                        em.pre("v", v)
                        em.do("v", v.tensor_copy(mv_m[:, mi:mi + 1], pb[4][:, 1:2]))
                    block.vector(b_c)
                    q_state["pb3"] = em.tok("v")
                return (None, f)

            def u_p2(wsrc, wdst_v, mi, r0, nr, cc0, ncol, cc_total, sl):
                def fl():
                    wn3b = wnat3[0] if sl == 0 else wnat3[-1]

                    def b_l(sy):
                        em.wait("qd", sy, q_state["wn_use"][sl])
                        q_state["wn"][sl] = em.dma_on(f"qn{sl}", sy.dma_start(
                            out=wn3b[:, 0:nr * ncol].rearrange("p (j f) -> p j f", j=nr),
                            in_=wsrc[r0 * 128:(r0 + nr) * 128, cc0:cc0 + ncol].rearrange(
                                "(j p) f -> p j f", p=128)))
                    block.sync(b_l)

                def f():
                    wn3b = wnat3[0] if sl == 0 else wnat3[-1]
                    nrd = max(1, (nr * ncol) // 1024)
                    strip = min(1024, nr * ncol)
                    ccn = max(1, strip // ncol)
                    jbn = min(ncol, strip) // 128
                    for rd in range(nrd):
                        goff = cc0 + (rd * strip) % ncol
                        gch = r0 + (rd * strip) // ncol
                        jblk = goff // 128
                        ws_ = rd % 2
                        pdst = ptp if ws_ == 0 else ptp2

                        def b_q(v, rd=rd, strip=strip):
                            em.pre("v", v, q_state["wn"][sl], q_state["zb_rd"][0])
                            em.do("v", v.tensor_scalar(
                                zf3[:, 0:strip], wn3b[:, rd * strip:(rd + 1) * strip],
                                sv_m[:, mi:mi + 1], 0.5, ALU.mult, ALU.is_gt))
                            em.pre("v", v)
                            tk = em.do("v", v.tensor_scalar(
                                wn3b[:, rd * strip:(rd + 1) * strip],
                                wn3b[:, rd * strip:(rd + 1) * strip],
                                sv_m[:, mi:mi + 1], -0.5, ALU.mult, ALU.is_lt))
                            if rd == nrd - 1:
                                q_state["wn_use"][sl] = tk
                            em.pre("v", v)
                            em.do("v", v.tensor_tensor(
                                zf3[:, 0:strip], zf3[:, 0:strip],
                                wn3b[:, rd * strip:(rd + 1) * strip], ALU.subtract))
                        block.vector(b_q)
                        t_zb = em.tok("v")

                        def b_t(t, strip=strip, ws_=ws_, pdst=pdst, t_zb=t_zb):
                            em.wait("t", t, t_zb, t_vini, q_state["tp_rd"][ws_])
                            for j in range(strip // 128):
                                ins = t.transpose(pdst[:, j * 128:(j + 1) * 128],
                                                  zf3[:, j * 128:(j + 1) * 128], ident[:])
                            em.do("t", ins)
                        block.tensor(b_t)
                        q_state["zb_rd"][0] = em.tok("t")
                        t_t3 = em.tok("t")

                        def b_e(v, ws_=ws_, pdst=pdst, t_t3=t_t3, strip=strip):
                            em.pre("v", v, t_t3, q_state["stg"][0])
                            em.do("v", v.tensor_copy(wstg3[:, 0:strip], pdst[:, 0:strip]))
                        block.vector(b_e)
                        q_state["tp_rd"][ws_] = em.tok("v")
                        t_e3 = em.tok("v")

                        def b_w(sy, jblk=jblk, gch=gch, t_e3=t_e3,
                                ccn=ccn, jbn=jbn, strip=strip):
                            em.wait("qd", sy, t_e3)
                            tk = None
                            for cc_i in range(ccn):
                                tk = em.dma_on("qw0", sy.dma_start(
                                    out=wdst_v[:, jblk:jblk + jbn,
                                               (gch + cc_i) * 128:(gch + cc_i + 1) * 128],
                                    in_=wstg3[:, cc_i * jbn * 128:(cc_i + 1) * jbn * 128
                                              ].rearrange("p (j r) -> p j r", r=128)))
                            q_state["stg"][0] = tk
                        block.sync(b_w)
                    q_state["i"] += 1
                return (fl, f)

            si = [0]
            units.append(u_clear())
            for (wsrc, wdst_v, R, Cc, mi) in specs:
                nr = max(1, QW // Cc)
                ncol = min(Cc, QW)
                for r0 in range(0, R // 128, nr):
                    for cc0 in range(0, Cc, ncol):
                        units.append(u_p1(wsrc, r0, nr, cc0, ncol, si[0] % 2, mi))
                        si[0] += 1
            fin_start = len(units)
            for (wsrc, wdst_v, R, Cc, mi) in specs:
                units.append(u_fin(mi, float(R) * Cc))
            p2start = len(units)
            for (wsrc, wdst_v, R, Cc, mi) in specs:
                nr = max(1, QW // Cc)
                ncol = min(Cc, QW)
                for r0 in range(0, R // 128, nr):
                    for cc0 in range(0, Cc, ncol):
                        units.append(u_p2(wsrc, wdst_v, mi, r0, nr, cc0, ncol, Cc,
                                          si[0] % 2))
                        si[0] += 1
            return units, fin_start, p2start

        q_state["tp_rd"] = [None, None]
        p3p_units, p3p_fin, p3p_p2start = _mk_units()
        p3p_pos = {"cp": 0, "ld": 0}
        P3P_LA = 2

        def emit_p3p(n, cap=None):
            lim = len(p3p_units) if cap is None else cap
            while n > 0 and p3p_pos["cp"] < lim:
                while (p3p_pos["ld"] < lim
                       and p3p_pos["ld"] < p3p_pos["cp"] + P3P_LA):
                    fl = p3p_units[p3p_pos["ld"]][0]
                    if fl is not None:
                        fl()
                    p3p_pos["ld"] += 1
                fc = p3p_units[p3p_pos["cp"]][1]
                fc()
                p3p_pos["cp"] += 1
                n -= 1


        p1_es = ExitStack()
        sb1 = lambda n, sh, dt: p1_es.enter_context(nc.sbuf_tensor(n, sh, dt))
        wdkv_sb = sb1("wdkv_sb", [128, c.KC * 2 * 128], BF16)
        wdkv_v = wdkv_sb[:].rearrange("p (k o f) -> p k o f", k=c.KC, o=2)
        wuk_sb = sb1("wuk_sb", [128, 2 * c.YD], BF16)
        wuk_v = wuk_sb[:].rearrange("p (k f) -> p k f", k=2)
        wuv_sb = sb1("wuv_sb", [128, 2 * c.YD], BF16)
        wuv_v = wuv_sb[:].rearrange("p (k f) -> p k f", k=2)
        wkr_sb = sb1("wkr_sb", [128, c.KC * 32], BF16)
        wkr_v = wkr_sb[:].rearrange("p (k f) -> p k f", k=c.KC)
        lnt = [sb1(f"p1ln{i}", [128, c.KC * c.NS], BF16) for i in range(2)]
        lnt_v = [x[:].rearrange("p (k t) -> p k t", k=c.KC) for x in lnt]
        ckv = sb1("ckv", [128, 2 * c.NS], BF16)
        ckv_v = ckv[:].rearrange("p (k t) -> p k t", k=2)
        sqb = sb1("sqb", [128, c.NS], BF16)
        krt = sb1("krt", [32, c.NS], BF16)
        knm = sb1("knm", [c.NHL, 1], F32)

        def p1_wload(sy):
            em.wait("qd", sy, t_wbs)
            em.dma_on("p1w", sy.dma_start(out=wdkv_v, in_=wdkv_bf.rearrange(
                "(k p) (o f) -> p k o f", p=128, f=128)))
            em.dma_on("p1w", sy.dma_start(out=wuk_v, in_=wuk_bf.rearrange("(k p) f -> p k f", p=128)))
            em.dma_on("p1w", sy.dma_start(out=wuv_v, in_=wuv_bf.rearrange("(k p) f -> p k f", p=128)))
            em.dma_on("p1w", sy.dma_start(out=wkr_v, in_=wkr_bf.rearrange("(k p) f -> p k f", p=128)))
        block.sync(p1_wload)
        t_p1w = em.tok("p1w")

        tok_ln1 = [None, None]
        tok_lnuse = [None, None]
        tok_ckv_war = None
        tok_pb2_rd = None
        tok_pb3_rd = None
        tok_pb4_rd = None
        tok_pb5_rd = None
        tok_sq_rd = None
        tok_ckv_rd = None

        p0_st = []
        P1R = c.NS // 128

        def emit_P1_tile(it):
            nonlocal tok_ckv_war, tok_pb2_rd, tok_pb3_rd, tok_pb4_rd, tok_pb5_rd
            nonlocal tok_sq_rd, tok_ckv_rd
            sl = it % 2
            ts0 = it * c.NS

            def b_ld(sy, sl=sl, ts0=ts0, it=it):
                em.wait("qd", sy, tok_lnuse[sl], p0_st[it * P1R + P1R - 1],
                        p0_st[it * P1R + P1R - 2] if P1R >= 2 else None)
                tok_ln1[sl] = em.dma_on(f"ql{sl}", sy.dma_start(
                    out=lnt_v[sl], in_=lnx_sv[:, :, ts0:ts0 + c.NS]))
            block.sync(b_ld)

            def b_ckv_mm(t, sl=sl):
                em.wait("t", t, tok_ln1[sl], t_p1w, tok_ckv_war)
                for o in range(2):
                    for kc in range(c.KC):
                        ins = t.matmul(pb[o][:, 0:c.NS], wdkv_v[:, kc, o, :],
                                       lnt_v[sl][:, kc, :],
                                       start=(kc == 0), stop=(kc == c.KC - 1))
                em.do("t", ins)
            block.tensor(b_ckv_mm)
            t_ckvmm = em.tok("t")

            def b_ckv_ev(v):
                em.pre("v", v, t_ckvmm, tok_ckv_rd)
                em.do("v", v.tensor_copy(ckv_v[:, 0, :], pb[0][:, 0:c.NS]))
                em.pre("v", v)
                em.do("v", v.tensor_copy(ckv_v[:, 1, :], pb[1][:, 0:c.NS]))
            block.vector(b_ckv_ev)
            tok_ckv_war = em.tok("v")
            t_ckv = em.tok("v")

            for oc in range(c.NHL // 2):
                def b_kc_mm(t, oc=oc):
                    em.wait("t", t, t_ckv, tok_pb2_rd)
                    for kc in range(2):
                        ins = t.matmul(pb[2][:, 0:c.NS], wuk_v[:, kc, oc * 128:(oc + 1) * 128],
                                       ckv_v[:, kc, :], start=(kc == 0), stop=(kc == 1))
                    em.do("t", ins)
                block.tensor(b_kc_mm)
                t_kcm = em.tok("t")

                def b_kc_ev(v, oc=oc, ts0=ts0):
                    em.pre("v", v, t_kcm)
                    em.do("v", v.tensor_copy(Kh[2 * oc][0:64, ts0:ts0 + c.NS],
                                             pb[2][0:64, 0:c.NS]))
                    em.pre("v", v)
                    em.do("v", v.tensor_copy(Kh[2 * oc + 1][0:64, ts0:ts0 + c.NS],
                                             pb[2][64:128, 0:c.NS]))
                block.vector(b_kc_ev)
                tok_pb2_rd = em.tok("v")

            def b_kr_mm(t, sl=sl):
                em.wait("t", t, tok_ln1[sl], tok_pb4_rd)
                for kc in range(c.KC):
                    ins = t.matmul(pb[4][0:32, 0:c.NS], wkr_v[:, kc, :], lnt_v[sl][:, kc, :],
                                   start=(kc == 0), stop=(kc == c.KC - 1))
                tok_lnuse[sl] = em.do("t", ins)
            block.tensor(b_kr_mm)
            t_krm = em.tok("t")

            def b_kr_ev(v, ts0=ts0):
                em.pre("v", v, t_krm, t_scp)
                em.do("v", v.tensor_tensor(krt[:], pb[4][0:32, 0:c.NS],
                                           scale_t[:, ts0:ts0 + c.NS], ALU.mult))
                for h in range(c.NHL):
                    em.pre("v", v)
                    em.do("v", v.tensor_copy(Kh[h][64:96, ts0:ts0 + c.NS], krt[:]))
            block.vector(b_kr_ev)
            tok_pb4_rd = em.tok("v")
            t_krope = em.tok("v")

            for h in range(c.NHL):
                def b_sq(s, h=h, ts0=ts0):
                    em.pre("s", s, t_krope, tok_sq_rd)
                    em.do("s", s.activation(sqb[:], Kh[h][:, ts0:ts0 + c.NS], AF.Square))
                block.scalar(b_sq)
                t_sq = em.tok("s")

                def b_kn_mm(t, h=h):
                    em.wait("t", t, t_sq, tok_pb3_rd if h == 0 else None)
                    em.do("t", t.matmul(pb[3][0:c.NHL, 0:c.NS],
                                        hkh[:, h * c.NHL:(h + 1) * c.NHL], sqb[:],
                                        start=(h == 0), stop=(h == c.NHL - 1)))
                block.tensor(b_kn_mm)
                tok_sq_rd = em.tok("t")
            t_knmm = em.tok("t")

            def b_kn_max(v):
                em.pre("v", v, t_knmm, t_vini)
                em.do("v", v.reduce_max(knm[:], pb[3][0:c.NHL, 0:c.NS], axis=AX.X))
                em.pre("v", v)
                em.do("v", v.tensor_tensor(maxk[:], maxk[:], knm[:], ALU.max))
            block.vector(b_kn_max)
            tok_pb3_rd = em.tok("v")

            for sub in range(c.NS // 128):
                tc = it * (c.NS // 128) + sub

                def b_v_mm(t, sub=sub):
                    em.wait("t", t, t_ckv, t_vones, tok_pb5_rd)
                    for kc in range(2):
                        ins = t.matmul(pb[5][:, 0:c.YD],
                                       ckv_v[:, kc, sub * 128:(sub + 1) * 128],
                                       wuv_v[:, kc, :], start=(kc == 0), stop=(kc == 1))
                    em.do("t", ins)
                block.tensor(b_v_mm)
                tok_ckv_rd = em.tok("t")
                t_vm = em.tok("t")

                def b_v_ev(v, tc=tc):
                    em.pre("v", v, t_vm)
                    em.do("v", v.tensor_copy(
                        Vp_v[:, tc, :, 0:64],
                        pb[5][:, 0:c.YD].rearrange("p (h d) -> p h d", d=64)))
                block.vector(b_v_ev)
                tok_pb5_rd = em.tok("v")

        p0_es = ExitStack()
        sb0 = lambda n, sh, dt: p0_es.enter_context(nc.sbuf_tensor(n, sh, dt))
        xb = [sb0(f"p0x{i}", [128, c.DM], F32) for i in range(2)]
        lnb = [sb0(f"p0ln{i}", [128, c.DM], BF16) for i in range(2)]
        fmb = [sb0(f"p0fm{i}", [128, c.DM], BF16) for i in range(2)]
        nst = max(1, c.DM // 512)
        wst = c.DM // nst
        st6 = sb0("p0st6", [128, 6 * nst], F32)
        mv2 = [sb0(f"p0mv{i}", [128, 2], F32) for i in range(2)]
        sdv = [sb0(f"p0sd{i}", [128, 3], F32) for i in range(2)]

        tok_xuse = [None, None]
        tok_x = [None, None]
        tok_lnwar = [None, None]
        tok_fmwar = [None, None]
        tok_ptp_ev = None
        t_sd2 = [None, None]

        def p0_front(i):
            sl = i % 2

            def b_load(sy, i=i, sl=sl):
                em.wait("qd", sy, tok_xuse[sl])
                tok_x[sl] = em.dma_on(f"qx{sl}", sy.dma_start(
                    out=xb[sl][:], in_=x_seq[i * 128:(i + 1) * 128, :]))
            block.sync(b_load)

            def b_stats(v, sl=sl):
                em.pre("v", v, tok_x[sl])
                for j in range(nst):
                    em.pre("v", v)
                    em.do("v", v.bn_stats(st6[:, 6 * j:6 * j + 6],
                                          xb[sl][:, j * wst:(j + 1) * wst]))
                em.pre("v", v)
                em.do("v", v.bn_aggr(mv2[sl][:], st6[:]))
                em.pre("v", v)
                em.do("v", v.tensor_scalar(sdv[sl][:, 0:1], mv2[sl][:, 1:2], EPS,
                                           None, ALU.add))
            block.vector(b_stats)
            t_veps = em.tok("v")

            def b_sd(s, sl=sl):
                em.pre("s", s, t_veps)
                em.do("s", s.activation(sdv[sl][:, 0:1], sdv[sl][:, 0:1], AF.Sqrt))
            block.scalar(b_sd)
            t_sd2[sl] = em.tok("s")

        def p0_back(i):
            nonlocal tok_ptp_ev
            sl = i % 2

            def b_rstd(v, sl=sl):
                em.pre("v", v, t_sd2[sl])
                em.do("v", v.reciprocal(sdv[sl][:, 1:2], sdv[sl][:, 0:1]))
                em.pre("v", v)
                em.do("v", v.tensor_scalar(sdv[sl][:, 2:3], mv2[sl][:, 0:1],
                                           sdv[sl][:, 1:2], -1.0, ALU.mult, ALU.mult))
            block.vector(b_rstd)
            t_rstd = em.tok("v")

            def b_apply(s, sl=sl):
                em.pre("s", s, t_rstd, tok_lnwar[sl])
                tok_xuse[sl] = em.do("s", s.activation(
                    lnb[sl][:], xb[sl][:], AF.Identity, bias=sdv[sl][:, 2:3],
                    scale=sdv[sl][:, 1:2]))
            block.scalar(b_apply)
            t_ln = em.tok("s")

            def b_tp(t, sl=sl):
                em.wait("t", t, t_ln, t_vini, tok_ptp_ev)
                for cc_ in range(c.KC):
                    ins = t.transpose(ptp[:, cc_ * 128:(cc_ + 1) * 128],
                                      lnb[sl][:, cc_ * 128:(cc_ + 1) * 128], ident[:])
                tok_lnwar[sl] = em.do("t", ins)
            block.tensor(b_tp)
            t_tp = em.tok("t")

            def b_evac(v, sl=sl):
                em.pre("v", v, t_tp, tok_fmwar[sl])
                em.do("v", v.tensor_copy(fmb[sl][:], ptp[:, 0:c.DM]))
            block.vector(b_evac)
            tok_ptp_ev = em.tok("v")
            t_ev = em.tok("v")

            def b_store(sy, i=i, sl=sl):
                em.wait("qd", sy, t_ev)
                tok_fmwar[sl] = em.dma_on(f"qf{sl}", sy.dma_start(
                    out=lnx_sv[:, :, i * 128:(i + 1) * 128],
                    in_=fmb[sl][:].rearrange("p (c t) -> p c t", c=c.KC)))
            block.sync(b_store)
            p0_st.append(tok_fmwar[sl])
            emit_p3p(2, cap=p3p_fin)
            if (i + 1) % P1R == 0:
                emit_P1_tile((i + 1) // P1R - 1)

        for i in range(c.T // 128):
            p0_front(i)
            if i > 0:
                p0_back(i - 1)
        p0_back(c.T // 128 - 1)
        emit_p3p(len(p3p_units), cap=p3p_fin)

        barrier(em, block)
        p0_es.close()
        p1_es.close()
        wn2_es.close()
        scale_es.close()



        p2a_es = ExitStack()
        sb2a = lambda n, sh, dt: p2a_es.enter_context(nc.sbuf_tensor(n, sh, dt))
        lnxf = sb2a("lnxf", [128, c.KC * c.T], BF16)
        lnxf_v = lnxf[:].rearrange("p (k t) -> p k t", k=c.KC)
        wdq_b = [sb2a(f"wdqb{i}", [128, c.KC * 128], BF16) for i in range(2)]
        wdq_bv = [x[:].rearrange("p (k f) -> p k f", k=c.KC) for x in wdq_b]
        cqt = [sb2a(f"cqt{i}", [128, c.NS], BF16) for i in range(2)]
        wnat3.append(sb2a("wnat3y", [128, QW], F32))

        def p2a_lnx(sy):
            em.dma("qd", sy.dma_start(out=lnxf_v, in_=lnx_sv))
        block.sync(p2a_lnx)
        t_lnxf = em.tok("qd")

        tok_wdq = [None, None]
        tok_wdq_use = [None, None]
        tok_cqt_st = [None] * 4
        tok_pbsc_rd = [None] * 4
        n2 = c.T // c.NS
        for oc in range(c.CQC):
            sl = oc % 2

            def b_w(sy, oc=oc, sl=sl):
                em.wait("qd", sy, tok_wdq_use[sl], t_wbl)
                tok_wdq[sl] = em.dma_on(f"gw{sl}", sy.dma_start(
                    out=wdq_bv[sl],
                    in_=wdq_bf[:, oc * 128:(oc + 1) * 128].rearrange("(k p) f -> p k f", p=128)))
            block.sync(b_w)

            for ns in range(n2):
                psl = ns % 4

                def b_mm(t, oc=oc, sl=sl, ns=ns, psl=psl):
                    em.wait("t", t, tok_wdq[sl], t_lnxf, tok_pbsc_rd[psl])
                    for kc in range(c.KC):
                        ins = t.matmul(pb[psl][:, 0:c.NS], wdq_bv[sl][:, kc, :],
                                       lnxf_v[:, kc, ns * c.NS:(ns + 1) * c.NS],
                                       start=(kc == 0), stop=(kc == c.KC - 1))
                    tk = em.do("t", ins)
                    if ns == n2 - 1:
                        tok_wdq_use[sl] = tk
                block.tensor(b_mm)
                t_mm = em.tok("t")

                def b_ev(v, psl=psl, csl=ns % 2):
                    em.pre("v", v, t_mm, tok_cqt_st[csl])
                    em.do("v", v.tensor_copy(cqt[csl][:], pb[psl][:, 0:c.NS]))
                block.vector(b_ev)
                tok_pbsc_rd[psl] = em.tok("v")
                t_e = em.tok("v")

                def b_st(sy, oc=oc, ns=ns, csl=ns % 2):
                    em.wait("qd", sy, t_e)
                    tok_cqt_st[csl] = em.dma_on(f"qc{csl}", sy.dma_start(
                        out=cq_sv[:, oc, ns * c.NS:(ns + 1) * c.NS], in_=cqt[csl][:]))
                block.sync(b_st)
            emit_p3p(2)
        emit_p3p(len(p3p_units))
        barrier(em, block)
        p2a_es.close()
        quant_es.close()

        p2b_es = ExitStack()
        sb2b = lambda n, sh, dt: p2b_es.enter_context(nc.sbuf_tensor(n, sh, dt))
        wuq_sb = sb2b("wuq_sb", [128, c.CQC * c.YD], BF16)
        wuq_v = wuq_sb[:].rearrange("p (k f) -> p k f", k=c.CQC)
        wqr_sb = sb2b("wqr_sb", [128, c.CQC * c.NHL * 32], BF16)
        wqr_v = wqr_sb[:].rearrange("p (k f) -> p k f", k=c.CQC)
        cq_t = sb2b("cq_t", [128, c.CQC * c.NS], BF16)
        cq_tv = cq_t[:].rearrange("p (k t) -> p k t", k=c.CQC)
        qoutb = [sb2b(f"qout{i}", [128, c.NS], BF16) for i in range(2)]
        scp4 = sb2b("scp4", [128, c.T], BF16)

        def p2b_dup(g):
            em.dma_on("scp", g.dma_start(out=scp4[0:32, :], in_=scp_c[:]))
            for b0 in (32, 64, 96):
                em.dma_on("scp", g.dma_start(out=scp4[b0:b0 + 32, :], in_=scp_c[:]))
        block.gpsimd(p2b_dup)

        def p2b_w(sy):
            em.wait("qd", sy, t_wbl)
            em.dma_on("p2bw", sy.dma_start(out=wuq_v, in_=wuq_bf.rearrange("(k p) f -> p k f", p=128)))
            em.dma_on("p2bw", sy.dma_start(out=wqr_v, in_=wqr_bf.rearrange("(k p) f -> p k f", p=128)))
        block.sync(p2b_w)
        t_p2bw = em.tok("p2bw")

        tok_cqld = None
        tok_qst = [None, None]
        tok_pbq_rd = [None, None]
        nqc = c.NHL // 2
        for ns in range(n2):
            def b_cq(sy, ns=ns):
                em.wait("qd", sy, tok_cqld)
                em.dma_on("ql2", sy.dma_start(out=cq_tv, in_=cq_sv[:, :, ns * c.NS:(ns + 1) * c.NS]))
            block.sync(b_cq)
            t_cq = em.tok("ql2")

            for j in range(nqc + c.QRC):
                psl = j % 2
                is_qr = j >= nqc
                oc = j - nqc if is_qr else j

                def b_mm(t, j=j, oc=oc, is_qr=is_qr, psl=psl):
                    em.wait("t", t, t_cq, t_p2bw, tok_pbq_rd[psl])
                    w_v = wqr_v if is_qr else wuq_v
                    for kc in range(c.CQC):
                        ins = t.matmul(pb[psl][:, 0:c.NS], w_v[:, kc, oc * 128:(oc + 1) * 128],
                                       cq_tv[:, kc, :], start=(kc == 0), stop=(kc == c.CQC - 1))
                    tk = em.do("t", ins)
                    if j == nqc + c.QRC - 1:
                        nonlocal tok_cqld
                        tok_cqld = tk
                block.tensor(b_mm)
                t_mm = em.tok("t")

                if is_qr:
                    def b_ev(v, psl=psl, ns=ns):
                        em.pre("v", v, t_mm, tok_qst[psl], em.tok("scp"))
                        em.do("v", v.tensor_tensor(qoutb[psl][:], pb[psl][:, 0:c.NS],
                                                   scp4[:, ns * c.NS:(ns + 1) * c.NS], ALU.mult))
                    block.vector(b_ev)
                    tok_pbq_rd[psl] = em.tok("v")
                    t_e = em.tok("v")
                else:
                    def b_ev(s, psl=psl):
                        em.pre("s", s, t_mm, tok_qst[psl])
                        em.do("s", s.activation(qoutb[psl][:], pb[psl][:, 0:c.NS],
                                                AF.Identity, scale=RS96))
                    block.scalar(b_ev)
                    tok_pbq_rd[psl] = em.tok("s")
                    t_e = em.tok("s")

                def b_st(sy, oc=oc, is_qr=is_qr, psl=psl, ns=ns):
                    em.wait("qd", sy, t_e)
                    tk = None
                    if is_qr:
                        for j in range(4):
                            tk = em.dma_on(f"qs{psl}", sy.dma_start(
                                out=qs_s[(4 * oc + j) * 128 + 64:(4 * oc + j) * 128 + 96,
                                         ns * c.NS:(ns + 1) * c.NS],
                                in_=qoutb[psl][j * 32:(j + 1) * 32, :]))
                    else:
                        for j in range(2):
                            tk = em.dma_on(f"qs{psl}", sy.dma_start(
                                out=qs_s[(2 * oc + j) * 128:(2 * oc + j) * 128 + 64,
                                         ns * c.NS:(ns + 1) * c.NS],
                                in_=qoutb[psl][j * 64:(j + 1) * 64, :]))
                    tok_qst[psl] = tk
                block.sync(b_st)
        barrier(em, block)
        p2b_es.close()

        p2c_es = ExitStack()
        sb2c = lambda n, sh, dt: p2c_es.enter_context(nc.sbuf_tensor(n, sh, dt))
        qsl = sb2c("qsl", [128, c.NHL * c.SLOT], BF16)
        qh_v = qsl[:].rearrange("p (o t) -> p o t", o=c.NHL)
        sq2 = sb2c("sq2", [128, c.SLOT], BF16)
        qn = sb2c("qn", [c.NHL, c.SLOT], F32)
        qmx = sb2c("qmx", [c.NHL, 1], F32)
        ngc = sb2c("ngc", [c.NHL, 1], F32)
        ngb = sb2c("ngb", [c.NHL, 1], BF16)
        bias_sb = sb2c("bias_sb", [128, c.NHL], F32)
        expA = sb2c("expA", [128, c.NKT * c.SLOT], BF16)
        expA_v = expA[:].rearrange("p (k t) -> p k t", k=c.NKT)
        yun_sb = [sb2c(f"yun{i}", [128, c.SLOT], BF16) for i in range(c.YC)]
        den_f = sb2c("den_f", [1, c.NHL * c.SLOT], F32)
        y_sb = [sb2c(f"ysb{i}", [128, c.SLOT], BF16) for i in range(c.YC)]
        wop_sb = sb2c("wop_sb", [128, c.YC * c.DM], BF16)
        wop_v = wop_sb[:].rearrange("p (k f) -> p k f", k=c.YC)
        aop_t = [sb2c(f"aopt{i}", [128, c.SLOT], F32) for i in range(2)]
        aop_sv = aop_s.rearrange("j (o p) t -> p j o t", p=128)

        def p2c_wo(sy):
            em.wait("qd", sy, t_wbl)
            em.dma_on("wop", sy.dma_start(out=wop_v, in_=wo_bf.rearrange("(k p) f -> p k f", p=128)))
        block.sync(p2c_wo)
        t_wop = em.tok("wop")

        def p2c_zero(v):
            em.pre("v", v)
            em.do("v", v.memset(qsl[:], 0.0))
        block.vector(p2c_zero)
        t_qrz = em.tok("v")

        tok_qsl_use = None
        tok_avprev = None
        tok_yun_rd = None
        pb01_rd = [None, None]
        tok_py_rd = [None, None]
        tok_pb3q_rd = None
        tok_pb4e_rd = None
        tok_ngb_rd = None
        tok_ywr = None
        tok_sq2_rd = None
        tok_aop_st = [None, None]
        for slot in range(c.NSLOT):
            q0 = slot * c.SLOT
            nkt = (q0 + c.SLOT) // 128

            def b_qld(sy, q0=q0):
                em.wait("qd", sy, tok_qsl_use, t_qrz)
                em.dma_on("qq", sy.dma_start(out=qh_v[0:96, :, :],
                                             in_=qs_sv[0:96, :, q0:q0 + c.SLOT]))
            block.sync(b_qld)
            t_qc_ld = em.tok("qq")
            t_qr_ld = t_qc_ld

            for j in range(c.NHL):
                def b_sqq(v, j=j):
                    em.pre("v", v, t_qc_ld, tok_sq2_rd)
                    em.do("v", v.tensor_tensor(sq2[:], qh_v[:, j, :], qh_v[:, j, :], ALU.mult))
                block.vector(b_sqq)
                t_s1 = em.tok("v")

                def b_nm(t, j=j):
                    em.wait("t", t, t_s1, tok_pb3q_rd if j == 0 else None)
                    em.do("t", t.matmul(pb[3][0:c.NHL, 0:c.SLOT],
                                        hkh[:, j * c.NHL:(j + 1) * c.NHL], sq2[:],
                                        start=(j == 0), stop=(j == c.NHL - 1)))
                block.tensor(b_nm)
                tok_sq2_rd = em.tok("t")
            t_qn_mm = em.tok("t")

            def b_ngc(v):
                em.pre("v", v, t_qn_mm, tok_ngb_rd)
                em.do("v", v.reduce_max(qmx[:], pb[3][0:c.NHL, 0:c.SLOT], axis=AX.X))
                em.pre("v", v)
                em.do("v", v.tensor_scalar(ngc[:], qmx[:], maxk[:], None, ALU.mult))
            block.vector(b_ngc)
            tok_pb3q_rd = em.tok("v")
            t_c2 = em.tok("v")

            def b_csq(s):
                em.pre("s", s, t_c2)
                em.do("s", s.activation(ngc[:], ngc[:], AF.Sqrt))
            block.scalar(b_csq)
            t_cs = em.tok("s")

            def b_ng2(v):
                em.pre("v", v, t_cs)
                em.do("v", v.tensor_scalar(ngc[:], ngc[:], -1.0, None, ALU.mult))
                em.pre("v", v)
                em.do("v", v.tensor_copy(ngb[:], ngc[:]))
            block.vector(b_ng2)
            t_ngb = em.tok("v")

            def b_ext(t):
                em.wait("t", t, t_ngb, tok_pb4e_rd)
                for h in range(c.NHL):
                    ins = t.matmul(pb[4][:, h:h + 1], ehot_v[:, h, :], ngb[:],
                                   start=True, stop=True)
                em.do("t", ins)
            block.tensor(b_ext)
            tok_ngb_rd = em.tok("t")
            t_ext = em.tok("t")

            def b_bias(v):
                em.pre("v", v, t_ext)
                em.do("v", v.tensor_copy(bias_sb[:, 0:c.NHL], pb[4][:, 0:c.NHL]))
            block.vector(b_bias)
            tok_pb4e_rd = em.tok("v")
            t_bias = em.tok("v")

            offs = [max(0, kt * 128 - q0) for kt in range(nkt)]
            for h in range(c.NHL):
                pby = pb[5] if h % 2 == 0 else pb[2]
                ybk = h % 2
                sbase = em.cnt["s"]

                def b_sc_all(t, h=h, nkt=nkt, sbase=sbase):
                    em.wait("t", t, t_qc_ld, t_bias, ("s", sbase),
                            pb01_rd[0], pb01_rd[1])
                    for kt in range(nkt):
                        off = offs[kt]
                        psl = kt % 2
                        if kt >= 2:
                            em.wait("t", t, ("s", sbase + kt - 1))
                        ins = t.matmul(pb[psl][:, off:c.SLOT],
                                       Kh[h][0:96, kt * 128:(kt + 1) * 128],
                                       qh_v[0:96, h, off:c.SLOT], start=True, stop=True)
                        em.do("t", ins)
                block.tensor(b_sc_all)
                tbase = em.cnt["t"] - nkt

                def b_exp_all(s, h=h, nkt=nkt, tbase=tbase):
                    em.pre("s", s, tok_avprev)
                    for kt in range(nkt):
                        off = offs[kt]
                        em.wait("s", s, ("t", tbase + kt + 1))
                        em.do("s", s.activation(expA_v[:, kt, off:c.SLOT],
                                                pb[kt % 2][:, off:c.SLOT], AF.Exp,
                                                bias=bias_sb[:, h:h + 1], scale=1.0))
                block.scalar(b_exp_all)
                assert em.cnt["s"] == sbase + nkt

                ndiag = c.SLOT // 128
                vbase = em.cnt["v"]

                def b_msk_all(v, nkt=nkt, sbase=sbase, ndiag=ndiag):
                    for i_, kt in enumerate(range(nkt - ndiag, nkt)):
                        off = offs[kt]
                        em.pre("v", v, ("s", sbase + kt + 1))
                        em.do("v", v.tensor_tensor(expA_v[:, kt, off:off + 128],
                                                   expA_v[:, kt, off:off + 128],
                                                   tri[:], ALU.mult))
                block.vector(b_msk_all)
                assert em.cnt["v"] == vbase + ndiag

                def b_av_all(t, h=h, nkt=nkt, sbase=sbase, vbase=vbase, ndiag=ndiag,
                             pby=pby, ybk=ybk):
                    em.wait("t", t, tok_py_rd[ybk])
                    for kt in range(nkt):
                        off = offs[kt]
                        if kt >= nkt - ndiag:
                            em.wait("t", t, ("v", vbase + ndiag - (nkt - kt) + 1))
                        else:
                            em.wait("t", t, ("s", sbase + kt + 1))
                        ins = t.matmul(pby[0:65, off:c.SLOT], Vp_v[:, kt, h, :],
                                       expA_v[:, kt, off:c.SLOT],
                                       start=(kt == 0), stop=(kt == nkt - 1))
                    em.do("t", ins)
                block.tensor(b_av_all)
                tok_avprev = em.tok("t")
                t_y = em.tok("t")
                if h == c.NHL - 1:
                    tok_qsl_use = t_y

                def b_yev(v, h=h, ch=h // 2, b0=(h % 2) * 64, pby=pby, ybk=ybk):
                    em.pre("v", v, t_y, tok_yun_rd)
                    em.do("v", v.tensor_copy(yun_sb[ch][b0:b0 + 64, 0:c.SLOT],
                                             pby[0:64, 0:c.SLOT]))
                    em.pre("v", v)
                    em.do("v", v.tensor_copy(den_f[0:1, h * c.SLOT:(h + 1) * c.SLOT],
                                             pby[64:65, 0:c.SLOT]))
                block.vector(b_yev)
                tok_py_rd[ybk] = em.tok("v")
            t_yun = em.tok("v")

            def b_rcp_all(v):
                em.pre("v", v, t_yun)
                em.do("v", v.reciprocal(den_f[0:1, :], den_f[0:1, :]))
            block.vector(b_rcp_all)
            t_rcp = em.tok("v")

            def b_bc_all(t):
                em.wait("t", t, t_rcp, pb01_rd[0], pb01_rd[1], tok_pb3q_rd, tok_pb4e_rd)
                for h in range(c.NHL):
                    bank = (pb[0], pb[1], pb[3], pb[4])[h // 2]
                    ins = t.matmul(bank[(h % 2) * 64:(h % 2) * 64 + 64, 0:c.SLOT],
                                   ones_f[0:1, 0:64],
                                   den_f[0:1, h * c.SLOT:(h + 1) * c.SLOT],
                                   start=True, stop=True)
                em.do("t", ins)
            block.tensor(b_bc_all)
            t_bc2 = em.tok("t")

            def b_norm(v):
                em.pre("v", v, t_bc2, tok_ywr)
                for h in range(c.NHL):
                    bank = (pb[0], pb[1], pb[3], pb[4])[h // 2]
                    em.pre("v", v)
                    em.do("v", v.tensor_tensor(
                        y_sb[h // 2][(h % 2) * 64:(h % 2) * 64 + 64, 0:c.SLOT],
                        yun_sb[h // 2][(h % 2) * 64:(h % 2) * 64 + 64, 0:c.SLOT],
                        bank[(h % 2) * 64:(h % 2) * 64 + 64, 0:c.SLOT], ALU.mult))
            block.vector(b_norm)
            tok_yun_rd = em.tok("v")
            tok_pb3q_rd = em.tok("v")
            tok_pb4e_rd = em.tok("v")
            pb01_rd[0] = em.tok("v")
            pb01_rd[1] = em.tok("v")
            t_yall = em.tok("v")

            for oc in range(c.KC):
                psl2 = oc % 2

                def b_wo_mm(t, oc=oc, psl2=psl2):
                    em.wait("t", t, t_yall, t_wop, pb01_rd[psl2])
                    for kc in range(c.YC):
                        ins = t.matmul(pb[psl2][:, 0:c.SLOT],
                                       wop_v[:, kc, oc * 128:(oc + 1) * 128],
                                       y_sb[kc][:, 0:c.SLOT],
                                       start=(kc == 0), stop=(kc == c.YC - 1))
                    tk = em.do("t", ins)
                    if oc == c.KC - 1:
                        nonlocal tok_ywr
                        tok_ywr = tk
                block.tensor(b_wo_mm)
                t_wm = em.tok("t")

                def b_wo_ev(v, oc=oc, psl2=psl2):
                    em.pre("v", v, t_wm, tok_aop_st[psl2])
                    em.do("v", v.tensor_copy(aop_t[psl2][:], pb[psl2][:, 0:c.SLOT]))
                block.vector(b_wo_ev)
                pb01_rd[psl2] = em.tok("v")
                t_ae = em.tok("v")

                def b_wo_st(sy, q0=q0, oc=oc, psl2=psl2):
                    em.wait("qd", sy, t_ae)
                    th = q0 // c.OWN
                    c0 = q0 % c.OWN
                    tok_aop_st[psl2] = em.dma_on(f"qa{psl2}", sy.dma_start(
                        out=aop_sv[:, th, oc, c0:c0 + c.SLOT], in_=aop_t[psl2][:]))
                block.sync(b_wo_st)

        barrier(em, block)

        def b_cc(g):
            em.do("cc", g.collective_compute(
                "ReduceScatter", mybir.AluOpType.add, groups, [aop_s.opt()], [aog_s.opt()]), 1)
        block.gpsimd(b_cc)

        barrier(em, block)
        p2c_es.close()
        att_es.close()


        p3_es = ExitStack()
        sb3 = lambda n, sh, dt, **kw: p3_es.enter_context(nc.sbuf_tensor(n, sh, dt, **kw))
        SUBS = c.MEGA // 128
        KC2 = c.KC // 2
        RW = KC2 * 128
        NW = c.INNER // c.NS
        NON = c.DM // c.NS
        assert NW >= 2 * SUBS and RW <= 512 and c.DM == 2 * RW
        aofm = sb3("aofm", [128, KC2 * 128], F32)
        aofm_v = aofm[:].rearrange("p (k t) -> p k t", k=KC2)
        xt = [sb3(f"xt{i}", [128, c.DM], F32) for i in range(2)]
        res1b = [sb3(f"res1{i}", [128, SUBS * c.DM], F32) for i in range(2)]
        res1v = [x[:].rearrange("p (u f) -> p u f", u=SUBS) for x in res1b]
        ln2x = sb3("ln2x", [128, c.DM], F32)
        qi_b = sb3("qi_b", [128, c.DM], I32)
        qtm_b = sb3("qtm_b", [128, c.DM], BF16)
        zf_b = sb3("zf_b", [128, c.DM], BF16)
        xq_fm = [sb3(f"xq_fm{i}", [128, c.KC * c.MEGA], BF16) for i in range(2)]
        xq_fv = [x[:].rearrange("p (k t) -> p k t", k=c.KC) for x in xq_fm]
        alg = sb3("alg", [128, 2 * SUBS], F32)
        alu_ = sb3("alu", [128, 2 * SUBS], F32)
        gam = sb3("gam", [128, SUBS], F32)
        tmp1 = sb3("tmp1", [128, 4], F32)
        st6b = sb3("st6b", [128, 6 * nst], F32)
        mv2b = sb3("mv2b", [128, 2], F32)
        sdvb = sb3("sdvb", [128, 3], F32)
        gsl = [sb3(f"gsl{i}", [128, c.NS], F32) for i in range(2)]
        wgsl = [sb3(f"wgsl{i}", [128, c.KC * c.NS], BF16) for i in range(2)]
        wgsl_v = [x[:].rearrange("p (k t) -> p k t", k=c.KC) for x in wgsl]
        wusl = [sb3(f"wusl{i}", [128, c.KC * c.NS], BF16) for i in range(2)]
        wusl_v = [x[:].rearrange("p (k t) -> p k t", k=c.KC) for x in wusl]
        hq_fm = sb3("hq_fm", [128, c.IC * c.MEGA], BF16)
        hq_fv = hq_fm[:].rearrange("p (k t) -> p k t", k=c.IC)
        wdsl = [sb3(f"wdsl{i}", [128, c.IC * c.NS], BF16) for i in range(2)]
        wdsl_v = [x[:].rearrange("p (k t) -> p k t", k=c.IC) for x in wdsl]
        hh = sb3("hh", [128, SUBS * c.INNER], F32, side="right")
        hh_v = hh[:].rearrange("p (u f) -> p u f", u=SUBS)

        wgt_sv = wgt_s.rearrange("(k p) t -> p k t", p=128)
        wut_sv = wut_s.rearrange("(k p) t -> p k t", p=128)
        wdt_sv = wdt_s.rearrange("(k p) t -> p k t", p=128)
        aog_sv = aog_s.rearrange("(o p) t -> p o t", p=128)

        st = {
            "yg": None, "yg_use": None,
            "xt": [None, None], "xt_use": [None, None],
            "res1_rd": [None, None],
            "ptp2_rd": None, "ptp_rd": None,
            "xqi_rd": None, "zfb_rd": None, "qtm_rd": None,
            "xq_rd": [None, None], "xq_rdy": [None, None],
            "a_qtm": None,
            "hq_rdy": [None] * SUBS, "hqfm_rd": None,
            "wgu_tok": [None, None], "wgu_use": [None, None], "wg_next": 0,
            "wd_tok": [None, None], "wd_use": [None, None], "wd_next": 0,
            "wd_last": None,
            "hh_rd": None, "hh_done": None,
            "gsl_rd": [None, None], "outb": [None, None],
        }
        pbrd = {i: None for i in range(6)}

        def emit_A_load(mg, sub):
            t0 = mg * c.MEGA + sub * 128
            sl = sub % 2

            def b_ao(sy, t0=t0):
                em.wait("qd", sy, st["yg_use"])
                st["yg"] = em.dma_on("qao", sy.dma_start(
                    out=aofm_v, in_=aog_sv[:, 0:KC2, t0:t0 + 128]))
            block.sync(b_ao)

            def b_xl(sy, t0=t0, sl=sl):
                em.wait("qd", sy, st["xt_use"][sl])
                st["xt"][sl] = em.dma_on("qxo", sy.dma_start(
                    out=xt[sl][:], in_=x_own[t0:t0 + 128, :]))
            block.sync(b_xl)

        def emit_A_main(mg, sub):
            par = mg % 2
            sl = sub % 2
            for r in range(2):
                if r == 1:
                    def b_ao1(sy, mg=mg, sub=sub):
                        em.wait("qd", sy, st["yg_use"])
                        t0 = mg * c.MEGA + sub * 128
                        st["yg"] = em.dma_on("qao", sy.dma_start(
                            out=aofm_v, in_=aog_sv[:, KC2:c.KC, t0:t0 + 128]))
                    block.sync(b_ao1)

                def b_aotp(t):
                    em.wait("t", t, st["yg"], pbrd[4])
                    for oc in range(KC2):
                        ins = t.transpose(pb[4][:, oc * 128:(oc + 1) * 128],
                                          aofm_v[:, oc, :], ident_f[:])
                    st["yg_use"] = em.do("t", ins)
                block.tensor(b_aotp)
                t_aotp = em.tok("t")

                def b_res1(v, r=r, sub=sub, sl=sl, par=par):
                    em.pre("v", v, t_aotp, st["xt"][sl], st["res1_rd"][par])
                    em.do("v", v.tensor_tensor(
                        res1v[par][:, sub, r * RW:(r + 1) * RW],
                        pb[4][:, 0:RW], xt[sl][:, r * RW:(r + 1) * RW], ALU.add))
                block.vector(b_res1)
                pbrd[4] = em.tok("v")
            st["xt_use"][sl] = em.tok("v")
            t_r1 = em.tok("v")

            def b_ln2a(v, sub=sub, par=par):
                em.pre("v", v, t_r1)
                for j in range(nst):
                    em.pre("v", v)
                    em.do("v", v.bn_stats(st6b[:, 6 * j:6 * j + 6],
                                          res1v[par][:, sub, j * wst:(j + 1) * wst]))
                em.pre("v", v)
                em.do("v", v.bn_aggr(mv2b[:], st6b[:]))
                em.pre("v", v)
                em.do("v", v.tensor_scalar(sdvb[:, 0:1], mv2b[:, 1:2], EPS, None, ALU.add))
            block.vector(b_ln2a)
            t_l2a = em.tok("v")

            def b_ln2b(s):
                em.pre("s", s, t_l2a)
                em.do("s", s.activation(sdvb[:, 0:1], sdvb[:, 0:1], AF.Sqrt))
            block.scalar(b_ln2b)
            t_l2b = em.tok("s")

            cidx = par * SUBS + sub

            def b_aq(v, sub=sub, par=par, cidx=cidx):
                em.pre("v", v, t_l2b)
                em.do("v", v.reciprocal(sdvb[:, 1:2], sdvb[:, 0:1]))
                em.pre("v", v)
                em.do("v", v.tensor_scalar(sdvb[:, 2:3], mv2b[:, 0:1], sdvb[:, 1:2],
                                           -1.0, ALU.mult, ALU.mult))
                em.pre("v", v)
                em.do("v", v.tensor_scalar(ln2x[:], res1v[par][:, sub, :],
                                           sdvb[:, 1:2], sdvb[:, 2:3], ALU.mult, ALU.add))
                em.pre("v", v)
                em.do("v", v.reduce_max(tmp1[:, 0:1], ln2x[:], axis=AX.X,
                                        apply_absolute_value=True))
                em.pre("v", v)
                em.do("v", v.tensor_scalar(tmp1[:, 1:2], tmp1[:, 0:1], 1e-5, None, ALU.max))
                em.pre("v", v)
                em.do("v", v.reciprocal(tmp1[:, 2:3], tmp1[:, 1:2]))
                em.pre("v", v)
                em.do("v", v.tensor_scalar(tmp1[:, 3:4], tmp1[:, 2:3], 127.0, None, ALU.mult))
                em.pre("v", v)
                em.do("v", v.tensor_scalar(alg[:, cidx:cidx + 1], tmp1[:, 1:2],
                                           mv_m[:, 0:1], 1.0 / 127.0, ALU.mult, ALU.mult))
                em.pre("v", v)
                em.do("v", v.tensor_scalar(alu_[:, cidx:cidx + 1], tmp1[:, 1:2],
                                           mv_m[:, 1:2], 1.0 / 127.0, ALU.mult, ALU.mult))
                em.pre("v", v, st["xqi_rd"])
                em.do("v", v.tensor_scalar(qi_b[:], ln2x[:], tmp1[:, 3:4], None, ALU.mult))
                em.pre("v", v, st["zfb_rd"])
                em.do("v", v.tensor_copy(zf_b[:], qi_b[:]))
                em.pre("v", v)
                em.do("v", v.scalar_tensor_tensor(ln2x[:], ln2x[:], tmp1[:, 3:4],
                                                  zf_b[:], ALU.mult, ALU.subtract))
                em.pre("v", v, st["qtm_rd"])
                em.do("v", v.tensor_scalar(qtm_b[:], ln2x[:], 0.5, None, ALU.is_gt))
                em.pre("v", v)
                em.do("v", v.tensor_scalar(ln2x[:], ln2x[:], -0.5, None, ALU.is_lt))
                em.pre("v", v)
                em.do("v", v.tensor_tensor(zf_b[:], zf_b[:], qtm_b[:], ALU.add))
                em.pre("v", v)
                em.do("v", v.tensor_tensor(qtm_b[:], zf_b[:], ln2x[:], ALU.subtract))
            block.vector(b_aq)
            st["xqi_rd"] = em.tok("v")
            st["zfb_rd"] = em.tok("v")
            st["a_qtm"] = em.tok("v")

        def emit_A_tp(mg, sub):
            par = mg % 2

            def b_xqtp(t):
                em.wait("t", t, st["a_qtm"], st["ptp2_rd"])
                for oc in range(c.KC):
                    ins = t.transpose(ptp2[:, oc * 128:(oc + 1) * 128],
                                      qtm_b[:, oc * 128:(oc + 1) * 128], ident[:])
                em.do("t", ins)
            block.tensor(b_xqtp)
            st["qtm_rd"] = em.tok("t")
            t_xtp = em.tok("t")

            def b_xqev(v, sub=sub, par=par):
                em.pre("v", v, t_xtp, st["xq_rd"][par])
                em.do("v", v.tensor_copy(
                    xq_fv[par][:, :, sub * 128:(sub + 1) * 128],
                    ptp2[:, 0:c.DM].rearrange("p (k t) -> p k t", k=c.KC)))
            block.vector(b_xqev)
            st["ptp2_rd"] = em.tok("v")

        def emit_wgu_load():
            g = st["wg_next"]
            if g >= c.NMEGA * NW:
                return
            st["wg_next"] = g + 1
            isl = g % NW
            sl = g % 2

            def b_wgu(sy, isl=isl, sl=sl):
                em.wait("qd", sy, st["wgu_use"][sl])
                tg = em.dma_on(f"qg{sl}", sy.dma_start(
                    out=wgsl_v[sl], in_=wgt_sv[:, :, isl * c.NS:(isl + 1) * c.NS]))
                tu = em.dma_on(f"qg2{sl}", sy.dma_start(
                    out=wusl_v[sl], in_=wut_sv[:, :, isl * c.NS:(isl + 1) * c.NS]))
                st["wgu_tok"][sl] = (tg, tu)
            block.sync(b_wgu)

        def emit_wd_load():
            g = st["wd_next"]
            if g >= c.NMEGA * NON:
                return
            st["wd_next"] = g + 1
            on = g % NON
            sl = g % 2

            def b_wd(sy, on=on, sl=sl):
                em.wait("qd", sy, st["wd_use"][sl])
                st["wd_tok"][sl] = em.dma_on(f"qwd{sl}", sy.dma_start(
                    out=wdsl_v[sl], in_=wdt_sv[:, :, on * c.NS:(on + 1) * c.NS]))
            block.sync(b_wd)

        def emit_BC_isl(mg, isl):
            par = mg % 2
            wsl = (mg * NW + isl) % 2

            sfut = em.cnt["s"]
            vfut0 = em.cnt["v"]
            tmm = []
            for sub in range(SUBS):
                gb = 2 * (sub % 2)
                ub = gb + 1

                def b_gu_mm(t, sub=sub, gb=gb, ub=ub, par=par, wsl=wsl):
                    em.wait("t", t, st["wgu_tok"][wsl][0], st["wgu_tok"][wsl][1],
                            st["xq_rdy"][par])
                    if sub < 2:
                        em.wait("t", t, pbrd[gb], pbrd[ub])
                    else:
                        em.wait("t", t, ("s", sfut + (sub - 2) + 1),
                                ("v", vfut0 + (sub - 2) + 1))
                    for kc in range(c.KC):
                        t.matmul(pb[gb][:, 0:c.NS],
                                 xq_fv[par][:, kc, sub * 128:(sub + 1) * 128],
                                 wgsl_v[wsl][:, kc, :], start=(kc == 0),
                                 stop=(kc == c.KC - 1))
                    for kc in range(c.KC):
                        ins = t.matmul(pb[ub][:, 0:c.NS],
                                       xq_fv[par][:, kc, sub * 128:(sub + 1) * 128],
                                       wusl_v[wsl][:, kc, :], start=(kc == 0),
                                       stop=(kc == c.KC - 1))
                    tk = em.do("t", ins)
                    if sub == SUBS - 1:
                        st["wgu_use"][wsl] = tk
                    st["xq_rd"][par] = tk
                block.tensor(b_gu_mm)
                tmm.append(em.tok("t"))
            emit_wgu_load()

            def b_silu(s, par=par):
                for sub in range(SUBS):
                    gb = 2 * (sub % 2)
                    cidx = par * SUBS + sub
                    em.pre("s", s, tmm[sub], st["gsl_rd"][sub % 2],
                           st["outb"][sub % 2])
                    em.do("s", s.activation(gsl[sub % 2][:], pb[gb][:, 0:c.NS], AF.Silu,
                                            scale=alg[:, cidx:cidx + 1]))
            block.scalar(b_silu)
            assert em.cnt["s"] == sfut + SUBS
            for sub in range(SUBS):
                pbrd[2 * (sub % 2)] = ("s", sfut + sub + 1)

            def b_hh(v, isl=isl):
                for sub in range(SUBS):
                    ub = 2 * (sub % 2) + 1
                    em.pre("v", v, ("s", sfut + sub + 1), st["hh_rd"])
                    em.do("v", v.tensor_tensor(
                        hh_v[:, sub, isl * c.NS:(isl + 1) * c.NS], gsl[sub % 2][:],
                        pb[ub][:, 0:c.NS], ALU.mult))
            block.vector(b_hh)
            assert em.cnt["v"] == vfut0 + SUBS
            for sub in range(SUBS):
                pbrd[2 * (sub % 2) + 1] = ("v", vfut0 + sub + 1)
                st["gsl_rd"][sub % 2] = ("v", vfut0 + sub + 1)
            st["hh_done"] = ("v", vfut0 + SUBS)

        npc = max(1, c.INNER // c.DM)
        wpc = c.INNER // npc
        ntp = wpc // 128

        def emit_D_sub(mg, sub):
            par = mg % 2
            cidx = par * SUBS + sub

            def b_hq(v, sub=sub, cidx=cidx):
                em.pre("v", v, st["hh_done"])
                em.do("v", v.reduce_max(tmp1[:, 0:1], hh_v[:, sub, :], axis=AX.X,
                                        apply_absolute_value=True))
                em.pre("v", v)
                em.do("v", v.tensor_scalar(tmp1[:, 1:2], tmp1[:, 0:1],
                                           alu_[:, cidx:cidx + 1], None, ALU.mult))
                em.pre("v", v)
                em.do("v", v.tensor_scalar(tmp1[:, 1:2], tmp1[:, 1:2], 1e-5, None, ALU.max))
                em.pre("v", v)
                em.do("v", v.reciprocal(tmp1[:, 2:3], tmp1[:, 1:2]))
                em.pre("v", v)
                em.do("v", v.tensor_scalar(tmp1[:, 3:4], tmp1[:, 2:3],
                                           alu_[:, cidx:cidx + 1], 127.0, ALU.mult, ALU.mult))
                em.pre("v", v)
                em.do("v", v.tensor_scalar(gam[:, sub:sub + 1], tmp1[:, 1:2],
                                           mv_m[:, 2:3], 1.0 / 127.0, ALU.mult, ALU.mult))
            block.vector(b_hq)

            for pc in range(npc):
                def b_hp(v, sub=sub, pc=pc):
                    em.pre("v", v, st["xqi_rd"])
                    em.do("v", v.tensor_scalar(qi_b[:, 0:wpc],
                                               hh_v[:, sub, pc * wpc:(pc + 1) * wpc],
                                               tmp1[:, 3:4], None, ALU.mult))
                    em.pre("v", v, st["zfb_rd"])
                    em.do("v", v.tensor_copy(zf_b[:, 0:wpc], qi_b[:, 0:wpc]))
                    em.pre("v", v)
                    em.do("v", v.scalar_tensor_tensor(
                        hh_v[:, sub, pc * wpc:(pc + 1) * wpc],
                        hh_v[:, sub, pc * wpc:(pc + 1) * wpc], tmp1[:, 3:4],
                        zf_b[:, 0:wpc], ALU.mult, ALU.subtract))
                    em.pre("v", v, st["qtm_rd"])
                    em.do("v", v.tensor_scalar(qtm_b[:, 0:wpc],
                                               hh_v[:, sub, pc * wpc:(pc + 1) * wpc],
                                               0.5, None, ALU.is_gt))
                    em.pre("v", v)
                    em.do("v", v.tensor_scalar(hh_v[:, sub, pc * wpc:(pc + 1) * wpc],
                                               hh_v[:, sub, pc * wpc:(pc + 1) * wpc],
                                               -0.5, None, ALU.is_lt))
                    em.pre("v", v)
                    em.do("v", v.tensor_tensor(zf_b[:, 0:wpc], zf_b[:, 0:wpc],
                                               qtm_b[:, 0:wpc], ALU.add))
                    em.pre("v", v)
                    em.do("v", v.tensor_tensor(qtm_b[:, 0:wpc], zf_b[:, 0:wpc],
                                               hh_v[:, sub, pc * wpc:(pc + 1) * wpc],
                                               ALU.subtract))
                block.vector(b_hp)
                st["xqi_rd"] = em.tok("v")
                st["zfb_rd"] = em.tok("v")
                t_hqc = em.tok("v")

                def b_hqtp(t):
                    em.wait("t", t, t_hqc, st["ptp_rd"])
                    for j in range(ntp):
                        ins = t.transpose(ptp[:, j * 128:(j + 1) * 128],
                                          qtm_b[:, j * 128:(j + 1) * 128], ident[:])
                    em.do("t", ins)
                block.tensor(b_hqtp)
                st["qtm_rd"] = em.tok("t")
                t_htp = em.tok("t")

                def b_hqev(v, sub=sub, pc=pc):
                    em.pre("v", v, t_htp,
                           st["hqfm_rd"] if (sub == 0 and pc == 0) else None)
                    em.do("v", v.tensor_copy(
                        hq_fv[:, pc * ntp:(pc + 1) * ntp, sub * 128:(sub + 1) * 128],
                        ptp[:, 0:ntp * 128].rearrange("p (k t) -> p k t", k=ntp)))
                block.vector(b_hqev)
                st["ptp_rd"] = em.tok("v")
            st["hq_rdy"][sub] = em.tok("v")
            if sub == SUBS - 1:
                st["hh_rd"] = em.tok("v")

        def emit_E_mm(mg, on, sub):
            sl = sub % 2
            dsl = (mg * NON + on) % 2

            def b_d_mm(t, on=on, sub=sub, sl=sl, dsl=dsl):
                em.wait("t", t, st["wd_tok"][dsl], st["hq_rdy"][sub], pbrd[2 + sl])
                for ic in range(c.IC):
                    ins = t.matmul(pb[2 + sl][:, 0:c.NS],
                                   hq_fv[:, ic, sub * 128:(sub + 1) * 128],
                                   wdsl_v[dsl][:, ic, :], start=(ic == 0),
                                   stop=(ic == c.IC - 1))
                tk = em.do("t", ins)
                if sub == SUBS - 1:
                    st["wd_use"][dsl] = tk
                st["hqfm_rd"] = tk
            block.tensor(b_d_mm)
            return em.tok("t")

        def make_ev(mg, on, sub, t_dm):
            sl = sub % 2
            par = mg % 2

            def run():
                def b_d_ev(v, on=on, sub=sub, sl=sl, par=par):
                    em.pre("v", v, t_dm, st["outb"][sl], st["gsl_rd"][sl])
                    em.do("v", v.scalar_tensor_tensor(
                        gsl[sl][:], pb[2 + sl][:, 0:c.NS], gam[:, sub:sub + 1],
                        res1v[par][:, sub, on * c.NS:(on + 1) * c.NS],
                        ALU.mult, ALU.add))
                block.vector(b_d_ev)
                pbrd[2 + sl] = em.tok("v")
                if on == NON - 1 and sub == SUBS - 1:
                    st["res1_rd"][par] = em.tok("v")
                t_de = em.tok("v")

                def b_out(sy, on=on, sub=sub, sl=sl):
                    em.wait("qd", sy, t_de)
                    t0 = mg * c.MEGA + sub * 128
                    st["outb"][sl] = em.dma_on(f"qo{sl}", sy.dma_start(
                        out=out[t0:t0 + 128, on * c.NS:(on + 1) * c.NS],
                        in_=gsl[sl][:]))
                block.sync(b_out)
            return run

        emit_wgu_load()
        emit_wgu_load()
        emit_wd_load()
        emit_wd_load()
        for sub in range(SUBS):
            emit_A_load(0, sub)
            emit_A_main(0, sub)
            emit_A_tp(0, sub)
        st["xq_rdy"][0] = em.tok("v")

        for mg in range(c.NMEGA):
            nxt = mg + 1

            for isl in range(NW):
                emit_BC_isl(mg, isl)
                if nxt < c.NMEGA:
                    if isl == 0:
                        emit_A_load(nxt, 0)
                    elif isl % 2 == 1:
                        s = isl // 2
                        if s < SUBS:
                            if s >= 1:
                                emit_A_tp(nxt, s - 1)
                            emit_A_main(nxt, s)
                            if s + 1 < SUBS:
                                emit_A_load(nxt, s + 1)
            if nxt < c.NMEGA:
                emit_A_tp(nxt, SUBS - 1)
                st["xq_rdy"][nxt % 2] = em.tok("v")

            for on in range(NON):
                prev = None
                for sub in range(SUBS):
                    if on == 0:
                        emit_D_sub(mg, sub)
                    t_dm = emit_E_mm(mg, on, sub)
                    if prev is not None:
                        prev()
                    prev = make_ev(mg, on, sub, t_dm)
                prev()
                emit_wd_load()

        if dbg:
            def b_dbg(sy):
                em.wait("qd", sy, *em.all_tokens())
                em.dma("qd", sy.dma_start(out=d_sv[:, 0:3], in_=sv_m[:]))
                em.dma("qd", sy.dma_start(out=d_sv[:, 3:6], in_=mv_m[:]))
                em.dma("qd", sy.dma_start(out=d_wgt, in_=wgt_s[:, 0:512]))
                em.dma("qd", sy.dma_start(out=d_wdt, in_=wdt_s[0:512, :]))
                em.dma("qd", sy.dma_start(out=d_hh, in_=hh[:, 0:c.INNER]))
                em.dma("qd", sy.dma_start(out=d_r1, in_=res1b[0][:, 0:c.DM]))
                em.dma("qd", sy.dma_start(out=d_al[:, 0:c.MEGA // 128], in_=alg[:, 0:c.MEGA // 128]))
                em.dma("qd", sy.dma_start(out=d_al[:, c.MEGA // 128:2 * (c.MEGA // 128)], in_=alu_[:, 0:c.MEGA // 128]))
                em.dma("qd", sy.dma_start(out=d_al[:, 2 * (c.MEGA // 128):], in_=gam[:]))
                em.dma("qd", sy.dma_start(out=d_lnx, in_=lnx_s))
                for h in range(c.NHL):
                    em.dma("qd", sy.dma_start(out=d_qcs[h * 64:(h + 1) * 64, :],
                                              in_=qs_s[h * 128:h * 128 + 64, :]))
                    em.dma("qd", sy.dma_start(out=d_qrs[h * 32:(h + 1) * 32, :],
                                              in_=qs_s[h * 128 + 64:h * 128 + 96, :]))
                em.dma("qd", sy.dma_start(out=d_aop, in_=aop_s))
                em.dma("qd", sy.dma_start(out=d_aog, in_=aog_s))
                em.dma("qd", sy.dma_start(out=d_cq, in_=cq_s[:, 0:512]))
            block.sync(b_dbg)

            def b_dbg2(g):
                em.wait("gd", g, *em.all_tokens())
                em.dma("gd", g.dma_start(out=d_maxk, in_=maxk[:]))
            block.gpsimd(b_dbg2)

        barrier(em, block)
        p3_es.close()

    return nc


def _shard_inputs(c, inputs):
    x = np.ascontiguousarray(inputs["L0_attn_x"])
    W = {k: np.ascontiguousarray(v) for k, v in inputs.items()}
    maps = []
    for core in range(2 * c.n_pairs):
        p, hc = core // 2, core % 2
        m = {
            "x_seq": x[p],
            "x_own": x[p, hc * c.OWN:(hc + 1) * c.OWN],
            "wdq": W["L0_attn_W_DQ"],
            "wuq": W["L0_attn_W_UQ"][:, hc * c.YD:(hc + 1) * c.YD],
            "wqr": W["L0_attn_W_QR"][:, hc * c.NHL * 32:(hc + 1) * c.NHL * 32],
            "wdkv": W["L0_attn_W_DKV"],
            "wuk": W["L0_attn_W_UK"][:, hc * c.YD:(hc + 1) * c.YD],
            "wuv": W["L0_attn_W_UV"][:, hc * c.YD:(hc + 1) * c.YD],
            "wkr": W["L0_attn_W_KR"],
            "wo": W["L0_attn_W_O"][hc * c.YD:(hc + 1) * c.YD, :],
            "wg": W["L0_ffn_w_gate"],
            "wu": W["L0_ffn_w_up"],
            "wd": W["L0_ffn_w_down"],
        }
        maps.append({k: np.ascontiguousarray(v) for k, v in m.items()})
    return maps


def kernel(**inputs):
    c = FULL
    nc = build_nc(c)
    in_maps = _shard_inputs(c, inputs)
    res = run_bass_kernel_spmd(nc, in_maps, core_ids=list(range(2 * c.n_pairs)))
    B = c.n_pairs
    outp = np.empty((B, c.T, c.DM), dtype=np.float32)
    for core in range(2 * B):
        p, hc = core // 2, core % 2
        outp[p, hc * c.OWN:(hc + 1) * c.OWN] = res.results[core]["out"]
    return outp


# version 25
# speedup vs baseline: 1.2111x; 1.2111x over previous
import math
from contextlib import ExitStack

import numpy as np
import ml_dtypes

import concourse.bass as bass
import concourse.mybir as mybir
from concourse.bass_utils import run_bass_kernel_spmd

F32 = mybir.dt.float32
BF16 = mybir.dt.bfloat16
I32 = mybir.dt.int32
AF = mybir.ActivationFunctionType
ALU = mybir.AluOpType
AX = mybir.AxisListType

RS96 = 1.0 / math.sqrt(96.0)
EPS = 1e-5
TWO_PI = 2.0 * math.pi


class CFG:
    def __init__(self, T, DM, CQ, INNER, NHL, n_pairs, SLOT, NS, MEGA=None):
        self.T, self.DM, self.CQ, self.INNER = T, DM, CQ, INNER
        self.NHL, self.n_pairs, self.SLOT, self.NS = NHL, n_pairs, SLOT, NS
        self.DC = 256
        self.KC = DM // 128
        self.CQC = CQ // 128
        self.NKT = T // 128
        self.NSLOT = T // SLOT
        self.OWN = T // 2
        self.YD = NHL * 64
        self.YC = self.YD // 128
        self.QRC = NHL * 32 // 128
        self.IC = INNER // 128
        self.MEGA = MEGA if MEGA is not None else min(512, self.OWN)
        self.NMEGA = self.OWN // self.MEGA
        assert NHL % 2 == 0 and self.YC >= 1 and self.QRC >= 1
        assert SLOT % 128 == 0 and NS <= 512 and T % SLOT == 0 and T % NS == 0


FULL = CFG(T=4096, DM=1024, CQ=4096, INNER=4096, NHL=8, n_pairs=4, SLOT=512, NS=512,
           MEGA=256)
MINI = CFG(T=512, DM=512, CQ=512, INNER=512, NHL=4, n_pairs=1, SLOT=128, NS=128,
           MEGA=128)


class Em:
    STREAMS = ("t", "v", "s", "g", "qd", "gd", "cc")

    def __init__(self, nc, es):
        self.nc = nc
        self.es = es
        self.sem = {k: es.enter_context(nc.semaphore("sem_" + k)) for k in self.STREAMS}
        self.cnt = {k: 0 for k in self.STREAMS}
        self.waited = {}

    def dma_on(self, name, instr):
        if name not in self.sem:
            self.sem[name] = self.es.enter_context(self.nc.semaphore("sem_" + name))
            self.cnt[name] = 0
        return self.do(name, instr, 16)

    def tok(self, s):
        return (s, self.cnt[s])

    def _wait(self, ename, eng, s, c):
        if c <= 0:
            return
        key = (ename, s)
        if self.waited.get(key, 0) >= c:
            return
        eng.wait_ge(self.sem[s], c)
        self.waited[key] = c

    def wait(self, ename, eng, *tokens):
        need = {}
        for t in tokens:
            if t is None:
                continue
            s, cv = t
            if cv > need.get(s, 0):
                need[s] = cv
        for s in sorted(need):
            self._wait(ename, eng, s, need[s])

    def pre(self, ename, eng, *tokens, selfsync=True):
        if selfsync and ename in ("v", "s", "g"):
            self._wait(ename, eng, ename, self.cnt[ename])
        self.wait(ename, eng, *tokens)

    def do(self, s, instr, n=1):
        instr.then_inc(self.sem[s], n)
        self.cnt[s] += n
        return (s, self.cnt[s])

    def dma(self, s, instr):
        return self.do(s, instr, 16)

    def all_tokens(self):
        return [(s, cv) for s, cv in self.cnt.items() if cv > 0]


def barrier(em, block):
    toks = em.all_tokens()
    for name, dec in (("t", block.tensor), ("v", block.vector), ("s", block.scalar),
                      ("g", block.gpsimd), ("qd", block.sync)):
        def body(eng, _t=tuple(toks), _n=name):
            em.wait(_n, eng, *_t)
        dec(body)


def build_nc(cfg, dbg=False):
    c = cfg
    nc = bass.Bass("TRN2", target_bir_lowering=False)

    x_seq = nc.dram_tensor("x_seq", [c.T, c.DM], F32, kind="ExternalInput").ap()
    x_own = nc.dram_tensor("x_own", [c.OWN, c.DM], F32, kind="ExternalInput").ap()
    wdq = nc.dram_tensor("wdq", [c.DM, c.CQ], F32, kind="ExternalInput").ap()
    wuq = nc.dram_tensor("wuq", [c.CQ, c.YD], F32, kind="ExternalInput").ap()
    wqr = nc.dram_tensor("wqr", [c.CQ, c.NHL * 32], F32, kind="ExternalInput").ap()
    wdkv = nc.dram_tensor("wdkv", [c.DM, c.DC], F32, kind="ExternalInput").ap()
    wuk = nc.dram_tensor("wuk", [c.DC, c.YD], F32, kind="ExternalInput").ap()
    wuv = nc.dram_tensor("wuv", [c.DC, c.YD], F32, kind="ExternalInput").ap()
    wkr = nc.dram_tensor("wkr", [c.DM, 32], F32, kind="ExternalInput").ap()
    wo = nc.dram_tensor("wo", [c.YD, c.DM], F32, kind="ExternalInput").ap()
    wg = nc.dram_tensor("wg", [c.INNER, c.DM], F32, kind="ExternalInput").ap()
    wu = nc.dram_tensor("wu", [c.INNER, c.DM], F32, kind="ExternalInput").ap()
    wd = nc.dram_tensor("wd", [c.DM, c.INNER], F32, kind="ExternalInput").ap()
    out = nc.dram_tensor("out", [c.OWN, c.DM], F32, kind="ExternalOutput").ap()
    if dbg:
        d_lnx = nc.dram_tensor("d_lnx", [c.DM, c.T], BF16, kind="ExternalOutput").ap()
        d_qcs = nc.dram_tensor("d_qcs", [c.YD, c.T], BF16, kind="ExternalOutput").ap()
        d_qrs = nc.dram_tensor("d_qrs", [c.NHL * 32, c.T], BF16, kind="ExternalOutput").ap()
        d_aop = nc.dram_tensor("d_aop", [2, c.DM, c.OWN], F32, kind="ExternalOutput").ap()
        d_aog = nc.dram_tensor("d_aog", [c.DM, c.OWN], F32, kind="ExternalOutput").ap()
        d_maxk = nc.dram_tensor("d_maxk", [c.NHL, 1], F32, kind="ExternalOutput").ap()
        d_cq = nc.dram_tensor("d_cq", [c.CQ, 512], BF16, kind="ExternalOutput").ap()
        d_scale = nc.dram_tensor("d_scale", [32, c.T], F32, kind="ExternalOutput").ap()
        d_ini = nc.dram_tensor("d_ini", [32, 2050], F32, kind="ExternalOutput").ap()
        d_sv = nc.dram_tensor("d_sv", [128, 6], F32, kind="ExternalOutput").ap()
        d_wgt = nc.dram_tensor("d_wgt", [c.DM, 512], BF16, kind="ExternalOutput").ap()
        d_wdt = nc.dram_tensor("d_wdt", [512, c.DM], BF16, kind="ExternalOutput").ap()
        d_hh = nc.dram_tensor("d_hh", [128, c.INNER], F32, kind="ExternalOutput").ap()
        d_r1 = nc.dram_tensor("d_r1", [128, c.DM], F32, kind="ExternalOutput").ap()
        d_al = nc.dram_tensor("d_al", [128, 3 * (c.MEGA // 128)], F32, kind="ExternalOutput").ap()
        d_scp4 = nc.dram_tensor("d_scp4", [128, c.T], F32, kind="ExternalOutput").ap()

    lnx_s = nc.dram_tensor("lnx_s", [c.DM, c.T], BF16, kind="Internal").ap()
    cq_s = nc.dram_tensor("cq_s", [c.CQ, c.T], BF16, kind="Internal").ap()
    qs_s = nc.dram_tensor("qs_s", [c.NHL * 128, c.T], BF16, kind="Internal").ap()
    aop_s = nc.dram_tensor("aop_s", [2, c.DM, c.OWN], F32, kind="Internal").ap()
    aog_s = nc.dram_tensor("aog_s", [c.DM, c.OWN], F32, kind="Internal").ap()
    wdq_bf = nc.dram_tensor("wdq_bf", [c.DM, c.CQ], BF16, kind="Internal").ap()
    wuq_bf = nc.dram_tensor("wuq_bf", [c.CQ, c.YD], BF16, kind="Internal").ap()
    wqr_bf = nc.dram_tensor("wqr_bf", [c.CQ, c.NHL * 32], BF16, kind="Internal").ap()
    wo_bf = nc.dram_tensor("wo_bf", [c.YD, c.DM], BF16, kind="Internal").ap()
    wdkv_bf = nc.dram_tensor("wdkv_bf", [c.DM, c.DC], BF16, kind="Internal").ap()
    wuk_bf = nc.dram_tensor("wuk_bf", [c.DC, c.YD], BF16, kind="Internal").ap()
    wuv_bf = nc.dram_tensor("wuv_bf", [c.DC, c.YD], BF16, kind="Internal").ap()
    wkr_bf = nc.dram_tensor("wkr_bf", [c.DM, 32], BF16, kind="Internal").ap()
    wgt_s = nc.dram_tensor("wgt_s", [c.DM, c.INNER], BF16, kind="Internal").ap()
    wut_s = nc.dram_tensor("wut_s", [c.DM, c.INNER], BF16, kind="Internal").ap()
    wdt_s = nc.dram_tensor("wdt_s", [c.INNER, c.DM], BF16, kind="Internal").ap()

    lnx_sv = lnx_s.rearrange("(c p) t -> p c t", p=128)
    cq_sv = cq_s.rearrange("(o p) t -> p o t", p=128)
    qs_sv = qs_s.rearrange("(o p) t -> p o t", p=128)
    groups = [[2 * p, 2 * p + 1] for p in range(c.n_pairs)]

    es = ExitStack()
    sb = lambda n, sh, dt, **kw: es.enter_context(nc.sbuf_tensor(n, sh, dt, **kw))
    ps = lambda n, sh, dt: es.enter_context(nc.psum_tensor(n, sh, dt))
    with es:
        em = Em(nc, es)
        pb = [ps(f"pb{i}", [128, 512], F32) for i in range(6)]
        ptp = ps("ptp", [128, 1024], BF16)
        ptp2 = ps("ptp2", [128, 1024], BF16)
        PT = 6

        ident = sb("ident", [128, 128], BF16)
        tri = sb("tri", [128, 128], BF16)
        ones_f = sb("ones_f", [128, 128], F32)
        ident_f = sb("ident_f", [128, 128], F32)
        sv_m = sb("sv_m", [128, 3], F32)
        mv_m = sb("mv_m", [128, 3], F32)
        att_es = ExitStack()
        sba = lambda n, sh, dt, **kw: att_es.enter_context(nc.sbuf_tensor(n, sh, dt, **kw))
        ehot = sba("ehot", [c.NHL, c.NHL * 128], BF16)
        ehot_v = ehot[:].rearrange("p (h f) -> p h f", f=128)
        hkh = sba("hkh", [128, c.NHL * c.NHL], BF16)
        maxk = sba("maxk", [c.NHL, 1], F32)
        Kh = [sba(f"Kh{i}", [128, c.T], BF16) for i in range(c.NHL)]
        Vp = sba("Vp", [128, c.NKT * c.NHL * 65], BF16)
        Vp_v = Vp[:].rearrange("p (k h d) -> p k h d", h=c.NHL, d=65)
        scp_c = sba("scp_c", [32, c.T], F32)
        QW = 2048
        quant_es = ExitStack()
        sbq = lambda n, sh, dt: quant_es.enter_context(nc.sbuf_tensor(n, sh, dt))
        wnat3 = [sbq(f"wnat3{i}", [128, QW], F32) for i in range(1)]
        zf3 = sbq("zf3", [128, 1024], BF16)
        wabs3 = sbq("wabs3", [128, 4], F32)
        wsc3 = sbq("wsc3", [1, 2], F32)
        wstg3 = sbq("wstg3", [128, 1024], BF16)
        scale_es = ExitStack()
        scale_t = scale_es.enter_context(nc.sbuf_tensor("scale_t", [32, c.T], F32))
        wn2_es = ExitStack()
        wnat3.append(wn2_es.enter_context(nc.sbuf_tensor("wnat3x", [128, QW], F32)))

        block = es.enter_context(nc.Block())

        ini_es = ExitStack()
        sbi = lambda n, sh, dt: ini_es.enter_context(nc.sbuf_tensor(n, sh, dt))
        io_pf = sbi("io_pf", [128, 128], I32)
        io_j = sbi("io_j", [32, 1], I32)
        io_pos = sbi("io_pos", [1, 512], I32)
        pos_f = sbi("pos_f", [1, 512], F32)
        jf = sbi("jf", [32, 2], F32)
        invf = sbi("invf", [32, 1], F32)
        emb = sbi("emb", [32, 512], F32)
        rr = sbi("rr", [32, 512], F32)
        u32 = sbi("u32", [32, 512], I32)
        uf = sbi("uf", [32, 512], F32)
        sint = sbi("sint", [32, 512], F32)
        cpl = sbi("cpl", [32, 512], F32)
        pi4 = sbi("pi4", [32, 1], F32)
        io_e = sbi("io_e", [c.NHL, 128], I32)

        def ini_g(g):
            g.iota(io_pf[:], pattern=[[-1, 128]], base=0, channel_multiplier=1)
            g.iota(io_j[:], pattern=[[0, 1]], base=0, channel_multiplier=1)
            em.do("g", g.iota(io_e[:], pattern=[[0, 128]], base=0, channel_multiplier=1))
        block.gpsimd(ini_g)
        t_iota = em.tok("g")

        def ini_v(v):
            em.pre("v", v, t_iota)
            em.do("v", v.tensor_scalar(ident[:], io_pf[:], 0, None, ALU.is_equal))
            em.pre("v", v)
            em.do("v", v.memset(pi4[:], math.pi / 4.0))
            em.pre("v", v)
            em.do("v", v.tensor_scalar(tri[:], io_pf[:], 0, None, ALU.is_le))
            em.pre("v", v)
            em.do("v", v.tensor_scalar(ident_f[:], io_pf[:], 0, None, ALU.is_equal))
            em.pre("v", v)
            em.do("v", v.memset(ones_f[:], 1.0))
            em.pre("v", v)
            em.do("v", v.tensor_copy(jf[:, 0:1], io_j[:]))
            em.pre("v", v)
            em.do("v", v.tensor_scalar(jf[:, 1:2], jf[:, 0:1], 16.0, None, ALU.is_ge))
            em.pre("v", v)
            em.do("v", v.scalar_tensor_tensor(jf[:, 0:1], jf[:, 1:2], -16.0, jf[:, 0:1],
                                              ALU.mult, ALU.add))
            for h in range(c.NHL):
                em.pre("v", v)
                em.do("v", v.tensor_scalar(ehot_v[:, h, :], io_e[:], h, None, ALU.is_equal))
            em.pre("v", v)
            em.do("v", v.memset(maxk[:], 0.0))
            em.pre("v", v)
            em.do("v", v.memset(hkh[:], 0.0))
            for h in range(c.NHL):
                em.pre("v", v)
                em.do("v", v.memset(hkh[0:96, h * c.NHL + h: h * c.NHL + h + 1], 1.0))
            for h in range(c.NHL):
                em.pre("v", v)
                em.do("v", v.memset(Kh[h][:], 0.0))
        block.vector(ini_v)
        t_vini = em.tok("v")

        def ini_s(s):
            em.pre("s", s, t_vini)
            em.do("s", s.activation(invf[:], jf[:, 0:1], AF.Exp,
                                    scale=-math.log(10000.0) / 16.0))
        block.scalar(ini_s)
        t_invf = em.tok("s")

        wsl = min(512, c.T)
        nsl = c.T // wsl
        tok_posf_rd = None
        tok_iop_rd = None
        for i in range(nsl):
            def ini_gp(g, i=i):
                em.pre("g", g, tok_iop_rd)
                em.do("g", g.iota(io_pos[0:1, 0:wsl], pattern=[[1, wsl]], base=i * wsl,
                                  channel_multiplier=0))
            block.gpsimd(ini_gp)
            t_iop = em.tok("g")

            def ini_vp(v, i=i):
                em.pre("v", v, t_iop, tok_posf_rd)
                em.do("v", v.tensor_copy(pos_f[0:1, 0:wsl], io_pos[0:1, 0:wsl]))
            block.vector(ini_vp)
            tok_iop_rd = em.tok("v")
            t_pf = em.tok("v")

            def ini_tp(t, i=i):
                em.wait("t", t, t_invf, t_vini, t_pf)
                em.do("t", t.matmul(pb[0][0:32, 0:wsl], ones_f[0:1, 0:32],
                                    pos_f[0:1, 0:wsl], start=True, stop=True))
            block.tensor(ini_tp)
            tok_posf_rd = em.tok("t")
            t_posbc = em.tok("t")

            def ini_v3(v, i=i):
                em.pre("v", v, t_posbc)
                em.do("v", v.tensor_scalar(emb[:, 0:wsl], pb[0][0:32, 0:wsl],
                                           invf[:], None, ALU.mult))
                em.pre("v", v)
                em.do("v", v.tensor_scalar(u32[:, 0:wsl], emb[:, 0:wsl], 1.0 / TWO_PI, 0.125,
                                           ALU.mult, ALU.add))
                em.pre("v", v)
                em.do("v", v.tensor_copy(uf[:, 0:wsl], u32[:, 0:wsl]))
                em.pre("v", v)
                em.do("v", v.scalar_tensor_tensor(rr[:, 0:wsl], emb[:, 0:wsl], 1.0 / TWO_PI,
                                                  uf[:, 0:wsl], ALU.mult, ALU.subtract))
                em.pre("v", v)
                em.do("v", v.tensor_scalar(cpl[:, 0:wsl], rr[:, 0:wsl], 0.375, None, ALU.is_gt))
                em.pre("v", v)
                em.do("v", v.tensor_scalar(rr[:, 0:wsl], rr[:, 0:wsl], -0.625, None, ALU.is_lt))
                em.pre("v", v)
                em.do("v", v.scalar_tensor_tensor(cpl[:, 0:wsl], rr[:, 0:wsl], -1.0,
                                                  cpl[:, 0:wsl], ALU.mult, ALU.add))
                em.pre("v", v)
                em.do("v", v.tensor_tensor(uf[:, 0:wsl], uf[:, 0:wsl], cpl[:, 0:wsl], ALU.add))
                em.pre("v", v)
                em.do("v", v.scalar_tensor_tensor(rr[:, 0:wsl], uf[:, 0:wsl], -TWO_PI,
                                                  emb[:, 0:wsl], ALU.mult, ALU.add))
            block.vector(ini_v3)
            t_rr = em.tok("v")

            def ini_s2(s, i=i):
                em.pre("s", s, t_rr)
                em.do("s", s.activation(sint[:, 0:wsl], rr[:, 0:wsl], AF.Sin,
                                        bias=pi4[:], scale=1.0))
            block.scalar(ini_s2)
            t_sin = em.tok("s")

            def ini_v4(v, i=i):
                em.pre("v", v, t_sin)
                em.do("v", v.tensor_scalar(scale_t[:, i * wsl:(i + 1) * wsl], sint[:, 0:wsl],
                                           math.sqrt(2.0), None, ALU.mult))
                em.pre("v", v)
                em.do("v", v.tensor_scalar(scp_c[:, i * wsl:(i + 1) * wsl], sint[:, 0:wsl],
                                           math.sqrt(2.0) * RS96, None, ALU.mult))
            block.vector(ini_v4)
            if dbg and i == nsl - 1:
                def ini_dbg(g, i=i):
                    em.wait("gd", g, em.tok("v"))
                    em.dma("gd", g.dma_start(out=d_ini[:, 0:512], in_=emb[:, 0:512]))
                    em.dma("gd", g.dma_start(out=d_ini[:, 512:1024], in_=rr[:, 0:512]))
                    em.dma("gd", g.dma_start(out=d_ini[:, 1024:1536], in_=uf[:, 0:512]))
                    em.dma("gd", g.dma_start(out=d_ini[:, 1536:2048], in_=sint[:, 0:512]))
                    em.dma("gd", g.dma_start(out=d_ini[:, 2048:2049], in_=invf[:], allow_slow_non_contiguous=True))
                    em.dma("gd", g.dma_start(out=d_ini[:, 2049:2050], in_=jf[:, 0:1], allow_slow_non_contiguous=True))
                block.gpsimd(ini_dbg)
        t_scp = em.tok("v")

        def ini_g2(g):
            em.wait("gd", g, t_scp)
            em.pre("g", g)
            em.do("g", g.memset(Vp_v[:, :, :, 64:65], 1.0))
            em.dma_on("wbs", g.dma_start(out=wdkv_bf, in_=wdkv))
            em.dma_on("wbs", g.dma_start(out=wuk_bf, in_=wuk))
            em.dma_on("wbs", g.dma_start(out=wuv_bf, in_=wuv))
            em.dma_on("wbs", g.dma_start(out=wkr_bf, in_=wkr))
            em.dma_on("wbl", g.dma_start(out=wdq_bf, in_=wdq))
            em.dma_on("wbl", g.dma_start(out=wuq_bf, in_=wuq))
            em.dma_on("wbl", g.dma_start(out=wqr_bf, in_=wqr))
            em.dma_on("wbl", g.dma_start(out=wo_bf, in_=wo))
        block.gpsimd(ini_g2)
        t_wbs = em.tok("wbs")
        t_wbl = em.tok("wbl")
        t_scp4 = em.tok("gd")
        t_vones = em.tok("g")

        barrier(em, block)
        ini_es.close()

        q_state = {"wn": [None, None], "wn_use": [None, None], "zf_rd": None,
                   "zb_rd": [None, None], "stg": [None] * 4, "pb3": None,
                   "i": 0}
        wgt_v3 = wgt_s.rearrange("(j p) r -> p j r", p=128)
        wut_v3 = wut_s.rearrange("(j p) r -> p j r", p=128)
        wdt_v3 = wdt_s.rearrange("(j p) r -> p j r", p=128)

        def _mk_units():
            units = []
            specs = [(wg, wgt_v3, c.INNER, c.DM, 0), (wu, wut_v3, c.INNER, c.DM, 1),
                     (wd, wdt_v3, c.DM, c.INNER, 2)]

            def u_clear():
                def f():
                    def b(v):
                        em.pre("v", v)
                        em.do("v", v.memset(wabs3[:, 0:3], 0.0))
                    block.vector(b)
                return (None, f)

            def u_p1(wsrc, r0, nr, cc0, ncol, sl, mi):
                def fl():
                    wn1b = wnat3[0] if sl == 0 else wnat3[-1]

                    def b_l(sy):
                        em.wait("qd", sy, q_state["wn_use"][sl])
                        q_state["wn"][sl] = em.dma_on(f"qn{sl}", sy.dma_start(
                            out=wn1b[:, 0:nr * ncol].rearrange("p (j f) -> p j f", j=nr),
                            in_=wsrc[r0 * 128:(r0 + nr) * 128, cc0:cc0 + ncol].rearrange(
                                "(j p) f -> p j f", p=128)))
                    block.sync(b_l)

                def f():
                    wn1b = wnat3[0] if sl == 0 else wnat3[-1]

                    def b_a(v):
                        em.pre("v", v, q_state["wn"][sl])
                        em.do("v", v.reduce_sum(wabs3[:, 3:4], wn1b[:, 0:nr * ncol],
                                                axis=AX.X, apply_absolute_value=True))
                        em.pre("v", v)
                        q_state["wn_use"][sl] = em.do("v", v.tensor_tensor(
                            wabs3[:, mi:mi + 1], wabs3[:, mi:mi + 1], wabs3[:, 3:4],
                            ALU.add))
                    block.vector(b_a)
                return (fl, f)

            def u_fin(mi, RC):
                def f():
                    def b_s(t):
                        em.wait("t", t, em.tok("v"), q_state["pb3"])
                        em.do("t", t.matmul(pb[4][0:1, 0:1], wabs3[:, mi:mi + 1],
                                            ones_f[:, 0:1], start=True, stop=True))
                    block.tensor(b_s)
                    t_sm = em.tok("t")

                    def b_m(v):
                        em.pre("v", v, t_sm)
                        em.do("v", v.tensor_scalar(wsc3[0:1, 0:1], pb[4][0:1, 0:1],
                                                   1.0 / RC, 1e-5, ALU.mult, ALU.max))
                        em.pre("v", v)
                        em.do("v", v.reciprocal(wsc3[0:1, 1:2], wsc3[0:1, 0:1]))
                    block.vector(b_m)
                    t_mn = em.tok("v")

                    def b_b(t):
                        em.wait("t", t, t_mn)
                        t.matmul(pb[4][:, 0:1], ones_f[0:1, :], wsc3[0:1, 1:2],
                                 start=True, stop=True)
                        em.do("t", t.matmul(pb[4][:, 1:2], ones_f[0:1, :], wsc3[0:1, 0:1],
                                            start=True, stop=True))
                    block.tensor(b_b)
                    t_b1 = em.tok("t")

                    def b_c(v):
                        em.pre("v", v, t_b1)
                        em.do("v", v.tensor_copy(sv_m[:, mi:mi + 1], pb[4][:, 0:1]))
                        em.pre("v", v)
                        em.do("v", v.tensor_copy(mv_m[:, mi:mi + 1], pb[4][:, 1:2]))
                    block.vector(b_c)
                    q_state["pb3"] = em.tok("v")
                return (None, f)

            def u_p2(wsrc, wdst_v, mi, r0, nr, cc0, ncol, cc_total, sl):
                def fl():
                    wn3b = wnat3[0] if sl == 0 else wnat3[-1]

                    def b_l(sy):
                        em.wait("qd", sy, q_state["wn_use"][sl])
                        q_state["wn"][sl] = em.dma_on(f"qn{sl}", sy.dma_start(
                            out=wn3b[:, 0:nr * ncol].rearrange("p (j f) -> p j f", j=nr),
                            in_=wsrc[r0 * 128:(r0 + nr) * 128, cc0:cc0 + ncol].rearrange(
                                "(j p) f -> p j f", p=128)))
                    block.sync(b_l)

                def f():
                    wn3b = wnat3[0] if sl == 0 else wnat3[-1]
                    nrd = max(1, (nr * ncol) // 1024)
                    strip = min(1024, nr * ncol)
                    ccn = max(1, strip // ncol)
                    jbn = min(ncol, strip) // 128
                    for rd in range(nrd):
                        goff = cc0 + (rd * strip) % ncol
                        gch = r0 + (rd * strip) // ncol
                        jblk = goff // 128
                        ws_ = rd % 2
                        pdst = ptp if ws_ == 0 else ptp2

                        def b_q(v, rd=rd, strip=strip):
                            em.pre("v", v, q_state["wn"][sl], q_state["zb_rd"][0])
                            em.do("v", v.tensor_scalar(
                                zf3[:, 0:strip], wn3b[:, rd * strip:(rd + 1) * strip],
                                sv_m[:, mi:mi + 1], 0.5, ALU.mult, ALU.is_gt))
                            em.pre("v", v)
                            tk = em.do("v", v.tensor_scalar(
                                wn3b[:, rd * strip:(rd + 1) * strip],
                                wn3b[:, rd * strip:(rd + 1) * strip],
                                sv_m[:, mi:mi + 1], -0.5, ALU.mult, ALU.is_lt))
                            if rd == nrd - 1:
                                q_state["wn_use"][sl] = tk
                            em.pre("v", v)
                            em.do("v", v.tensor_tensor(
                                zf3[:, 0:strip], zf3[:, 0:strip],
                                wn3b[:, rd * strip:(rd + 1) * strip], ALU.subtract))
                        block.vector(b_q)
                        t_zb = em.tok("v")

                        def b_t(t, strip=strip, ws_=ws_, pdst=pdst, t_zb=t_zb):
                            em.wait("t", t, t_zb, t_vini, q_state["tp_rd"][ws_])
                            for j in range(strip // 128):
                                ins = t.transpose(pdst[:, j * 128:(j + 1) * 128],
                                                  zf3[:, j * 128:(j + 1) * 128], ident[:])
                            em.do("t", ins)
                        block.tensor(b_t)
                        q_state["zb_rd"][0] = em.tok("t")
                        t_t3 = em.tok("t")

                        def b_e(v, ws_=ws_, pdst=pdst, t_t3=t_t3, strip=strip):
                            em.pre("v", v, t_t3, q_state["stg"][0])
                            em.do("v", v.tensor_copy(wstg3[:, 0:strip], pdst[:, 0:strip]))
                        block.vector(b_e)
                        q_state["tp_rd"][ws_] = em.tok("v")
                        t_e3 = em.tok("v")

                        def b_w(sy, jblk=jblk, gch=gch, t_e3=t_e3,
                                ccn=ccn, jbn=jbn, strip=strip):
                            em.wait("qd", sy, t_e3)
                            tk = None
                            for cc_i in range(ccn):
                                tk = em.dma_on("qw0", sy.dma_start(
                                    out=wdst_v[:, jblk:jblk + jbn,
                                               (gch + cc_i) * 128:(gch + cc_i + 1) * 128],
                                    in_=wstg3[:, cc_i * jbn * 128:(cc_i + 1) * jbn * 128
                                              ].rearrange("p (j r) -> p j r", r=128)))
                            q_state["stg"][0] = tk
                        block.sync(b_w)
                    q_state["i"] += 1
                return (fl, f)

            si = [0]
            units.append(u_clear())
            for (wsrc, wdst_v, R, Cc, mi) in specs:
                nr = max(1, QW // Cc)
                ncol = min(Cc, QW)
                for r0 in range(0, R // 128, nr):
                    for cc0 in range(0, Cc, ncol):
                        units.append(u_p1(wsrc, r0, nr, cc0, ncol, si[0] % 2, mi))
                        si[0] += 1
            fin_start = len(units)
            for (wsrc, wdst_v, R, Cc, mi) in specs:
                units.append(u_fin(mi, float(R) * Cc))
            p2start = len(units)
            for (wsrc, wdst_v, R, Cc, mi) in specs:
                nr = max(1, QW // Cc)
                ncol = min(Cc, QW)
                for r0 in range(0, R // 128, nr):
                    for cc0 in range(0, Cc, ncol):
                        units.append(u_p2(wsrc, wdst_v, mi, r0, nr, cc0, ncol, Cc,
                                          si[0] % 2))
                        si[0] += 1
            return units, fin_start, p2start

        q_state["tp_rd"] = [None, None]
        p3p_units, p3p_fin, p3p_p2start = _mk_units()
        p3p_pos = {"cp": 0, "ld": 0}
        P3P_LA = 2

        def emit_p3p_flush():
            while p3p_pos["cp"] < p3p_pos["ld"]:
                fc = p3p_units[p3p_pos["cp"]][1]
                if fc is not None:
                    fc()
                p3p_pos["cp"] += 1

        def emit_p3p(n, cap=None):
            lim = len(p3p_units) if cap is None else cap
            while n > 0 and p3p_pos["cp"] < lim:
                while (p3p_pos["ld"] < lim
                       and p3p_pos["ld"] < p3p_pos["cp"] + P3P_LA):
                    fl = p3p_units[p3p_pos["ld"]][0]
                    if fl is not None:
                        fl()
                    p3p_pos["ld"] += 1
                fc = p3p_units[p3p_pos["cp"]][1]
                fc()
                p3p_pos["cp"] += 1
                n -= 1


        p1_es = ExitStack()
        sb1 = lambda n, sh, dt: p1_es.enter_context(nc.sbuf_tensor(n, sh, dt))
        wdkv_sb = sb1("wdkv_sb", [128, c.KC * 2 * 128], BF16)
        wdkv_v = wdkv_sb[:].rearrange("p (k o f) -> p k o f", k=c.KC, o=2)
        wuk_sb = sb1("wuk_sb", [128, 2 * c.YD], BF16)
        wuk_v = wuk_sb[:].rearrange("p (k f) -> p k f", k=2)
        wuv_sb = sb1("wuv_sb", [128, 2 * c.YD], BF16)
        wuv_v = wuv_sb[:].rearrange("p (k f) -> p k f", k=2)
        wkr_sb = sb1("wkr_sb", [128, c.KC * 32], BF16)
        wkr_v = wkr_sb[:].rearrange("p (k f) -> p k f", k=c.KC)
        lnt = [sb1(f"p1ln{i}", [128, c.KC * c.NS], BF16) for i in range(2)]
        lnt_v = [x[:].rearrange("p (k t) -> p k t", k=c.KC) for x in lnt]
        ckv = sb1("ckv", [128, 2 * c.NS], BF16)
        ckv_v = ckv[:].rearrange("p (k t) -> p k t", k=2)
        sqb = sb1("sqb", [128, c.NS], BF16)
        krt = sb1("krt", [32, c.NS], BF16)
        knm = sb1("knm", [c.NHL, 1], F32)

        def p1_wload(sy):
            em.wait("qd", sy, t_wbs)
            em.dma_on("p1w", sy.dma_start(out=wdkv_v, in_=wdkv_bf.rearrange(
                "(k p) (o f) -> p k o f", p=128, f=128)))
            em.dma_on("p1w", sy.dma_start(out=wuk_v, in_=wuk_bf.rearrange("(k p) f -> p k f", p=128)))
            em.dma_on("p1w", sy.dma_start(out=wuv_v, in_=wuv_bf.rearrange("(k p) f -> p k f", p=128)))
            em.dma_on("p1w", sy.dma_start(out=wkr_v, in_=wkr_bf.rearrange("(k p) f -> p k f", p=128)))
        block.sync(p1_wload)
        t_p1w = em.tok("p1w")

        tok_ln1 = [None, None]
        tok_lnuse = [None, None]
        tok_ckv_war = None
        tok_pb2_rd = None
        tok_pb3_rd = None
        tok_pb4_rd = None
        tok_pb5_rd = None
        tok_sq_rd = None
        tok_ckv_rd = None

        p0_st = []
        P1R = c.NS // 128

        def emit_P1_tile(it):
            nonlocal tok_ckv_war, tok_pb2_rd, tok_pb3_rd, tok_pb4_rd, tok_pb5_rd
            nonlocal tok_sq_rd, tok_ckv_rd
            sl = it % 2
            ts0 = it * c.NS

            def b_ld(sy, sl=sl, ts0=ts0, it=it):
                em.wait("qd", sy, tok_lnuse[sl], p0_st[it * P1R + P1R - 1],
                        p0_st[it * P1R + P1R - 2] if P1R >= 2 else None)
                tok_ln1[sl] = em.dma_on(f"ql{sl}", sy.dma_start(
                    out=lnt_v[sl], in_=lnx_sv[:, :, ts0:ts0 + c.NS]))
            block.sync(b_ld)

            def b_ckv_mm(t, sl=sl):
                em.wait("t", t, tok_ln1[sl], t_p1w, tok_ckv_war)
                for o in range(2):
                    for kc in range(c.KC):
                        ins = t.matmul(pb[o][:, 0:c.NS], wdkv_v[:, kc, o, :],
                                       lnt_v[sl][:, kc, :],
                                       start=(kc == 0), stop=(kc == c.KC - 1))
                em.do("t", ins)
            block.tensor(b_ckv_mm)
            t_ckvmm = em.tok("t")

            def b_ckv_ev(v):
                em.pre("v", v, t_ckvmm, tok_ckv_rd)
                em.do("v", v.tensor_copy(ckv_v[:, 0, :], pb[0][:, 0:c.NS]))
                em.pre("v", v)
                em.do("v", v.tensor_copy(ckv_v[:, 1, :], pb[1][:, 0:c.NS]))
            block.vector(b_ckv_ev)
            tok_ckv_war = em.tok("v")
            t_ckv = em.tok("v")

            for oc in range(c.NHL // 2):
                def b_kc_mm(t, oc=oc):
                    em.wait("t", t, t_ckv, tok_pb2_rd)
                    for kc in range(2):
                        ins = t.matmul(pb[2][:, 0:c.NS], wuk_v[:, kc, oc * 128:(oc + 1) * 128],
                                       ckv_v[:, kc, :], start=(kc == 0), stop=(kc == 1))
                    em.do("t", ins)
                block.tensor(b_kc_mm)
                t_kcm = em.tok("t")

                def b_kc_ev(v, oc=oc, ts0=ts0):
                    em.pre("v", v, t_kcm)
                    em.do("v", v.tensor_copy(Kh[2 * oc][0:64, ts0:ts0 + c.NS],
                                             pb[2][0:64, 0:c.NS]))
                    em.pre("v", v)
                    em.do("v", v.tensor_copy(Kh[2 * oc + 1][0:64, ts0:ts0 + c.NS],
                                             pb[2][64:128, 0:c.NS]))
                block.vector(b_kc_ev)
                tok_pb2_rd = em.tok("v")

            def b_kr_mm(t, sl=sl):
                em.wait("t", t, tok_ln1[sl], tok_pb4_rd)
                for kc in range(c.KC):
                    ins = t.matmul(pb[4][0:32, 0:c.NS], wkr_v[:, kc, :], lnt_v[sl][:, kc, :],
                                   start=(kc == 0), stop=(kc == c.KC - 1))
                tok_lnuse[sl] = em.do("t", ins)
            block.tensor(b_kr_mm)
            t_krm = em.tok("t")

            def b_kr_ev(v, ts0=ts0):
                em.pre("v", v, t_krm, t_scp)
                em.do("v", v.tensor_tensor(krt[:], pb[4][0:32, 0:c.NS],
                                           scale_t[:, ts0:ts0 + c.NS], ALU.mult))
                for h in range(c.NHL):
                    em.pre("v", v)
                    em.do("v", v.tensor_copy(Kh[h][64:96, ts0:ts0 + c.NS], krt[:]))
            block.vector(b_kr_ev)
            tok_pb4_rd = em.tok("v")
            t_krope = em.tok("v")

            for h in range(c.NHL):
                def b_sq(s, h=h, ts0=ts0):
                    em.pre("s", s, t_krope, tok_sq_rd)
                    em.do("s", s.activation(sqb[:], Kh[h][:, ts0:ts0 + c.NS], AF.Square))
                block.scalar(b_sq)
                t_sq = em.tok("s")

                def b_kn_mm(t, h=h):
                    em.wait("t", t, t_sq, tok_pb3_rd if h == 0 else None)
                    em.do("t", t.matmul(pb[3][0:c.NHL, 0:c.NS],
                                        hkh[:, h * c.NHL:(h + 1) * c.NHL], sqb[:],
                                        start=(h == 0), stop=(h == c.NHL - 1)))
                block.tensor(b_kn_mm)
                tok_sq_rd = em.tok("t")
            t_knmm = em.tok("t")

            def b_kn_max(v):
                em.pre("v", v, t_knmm, t_vini)
                em.do("v", v.reduce_max(knm[:], pb[3][0:c.NHL, 0:c.NS], axis=AX.X))
                em.pre("v", v)
                em.do("v", v.tensor_tensor(maxk[:], maxk[:], knm[:], ALU.max))
            block.vector(b_kn_max)
            tok_pb3_rd = em.tok("v")

            for sub in range(c.NS // 128):
                tc = it * (c.NS // 128) + sub

                def b_v_mm(t, sub=sub):
                    em.wait("t", t, t_ckv, t_vones, tok_pb5_rd)
                    for kc in range(2):
                        ins = t.matmul(pb[5][:, 0:c.YD],
                                       ckv_v[:, kc, sub * 128:(sub + 1) * 128],
                                       wuv_v[:, kc, :], start=(kc == 0), stop=(kc == 1))
                    em.do("t", ins)
                block.tensor(b_v_mm)
                tok_ckv_rd = em.tok("t")
                t_vm = em.tok("t")

                def b_v_ev(v, tc=tc):
                    em.pre("v", v, t_vm)
                    em.do("v", v.tensor_copy(
                        Vp_v[:, tc, :, 0:64],
                        pb[5][:, 0:c.YD].rearrange("p (h d) -> p h d", d=64)))
                block.vector(b_v_ev)
                tok_pb5_rd = em.tok("v")

        p0_es = ExitStack()
        sb0 = lambda n, sh, dt: p0_es.enter_context(nc.sbuf_tensor(n, sh, dt))
        xb = [sb0(f"p0x{i}", [128, c.DM], F32) for i in range(2)]
        lnb = [sb0(f"p0ln{i}", [128, c.DM], BF16) for i in range(2)]
        fmb = [sb0(f"p0fm{i}", [128, c.DM], BF16) for i in range(2)]
        nst = max(1, c.DM // 512)
        wst = c.DM // nst
        st6 = sb0("p0st6", [128, 6 * nst], F32)
        mv2 = [sb0(f"p0mv{i}", [128, 2], F32) for i in range(2)]
        sdv = [sb0(f"p0sd{i}", [128, 3], F32) for i in range(2)]

        tok_xuse = [None, None]
        tok_x = [None, None]
        tok_lnwar = [None, None]
        tok_fmwar = [None, None]
        tok_ptp_ev = None
        t_sd2 = [None, None]

        def p0_front(i):
            sl = i % 2

            def b_load(sy, i=i, sl=sl):
                em.wait("qd", sy, tok_xuse[sl])
                tok_x[sl] = em.dma_on(f"qx{sl}", sy.dma_start(
                    out=xb[sl][:], in_=x_seq[i * 128:(i + 1) * 128, :]))
            block.sync(b_load)

            def b_stats(v, sl=sl):
                em.pre("v", v, tok_x[sl])
                for j in range(nst):
                    em.pre("v", v)
                    em.do("v", v.bn_stats(st6[:, 6 * j:6 * j + 6],
                                          xb[sl][:, j * wst:(j + 1) * wst]))
                em.pre("v", v)
                em.do("v", v.bn_aggr(mv2[sl][:], st6[:]))
                em.pre("v", v)
                em.do("v", v.tensor_scalar(sdv[sl][:, 0:1], mv2[sl][:, 1:2], EPS,
                                           None, ALU.add))
            block.vector(b_stats)
            t_veps = em.tok("v")

            def b_sd(s, sl=sl):
                em.pre("s", s, t_veps)
                em.do("s", s.activation(sdv[sl][:, 0:1], sdv[sl][:, 0:1], AF.Sqrt))
            block.scalar(b_sd)
            t_sd2[sl] = em.tok("s")

        def p0_back(i):
            nonlocal tok_ptp_ev
            sl = i % 2

            def b_rstd(v, sl=sl):
                em.pre("v", v, t_sd2[sl])
                em.do("v", v.reciprocal(sdv[sl][:, 1:2], sdv[sl][:, 0:1]))
                em.pre("v", v)
                em.do("v", v.tensor_scalar(sdv[sl][:, 2:3], mv2[sl][:, 0:1],
                                           sdv[sl][:, 1:2], -1.0, ALU.mult, ALU.mult))
            block.vector(b_rstd)
            t_rstd = em.tok("v")

            def b_apply(v, sl=sl):
                em.pre("v", v, t_rstd, tok_lnwar[sl])
                tok_xuse[sl] = em.do("v", v.tensor_scalar(
                    lnb[sl][:], xb[sl][:], sdv[sl][:, 1:2], sdv[sl][:, 2:3],
                    ALU.mult, ALU.add))
            block.vector(b_apply)
            t_ln = em.tok("v")

            def b_tp(t, sl=sl):
                em.wait("t", t, t_ln, t_vini, tok_ptp_ev)
                for cc_ in range(c.KC):
                    ins = t.transpose(ptp[:, cc_ * 128:(cc_ + 1) * 128],
                                      lnb[sl][:, cc_ * 128:(cc_ + 1) * 128], ident[:])
                tok_lnwar[sl] = em.do("t", ins)
            block.tensor(b_tp)
            t_tp = em.tok("t")

            def b_evac(v, sl=sl):
                em.pre("v", v, t_tp, tok_fmwar[sl])
                em.do("v", v.tensor_copy(fmb[sl][:], ptp[:, 0:c.DM]))
            block.vector(b_evac)
            tok_ptp_ev = em.tok("v")
            t_ev = em.tok("v")

            def b_store(sy, i=i, sl=sl):
                em.wait("qd", sy, t_ev)
                tok_fmwar[sl] = em.dma_on(f"qf{sl}", sy.dma_start(
                    out=lnx_sv[:, :, i * 128:(i + 1) * 128],
                    in_=fmb[sl][:].rearrange("p (c t) -> p c t", c=c.KC)))
            block.sync(b_store)
            p0_st.append(tok_fmwar[sl])
            emit_p3p(1, cap=p3p_fin)
            if (i + 1) % P1R == 0:
                emit_P1_tile((i + 1) // P1R - 1)

        for i in range(c.T // 128):
            p0_front(i)
            if i > 0:
                p0_back(i - 1)
        p0_back(c.T // 128 - 1)
        emit_p3p_flush()

        barrier(em, block)
        p0_es.close()
        p1_es.close()
        wn2_es.close()
        scale_es.close()



        p2a_es = ExitStack()
        sb2a = lambda n, sh, dt: p2a_es.enter_context(nc.sbuf_tensor(n, sh, dt))
        lnxf = sb2a("lnxf", [128, c.KC * c.T], BF16)
        lnxf_v = lnxf[:].rearrange("p (k t) -> p k t", k=c.KC)
        wdq_b = [sb2a(f"wdqb{i}", [128, c.KC * 128], BF16) for i in range(2)]
        wdq_bv = [x[:].rearrange("p (k f) -> p k f", k=c.KC) for x in wdq_b]
        cqt = [sb2a(f"cqt{i}", [128, c.NS], BF16) for i in range(2)]
        wnat3.append(sb2a("wnat3y", [128, QW], F32))

        def p2a_lnx(sy):
            em.dma("qd", sy.dma_start(out=lnxf_v, in_=lnx_sv))
        block.sync(p2a_lnx)
        t_lnxf = em.tok("qd")

        tok_wdq = [None, None]
        tok_wdq_use = [None, None]
        tok_cqt_st = [None] * 4
        tok_pbsc_rd = [None] * 4
        n2 = c.T // c.NS
        for oc in range(c.CQC):
            sl = oc % 2

            def b_w(sy, oc=oc, sl=sl):
                em.wait("qd", sy, tok_wdq_use[sl], t_wbl)
                tok_wdq[sl] = em.dma_on(f"gw{sl}", sy.dma_start(
                    out=wdq_bv[sl],
                    in_=wdq_bf[:, oc * 128:(oc + 1) * 128].rearrange("(k p) f -> p k f", p=128)))
            block.sync(b_w)

            for ns in range(n2):
                psl = ns % 4

                def b_mm(t, oc=oc, sl=sl, ns=ns, psl=psl):
                    em.wait("t", t, tok_wdq[sl], t_lnxf, tok_pbsc_rd[psl])
                    for kc in range(c.KC):
                        ins = t.matmul(pb[psl][:, 0:c.NS], wdq_bv[sl][:, kc, :],
                                       lnxf_v[:, kc, ns * c.NS:(ns + 1) * c.NS],
                                       start=(kc == 0), stop=(kc == c.KC - 1))
                    tk = em.do("t", ins)
                    if ns == n2 - 1:
                        tok_wdq_use[sl] = tk
                block.tensor(b_mm)
                t_mm = em.tok("t")

                def b_ev(v, psl=psl, csl=ns % 2):
                    em.pre("v", v, t_mm, tok_cqt_st[csl])
                    em.do("v", v.tensor_copy(cqt[csl][:], pb[psl][:, 0:c.NS]))
                block.vector(b_ev)
                tok_pbsc_rd[psl] = em.tok("v")
                t_e = em.tok("v")

                def b_st(sy, oc=oc, ns=ns, csl=ns % 2):
                    em.wait("qd", sy, t_e)
                    tok_cqt_st[csl] = em.dma_on(f"qc{csl}", sy.dma_start(
                        out=cq_sv[:, oc, ns * c.NS:(ns + 1) * c.NS], in_=cqt[csl][:]))
                block.sync(b_st)
            emit_p3p(3)
        emit_p3p(len(p3p_units))
        barrier(em, block)
        p2a_es.close()
        quant_es.close()

        p2b_es = ExitStack()
        sb2b = lambda n, sh, dt: p2b_es.enter_context(nc.sbuf_tensor(n, sh, dt))
        wuq_sb = sb2b("wuq_sb", [128, c.CQC * c.YD], BF16)
        wuq_v = wuq_sb[:].rearrange("p (k f) -> p k f", k=c.CQC)
        wqr_sb = sb2b("wqr_sb", [128, c.CQC * c.NHL * 32], BF16)
        wqr_v = wqr_sb[:].rearrange("p (k f) -> p k f", k=c.CQC)
        cq_t = sb2b("cq_t", [128, c.CQC * c.NS], BF16)
        cq_tv = cq_t[:].rearrange("p (k t) -> p k t", k=c.CQC)
        qoutb = [sb2b(f"qout{i}", [128, c.NS], BF16) for i in range(2)]
        scp4 = sb2b("scp4", [128, c.T], BF16)

        def p2b_dup(g):
            em.dma_on("scp", g.dma_start(out=scp4[0:32, :], in_=scp_c[:]))
            for b0 in (32, 64, 96):
                em.dma_on("scp", g.dma_start(out=scp4[b0:b0 + 32, :], in_=scp_c[:]))
        block.gpsimd(p2b_dup)

        def p2b_w(sy):
            em.wait("qd", sy, t_wbl)
            em.dma_on("p2bw", sy.dma_start(out=wuq_v, in_=wuq_bf.rearrange("(k p) f -> p k f", p=128)))
            em.dma_on("p2bw", sy.dma_start(out=wqr_v, in_=wqr_bf.rearrange("(k p) f -> p k f", p=128)))
        block.sync(p2b_w)
        t_p2bw = em.tok("p2bw")

        tok_cqld = None
        tok_qst = [None, None]
        tok_pbq_rd = [None, None]
        nqc = c.NHL // 2
        for ns in range(n2):
            def b_cq(sy, ns=ns):
                em.wait("qd", sy, tok_cqld)
                em.dma_on("ql2", sy.dma_start(out=cq_tv, in_=cq_sv[:, :, ns * c.NS:(ns + 1) * c.NS]))
            block.sync(b_cq)
            t_cq = em.tok("ql2")

            for j in range(nqc + c.QRC):
                psl = j % 2
                is_qr = j >= nqc
                oc = j - nqc if is_qr else j

                def b_mm(t, j=j, oc=oc, is_qr=is_qr, psl=psl):
                    em.wait("t", t, t_cq, t_p2bw, tok_pbq_rd[psl])
                    w_v = wqr_v if is_qr else wuq_v
                    for kc in range(c.CQC):
                        ins = t.matmul(pb[psl][:, 0:c.NS], w_v[:, kc, oc * 128:(oc + 1) * 128],
                                       cq_tv[:, kc, :], start=(kc == 0), stop=(kc == c.CQC - 1))
                    tk = em.do("t", ins)
                    if j == nqc + c.QRC - 1:
                        nonlocal tok_cqld
                        tok_cqld = tk
                block.tensor(b_mm)
                t_mm = em.tok("t")

                if is_qr:
                    def b_ev(v, psl=psl, ns=ns):
                        em.pre("v", v, t_mm, tok_qst[psl], em.tok("scp"))
                        em.do("v", v.tensor_tensor(qoutb[psl][:], pb[psl][:, 0:c.NS],
                                                   scp4[:, ns * c.NS:(ns + 1) * c.NS], ALU.mult))
                    block.vector(b_ev)
                    tok_pbq_rd[psl] = em.tok("v")
                    t_e = em.tok("v")
                else:
                    def b_ev(s, psl=psl):
                        em.pre("s", s, t_mm, tok_qst[psl])
                        em.do("s", s.activation(qoutb[psl][:], pb[psl][:, 0:c.NS],
                                                AF.Identity, scale=RS96))
                    block.scalar(b_ev)
                    tok_pbq_rd[psl] = em.tok("s")
                    t_e = em.tok("s")

                def b_st(sy, oc=oc, is_qr=is_qr, psl=psl, ns=ns):
                    em.wait("qd", sy, t_e)
                    tk = None
                    if is_qr:
                        for j in range(4):
                            tk = em.dma_on(f"qs{psl}", sy.dma_start(
                                out=qs_s[(4 * oc + j) * 128 + 64:(4 * oc + j) * 128 + 96,
                                         ns * c.NS:(ns + 1) * c.NS],
                                in_=qoutb[psl][j * 32:(j + 1) * 32, :]))
                    else:
                        for j in range(2):
                            tk = em.dma_on(f"qs{psl}", sy.dma_start(
                                out=qs_s[(2 * oc + j) * 128:(2 * oc + j) * 128 + 64,
                                         ns * c.NS:(ns + 1) * c.NS],
                                in_=qoutb[psl][j * 64:(j + 1) * 64, :]))
                    tok_qst[psl] = tk
                block.sync(b_st)
        barrier(em, block)
        p2b_es.close()

        p2c_es = ExitStack()
        sb2c = lambda n, sh, dt: p2c_es.enter_context(nc.sbuf_tensor(n, sh, dt))
        qsl = sb2c("qsl", [128, c.NHL * c.SLOT], BF16)
        qh_v = qsl[:].rearrange("p (o t) -> p o t", o=c.NHL)
        sq2 = sb2c("sq2", [128, c.SLOT], BF16)
        qn = sb2c("qn", [c.NHL, c.SLOT], F32)
        qmx = sb2c("qmx", [c.NHL, 1], F32)
        ngc = sb2c("ngc", [c.NHL, 1], F32)
        ngb = sb2c("ngb", [c.NHL, 1], BF16)
        bias_sb = sb2c("bias_sb", [128, c.NHL], F32)
        expA = sb2c("expA", [128, c.NKT * c.SLOT], BF16)
        expA_v = expA[:].rearrange("p (k t) -> p k t", k=c.NKT)
        yun_sb = [sb2c(f"yun{i}", [128, c.SLOT], BF16) for i in range(c.YC)]
        den_f = sb2c("den_f", [1, c.NHL * c.SLOT], F32)
        y_sb = [sb2c(f"ysb{i}", [128, c.SLOT], BF16) for i in range(c.YC)]
        wop_sb = sb2c("wop_sb", [128, c.YC * c.DM], BF16)
        wop_v = wop_sb[:].rearrange("p (k f) -> p k f", k=c.YC)
        aop_t = [sb2c(f"aopt{i}", [128, c.SLOT], F32) for i in range(2)]
        aop_sv = aop_s.rearrange("j (o p) t -> p j o t", p=128)

        def p2c_wo(sy):
            em.wait("qd", sy, t_wbl)
            em.dma_on("wop", sy.dma_start(out=wop_v, in_=wo_bf.rearrange("(k p) f -> p k f", p=128)))
        block.sync(p2c_wo)
        t_wop = em.tok("wop")

        def p2c_zero(v):
            em.pre("v", v)
            em.do("v", v.memset(qsl[:], 0.0))
        block.vector(p2c_zero)
        t_qrz = em.tok("v")

        tok_qsl_use = None
        tok_avprev = None
        tok_yun_rd = None
        pb01_rd = [None, None]
        tok_py_rd = [None, None]
        tok_pb3q_rd = None
        tok_pb4e_rd = None
        tok_ngb_rd = None
        tok_ywr = None
        tok_sq2_rd = None
        tok_aop_st = [None, None]
        for slot in range(c.NSLOT):
            q0 = slot * c.SLOT
            nkt = (q0 + c.SLOT) // 128

            def b_qld(sy, q0=q0):
                em.wait("qd", sy, tok_qsl_use, t_qrz)
                em.dma_on("qq", sy.dma_start(out=qh_v[0:96, :, :],
                                             in_=qs_sv[0:96, :, q0:q0 + c.SLOT]))
            block.sync(b_qld)
            t_qc_ld = em.tok("qq")
            t_qr_ld = t_qc_ld

            for j in range(c.NHL):
                def b_sqq(v, j=j):
                    em.pre("v", v, t_qc_ld, tok_sq2_rd)
                    em.do("v", v.tensor_tensor(sq2[:], qh_v[:, j, :], qh_v[:, j, :], ALU.mult))
                block.vector(b_sqq)
                t_s1 = em.tok("v")

                def b_nm(t, j=j):
                    em.wait("t", t, t_s1, tok_pb3q_rd if j == 0 else None)
                    em.do("t", t.matmul(pb[3][0:c.NHL, 0:c.SLOT],
                                        hkh[:, j * c.NHL:(j + 1) * c.NHL], sq2[:],
                                        start=(j == 0), stop=(j == c.NHL - 1)))
                block.tensor(b_nm)
                tok_sq2_rd = em.tok("t")
            t_qn_mm = em.tok("t")

            def b_ngc(v):
                em.pre("v", v, t_qn_mm, tok_ngb_rd)
                em.do("v", v.reduce_max(qmx[:], pb[3][0:c.NHL, 0:c.SLOT], axis=AX.X))
                em.pre("v", v)
                em.do("v", v.tensor_scalar(ngc[:], qmx[:], maxk[:], None, ALU.mult))
            block.vector(b_ngc)
            tok_pb3q_rd = em.tok("v")
            t_c2 = em.tok("v")

            def b_csq(s):
                em.pre("s", s, t_c2)
                em.do("s", s.activation(ngc[:], ngc[:], AF.Sqrt))
            block.scalar(b_csq)
            t_cs = em.tok("s")

            def b_ng2(v):
                em.pre("v", v, t_cs)
                em.do("v", v.tensor_scalar(ngc[:], ngc[:], -1.0, None, ALU.mult))
                em.pre("v", v)
                em.do("v", v.tensor_copy(ngb[:], ngc[:]))
            block.vector(b_ng2)
            t_ngb = em.tok("v")

            def b_ext(t):
                em.wait("t", t, t_ngb, tok_pb4e_rd)
                for h in range(c.NHL):
                    ins = t.matmul(pb[4][:, h:h + 1], ehot_v[:, h, :], ngb[:],
                                   start=True, stop=True)
                em.do("t", ins)
            block.tensor(b_ext)
            tok_ngb_rd = em.tok("t")
            t_ext = em.tok("t")

            def b_bias(v):
                em.pre("v", v, t_ext)
                em.do("v", v.tensor_copy(bias_sb[:, 0:c.NHL], pb[4][:, 0:c.NHL]))
            block.vector(b_bias)
            tok_pb4e_rd = em.tok("v")
            t_bias = em.tok("v")

            offs = [max(0, kt * 128 - q0) for kt in range(nkt)]
            for h in range(c.NHL):
                pby = pb[5] if h % 2 == 0 else pb[2]
                ybk = h % 2
                sbase = em.cnt["s"]

                def b_sc_all(t, h=h, nkt=nkt, sbase=sbase):
                    em.wait("t", t, t_qc_ld, t_bias, ("s", sbase),
                            pb01_rd[0], pb01_rd[1], tok_pb4e_rd)
                    for kt in range(nkt):
                        off = offs[kt]
                        psl = (0, 1, 4)[kt % 3]
                        if kt >= 3:
                            em.wait("t", t, ("s", sbase + kt - 2))
                        ins = t.matmul(pb[psl][:, off:c.SLOT],
                                       Kh[h][0:96, kt * 128:(kt + 1) * 128],
                                       qh_v[0:96, h, off:c.SLOT], start=True, stop=True)
                        em.do("t", ins)
                block.tensor(b_sc_all)
                tbase = em.cnt["t"] - nkt

                def b_exp_all(s, h=h, nkt=nkt, tbase=tbase):
                    em.pre("s", s, tok_avprev)
                    for kt in range(nkt):
                        off = offs[kt]
                        em.wait("s", s, ("t", tbase + kt + 1))
                        em.do("s", s.activation(expA_v[:, kt, off:c.SLOT],
                                                pb[(0, 1, 4)[kt % 3]][:, off:c.SLOT],
                                                AF.Exp, bias=bias_sb[:, h:h + 1],
                                                scale=1.0))
                block.scalar(b_exp_all)
                assert em.cnt["s"] == sbase + nkt

                ndiag = c.SLOT // 128
                vbase = em.cnt["v"]

                def b_msk_all(v, nkt=nkt, sbase=sbase, ndiag=ndiag):
                    for i_, kt in enumerate(range(nkt - ndiag, nkt)):
                        off = offs[kt]
                        em.pre("v", v, ("s", sbase + kt + 1))
                        em.do("v", v.tensor_tensor(expA_v[:, kt, off:off + 128],
                                                   expA_v[:, kt, off:off + 128],
                                                   tri[:], ALU.mult))
                block.vector(b_msk_all)
                assert em.cnt["v"] == vbase + ndiag

                def b_av_all(t, h=h, nkt=nkt, sbase=sbase, vbase=vbase, ndiag=ndiag,
                             pby=pby, ybk=ybk):
                    em.wait("t", t, tok_py_rd[ybk])
                    for kt in range(nkt):
                        off = offs[kt]
                        if kt >= nkt - ndiag:
                            em.wait("t", t, ("v", vbase + ndiag - (nkt - kt) + 1))
                        else:
                            em.wait("t", t, ("s", sbase + kt + 1))
                        ins = t.matmul(pby[0:65, off:c.SLOT], Vp_v[:, kt, h, :],
                                       expA_v[:, kt, off:c.SLOT],
                                       start=(kt == 0), stop=(kt == nkt - 1))
                    em.do("t", ins)
                block.tensor(b_av_all)
                tok_avprev = em.tok("t")
                t_y = em.tok("t")
                if h == c.NHL - 1:
                    tok_qsl_use = t_y

                def b_yev(v, h=h, ch=h // 2, b0=(h % 2) * 64, pby=pby, ybk=ybk):
                    em.pre("v", v, t_y, tok_yun_rd)
                    em.do("v", v.tensor_copy(yun_sb[ch][b0:b0 + 64, 0:c.SLOT],
                                             pby[0:64, 0:c.SLOT]))
                    em.pre("v", v)
                    em.do("v", v.tensor_copy(den_f[0:1, h * c.SLOT:(h + 1) * c.SLOT],
                                             pby[64:65, 0:c.SLOT]))
                block.vector(b_yev)
                tok_py_rd[ybk] = em.tok("v")
            t_yun = em.tok("v")

            t_rcp = t_yun

            def b_bc_all(t):
                em.wait("t", t, t_rcp, pb01_rd[0], pb01_rd[1], tok_pb3q_rd, tok_pb4e_rd)
                for h in range(c.NHL):
                    bank = (pb[0], pb[1], pb[3], pb[4])[h // 2]
                    ins = t.matmul(bank[(h % 2) * 64:(h % 2) * 64 + 64, 0:c.SLOT],
                                   ones_f[0:1, 0:64],
                                   den_f[0:1, h * c.SLOT:(h + 1) * c.SLOT],
                                   start=True, stop=True)
                em.do("t", ins)
            block.tensor(b_bc_all)
            t_bc2 = em.tok("t")

            def b_recipb(v):
                em.pre("v", v, t_bc2)
                for bi in range(c.NHL // 2):
                    bank = (pb[0], pb[1], pb[3], pb[4])[bi]
                    rows = 128
                    em.pre("v", v)
                    em.do("v", v.reciprocal(bank[0:rows, 0:c.SLOT],
                                            bank[0:rows, 0:c.SLOT]))
            block.vector(b_recipb)
            t_bcr = em.tok("v")

            def b_norm(v):
                em.pre("v", v, t_bcr, tok_ywr)
                for h in range(c.NHL):
                    bank = (pb[0], pb[1], pb[3], pb[4])[h // 2]
                    em.pre("v", v)
                    em.do("v", v.tensor_tensor(
                        y_sb[h // 2][(h % 2) * 64:(h % 2) * 64 + 64, 0:c.SLOT],
                        yun_sb[h // 2][(h % 2) * 64:(h % 2) * 64 + 64, 0:c.SLOT],
                        bank[(h % 2) * 64:(h % 2) * 64 + 64, 0:c.SLOT], ALU.mult))
            block.vector(b_norm)
            tok_yun_rd = em.tok("v")
            tok_pb3q_rd = em.tok("v")
            tok_pb4e_rd = em.tok("v")
            pb01_rd[0] = em.tok("v")
            pb01_rd[1] = em.tok("v")
            t_yall = em.tok("v")

            for oc in range(c.KC):
                psl2 = oc % 2

                def b_wo_mm(t, oc=oc, psl2=psl2):
                    em.wait("t", t, t_yall, t_wop, pb01_rd[psl2])
                    for kc in range(c.YC):
                        ins = t.matmul(pb[psl2][:, 0:c.SLOT],
                                       wop_v[:, kc, oc * 128:(oc + 1) * 128],
                                       y_sb[kc][:, 0:c.SLOT],
                                       start=(kc == 0), stop=(kc == c.YC - 1))
                    tk = em.do("t", ins)
                    if oc == c.KC - 1:
                        nonlocal tok_ywr
                        tok_ywr = tk
                block.tensor(b_wo_mm)
                t_wm = em.tok("t")

                def b_wo_ev(v, oc=oc, psl2=psl2):
                    em.pre("v", v, t_wm, tok_aop_st[psl2])
                    em.do("v", v.tensor_copy(aop_t[psl2][:], pb[psl2][:, 0:c.SLOT]))
                block.vector(b_wo_ev)
                pb01_rd[psl2] = em.tok("v")
                t_ae = em.tok("v")

                def b_wo_st(sy, q0=q0, oc=oc, psl2=psl2):
                    em.wait("qd", sy, t_ae)
                    th = q0 // c.OWN
                    c0 = q0 % c.OWN
                    tok_aop_st[psl2] = em.dma_on(f"qa{psl2}", sy.dma_start(
                        out=aop_sv[:, th, oc, c0:c0 + c.SLOT], in_=aop_t[psl2][:]))
                block.sync(b_wo_st)

        barrier(em, block)

        def b_cc(g):
            em.do("cc", g.collective_compute(
                "ReduceScatter", mybir.AluOpType.add, groups, [aop_s.opt()], [aog_s.opt()]), 1)
        block.gpsimd(b_cc)

        barrier(em, block)
        p2c_es.close()
        att_es.close()


        p3_es = ExitStack()
        sb3 = lambda n, sh, dt, **kw: p3_es.enter_context(nc.sbuf_tensor(n, sh, dt, **kw))
        SUBS = c.MEGA // 128
        KC2 = c.KC // 2
        RW = KC2 * 128
        NW = c.INNER // c.NS
        NON = c.DM // c.NS
        assert NW >= 2 * SUBS and RW <= 512 and c.DM == 2 * RW
        aofm = sb3("aofm", [128, KC2 * 128], F32)
        aofm_v = aofm[:].rearrange("p (k t) -> p k t", k=KC2)
        xt = [sb3(f"xt{i}", [128, c.DM], F32) for i in range(2)]
        res1b = [sb3(f"res1{i}", [128, SUBS * c.DM], F32) for i in range(2)]
        res1v = [x[:].rearrange("p (u f) -> p u f", u=SUBS) for x in res1b]
        ln2x = sb3("ln2x", [128, c.DM], F32)
        qi_b = sb3("qi_b", [128, c.DM], I32)
        qtm_b = sb3("qtm_b", [128, c.DM], BF16)
        zf_b = sb3("zf_b", [128, c.DM], BF16)
        xq_fm = [sb3(f"xq_fm{i}", [128, c.KC * c.MEGA], BF16) for i in range(2)]
        xq_fv = [x[:].rearrange("p (k t) -> p k t", k=c.KC) for x in xq_fm]
        alg = sb3("alg", [128, 2 * SUBS], F32)
        alu_ = sb3("alu", [128, 2 * SUBS], F32)
        gam = sb3("gam", [128, SUBS], F32)
        tmp1 = sb3("tmp1", [128, 4], F32)
        st6b = sb3("st6b", [128, 6 * nst], F32)
        mv2b = sb3("mv2b", [128, 2], F32)
        sdvb = sb3("sdvb", [128, 3], F32)
        gsl = [sb3(f"gsl{i}", [128, c.NS], F32) for i in range(2)]
        wgsl = [sb3(f"wgsl{i}", [128, c.KC * c.NS], BF16) for i in range(2)]
        wgsl_v = [x[:].rearrange("p (k t) -> p k t", k=c.KC) for x in wgsl]
        wusl = [sb3(f"wusl{i}", [128, c.KC * c.NS], BF16) for i in range(2)]
        wusl_v = [x[:].rearrange("p (k t) -> p k t", k=c.KC) for x in wusl]
        hq_fm = sb3("hq_fm", [128, c.IC * c.MEGA], BF16)
        hq_fv = hq_fm[:].rearrange("p (k t) -> p k t", k=c.IC)
        wdsl = [sb3(f"wdsl{i}", [128, c.IC * c.NS], BF16) for i in range(2)]
        wdsl_v = [x[:].rearrange("p (k t) -> p k t", k=c.IC) for x in wdsl]
        hh = sb3("hh", [128, SUBS * c.INNER], F32, side="right")
        hh_v = hh[:].rearrange("p (u f) -> p u f", u=SUBS)

        wgt_sv = wgt_s.rearrange("(k p) t -> p k t", p=128)
        wut_sv = wut_s.rearrange("(k p) t -> p k t", p=128)
        wdt_sv = wdt_s.rearrange("(k p) t -> p k t", p=128)
        aog_sv = aog_s.rearrange("(o p) t -> p o t", p=128)

        st = {
            "yg": None, "yg_use": None,
            "xt": [None, None], "xt_use": [None, None],
            "res1_rd": [None, None],
            "ptp2_rd": None, "ptp_rd": None,
            "xqi_rd": None, "zfb_rd": None, "qtm_rd": None,
            "xq_rd": [None, None], "xq_rdy": [None, None],
            "a_qtm": None,
            "hq_rdy": [None] * SUBS, "hqfm_rd": None,
            "wgu_tok": [None, None], "wgu_use": [None, None], "wg_next": 0,
            "wd_tok": [None, None], "wd_use": [None, None], "wd_next": 0,
            "wd_last": None,
            "hh_rd": None, "hh_done": None,
            "gsl_rd": [None, None], "outb": [None, None],
        }
        pbrd = {i: None for i in range(6)}

        def emit_A_load(mg, sub):
            t0 = mg * c.MEGA + sub * 128
            sl = sub % 2

            def b_ao(sy, t0=t0):
                em.wait("qd", sy, st["yg_use"])
                st["yg"] = em.dma_on("qao", sy.dma_start(
                    out=aofm_v, in_=aog_sv[:, 0:KC2, t0:t0 + 128]))
            block.sync(b_ao)

            def b_xl(sy, t0=t0, sl=sl):
                em.wait("qd", sy, st["xt_use"][sl])
                st["xt"][sl] = em.dma_on("qxo", sy.dma_start(
                    out=xt[sl][:], in_=x_own[t0:t0 + 128, :]))
            block.sync(b_xl)

        def emit_A_main(mg, sub):
            par = mg % 2
            sl = sub % 2
            for r in range(2):
                if r == 1:
                    def b_ao1(sy, mg=mg, sub=sub):
                        em.wait("qd", sy, st["yg_use"])
                        t0 = mg * c.MEGA + sub * 128
                        st["yg"] = em.dma_on("qao", sy.dma_start(
                            out=aofm_v, in_=aog_sv[:, KC2:c.KC, t0:t0 + 128]))
                    block.sync(b_ao1)

                def b_aotp(t):
                    em.wait("t", t, st["yg"], pbrd[4])
                    for oc in range(KC2):
                        ins = t.transpose(pb[4][:, oc * 128:(oc + 1) * 128],
                                          aofm_v[:, oc, :], ident_f[:])
                    st["yg_use"] = em.do("t", ins)
                block.tensor(b_aotp)
                t_aotp = em.tok("t")

                def b_res1(v, r=r, sub=sub, sl=sl, par=par):
                    em.pre("v", v, t_aotp, st["xt"][sl], st["res1_rd"][par])
                    em.do("v", v.tensor_tensor(
                        res1v[par][:, sub, r * RW:(r + 1) * RW],
                        pb[4][:, 0:RW], xt[sl][:, r * RW:(r + 1) * RW], ALU.add))
                block.vector(b_res1)
                pbrd[4] = em.tok("v")
            st["xt_use"][sl] = em.tok("v")
            t_r1 = em.tok("v")

            def b_ln2a(v, sub=sub, par=par):
                em.pre("v", v, t_r1)
                for j in range(nst):
                    em.pre("v", v)
                    em.do("v", v.bn_stats(st6b[:, 6 * j:6 * j + 6],
                                          res1v[par][:, sub, j * wst:(j + 1) * wst]))
                em.pre("v", v)
                em.do("v", v.bn_aggr(mv2b[:], st6b[:]))
                em.pre("v", v)
                em.do("v", v.tensor_scalar(sdvb[:, 0:1], mv2b[:, 1:2], EPS, None, ALU.add))
            block.vector(b_ln2a)
            t_l2a = em.tok("v")

            def b_ln2b(s):
                em.pre("s", s, t_l2a)
                em.do("s", s.activation(sdvb[:, 0:1], sdvb[:, 0:1], AF.Sqrt))
            block.scalar(b_ln2b)
            t_l2b = em.tok("s")

            cidx = par * SUBS + sub

            def b_aq(v, sub=sub, par=par, cidx=cidx):
                em.pre("v", v, t_l2b)
                em.do("v", v.reciprocal(sdvb[:, 1:2], sdvb[:, 0:1]))
                em.pre("v", v)
                em.do("v", v.tensor_scalar(sdvb[:, 2:3], mv2b[:, 0:1], sdvb[:, 1:2],
                                           -1.0, ALU.mult, ALU.mult))
                em.pre("v", v)
                em.do("v", v.tensor_scalar(ln2x[:], res1v[par][:, sub, :],
                                           sdvb[:, 1:2], sdvb[:, 2:3], ALU.mult, ALU.add))
                em.pre("v", v)
                em.do("v", v.reduce_max(tmp1[:, 0:1], ln2x[:], axis=AX.X,
                                        apply_absolute_value=True))
                em.pre("v", v)
                em.do("v", v.tensor_scalar(tmp1[:, 1:2], tmp1[:, 0:1], 1e-5, None, ALU.max))
                em.pre("v", v)
                em.do("v", v.reciprocal(tmp1[:, 2:3], tmp1[:, 1:2]))
                em.pre("v", v)
                em.do("v", v.tensor_scalar(tmp1[:, 3:4], tmp1[:, 2:3], 127.0, None, ALU.mult))
                em.pre("v", v)
                em.do("v", v.tensor_scalar(alg[:, cidx:cidx + 1], tmp1[:, 1:2],
                                           mv_m[:, 0:1], 1.0 / 127.0, ALU.mult, ALU.mult))
                em.pre("v", v)
                em.do("v", v.tensor_scalar(alu_[:, cidx:cidx + 1], tmp1[:, 1:2],
                                           mv_m[:, 1:2], 1.0 / 127.0, ALU.mult, ALU.mult))
                em.pre("v", v, st["xqi_rd"])
                em.do("v", v.tensor_scalar(qi_b[:], ln2x[:], tmp1[:, 3:4], None, ALU.mult))
                em.pre("v", v, st["zfb_rd"])
                em.do("v", v.tensor_copy(zf_b[:], qi_b[:]))
                em.pre("v", v)
                em.do("v", v.scalar_tensor_tensor(ln2x[:], ln2x[:], tmp1[:, 3:4],
                                                  zf_b[:], ALU.mult, ALU.subtract))
                em.pre("v", v, st["qtm_rd"])
                em.do("v", v.tensor_scalar(qtm_b[:], ln2x[:], 0.5, None, ALU.is_gt))
                em.pre("v", v)
                em.do("v", v.tensor_scalar(ln2x[:], ln2x[:], -0.5, None, ALU.is_lt))
                em.pre("v", v)
                em.do("v", v.tensor_tensor(zf_b[:], zf_b[:], qtm_b[:], ALU.add))
                em.pre("v", v)
                em.do("v", v.tensor_tensor(qtm_b[:], zf_b[:], ln2x[:], ALU.subtract))
            block.vector(b_aq)
            st["xqi_rd"] = em.tok("v")
            st["zfb_rd"] = em.tok("v")
            st["a_qtm"] = em.tok("v")

        def emit_A_tp(mg, sub):
            par = mg % 2

            def b_xqtp(t):
                em.wait("t", t, st["a_qtm"], st["ptp2_rd"])
                for oc in range(c.KC):
                    ins = t.transpose(ptp2[:, oc * 128:(oc + 1) * 128],
                                      qtm_b[:, oc * 128:(oc + 1) * 128], ident[:])
                em.do("t", ins)
            block.tensor(b_xqtp)
            st["qtm_rd"] = em.tok("t")
            t_xtp = em.tok("t")

            def b_xqev(v, sub=sub, par=par):
                em.pre("v", v, t_xtp, st["xq_rd"][par])
                em.do("v", v.tensor_copy(
                    xq_fv[par][:, :, sub * 128:(sub + 1) * 128],
                    ptp2[:, 0:c.DM].rearrange("p (k t) -> p k t", k=c.KC)))
            block.vector(b_xqev)
            st["ptp2_rd"] = em.tok("v")

        def emit_wgu_load():
            g = st["wg_next"]
            if g >= c.NMEGA * NW:
                return
            st["wg_next"] = g + 1
            isl = g % NW
            sl = g % 2

            def b_wgu(sy, isl=isl, sl=sl):
                em.wait("qd", sy, st["wgu_use"][sl])
                tg = em.dma_on(f"qg{sl}", sy.dma_start(
                    out=wgsl_v[sl], in_=wgt_sv[:, :, isl * c.NS:(isl + 1) * c.NS]))
                tu = em.dma_on(f"qg2{sl}", sy.dma_start(
                    out=wusl_v[sl], in_=wut_sv[:, :, isl * c.NS:(isl + 1) * c.NS]))
                st["wgu_tok"][sl] = (tg, tu)
            block.sync(b_wgu)

        def emit_wd_load():
            g = st["wd_next"]
            if g >= c.NMEGA * NON:
                return
            st["wd_next"] = g + 1
            on = g % NON
            sl = g % 2

            def b_wd(sy, on=on, sl=sl):
                em.wait("qd", sy, st["wd_use"][sl])
                st["wd_tok"][sl] = em.dma_on(f"qwd{sl}", sy.dma_start(
                    out=wdsl_v[sl], in_=wdt_sv[:, :, on * c.NS:(on + 1) * c.NS]))
            block.sync(b_wd)

        def emit_BC_isl(mg, isl):
            par = mg % 2
            wsl = (mg * NW + isl) % 2

            sfut = em.cnt["s"]
            vfut0 = em.cnt["v"]
            tmm = []
            for sub in range(SUBS):
                gb = 2 * (sub % 2)
                ub = gb + 1

                def b_gu_mm(t, sub=sub, gb=gb, ub=ub, par=par, wsl=wsl):
                    em.wait("t", t, st["wgu_tok"][wsl][0], st["wgu_tok"][wsl][1],
                            st["xq_rdy"][par])
                    if sub < 2:
                        em.wait("t", t, pbrd[gb], pbrd[ub])
                    else:
                        em.wait("t", t, ("s", sfut + (sub - 2) + 1),
                                ("v", vfut0 + (sub - 2) + 1))
                    for kc in range(c.KC):
                        t.matmul(pb[gb][:, 0:c.NS],
                                 xq_fv[par][:, kc, sub * 128:(sub + 1) * 128],
                                 wgsl_v[wsl][:, kc, :], start=(kc == 0),
                                 stop=(kc == c.KC - 1))
                    for kc in range(c.KC):
                        ins = t.matmul(pb[ub][:, 0:c.NS],
                                       xq_fv[par][:, kc, sub * 128:(sub + 1) * 128],
                                       wusl_v[wsl][:, kc, :], start=(kc == 0),
                                       stop=(kc == c.KC - 1))
                    tk = em.do("t", ins)
                    if sub == SUBS - 1:
                        st["wgu_use"][wsl] = tk
                    st["xq_rd"][par] = tk
                block.tensor(b_gu_mm)
                tmm.append(em.tok("t"))
            emit_wgu_load()

            def b_silu(s, par=par):
                for sub in range(SUBS):
                    gb = 2 * (sub % 2)
                    cidx = par * SUBS + sub
                    em.pre("s", s, tmm[sub], st["gsl_rd"][sub % 2],
                           st["outb"][sub % 2])
                    em.do("s", s.activation(gsl[sub % 2][:], pb[gb][:, 0:c.NS], AF.Silu,
                                            scale=alg[:, cidx:cidx + 1]))
            block.scalar(b_silu)
            assert em.cnt["s"] == sfut + SUBS
            for sub in range(SUBS):
                pbrd[2 * (sub % 2)] = ("s", sfut + sub + 1)

            def b_hh(v, isl=isl):
                for sub in range(SUBS):
                    ub = 2 * (sub % 2) + 1
                    em.pre("v", v, ("s", sfut + sub + 1), st["hh_rd"])
                    em.do("v", v.tensor_tensor(
                        hh_v[:, sub, isl * c.NS:(isl + 1) * c.NS], gsl[sub % 2][:],
                        pb[ub][:, 0:c.NS], ALU.mult))
            block.vector(b_hh)
            assert em.cnt["v"] == vfut0 + SUBS
            for sub in range(SUBS):
                pbrd[2 * (sub % 2) + 1] = ("v", vfut0 + sub + 1)
                st["gsl_rd"][sub % 2] = ("v", vfut0 + sub + 1)
            st["hh_done"] = ("v", vfut0 + SUBS)

        npc = max(1, c.INNER // c.DM)
        wpc = c.INNER // npc
        ntp = wpc // 128

        def emit_D_sub(mg, sub):
            par = mg % 2
            cidx = par * SUBS + sub

            def b_hq(v, sub=sub, cidx=cidx):
                em.pre("v", v, st["hh_done"])
                em.do("v", v.reduce_max(tmp1[:, 0:1], hh_v[:, sub, :], axis=AX.X,
                                        apply_absolute_value=True))
                em.pre("v", v)
                em.do("v", v.tensor_scalar(tmp1[:, 1:2], tmp1[:, 0:1],
                                           alu_[:, cidx:cidx + 1], None, ALU.mult))
                em.pre("v", v)
                em.do("v", v.tensor_scalar(tmp1[:, 1:2], tmp1[:, 1:2], 1e-5, None, ALU.max))
                em.pre("v", v)
                em.do("v", v.reciprocal(tmp1[:, 2:3], tmp1[:, 1:2]))
                em.pre("v", v)
                em.do("v", v.tensor_scalar(tmp1[:, 3:4], tmp1[:, 2:3],
                                           alu_[:, cidx:cidx + 1], 127.0, ALU.mult, ALU.mult))
                em.pre("v", v)
                em.do("v", v.tensor_scalar(gam[:, sub:sub + 1], tmp1[:, 1:2],
                                           mv_m[:, 2:3], 1.0 / 127.0, ALU.mult, ALU.mult))
            block.vector(b_hq)

            for pc in range(npc):
                def b_hp(v, sub=sub, pc=pc):
                    em.pre("v", v, st["xqi_rd"])
                    em.do("v", v.tensor_scalar(qi_b[:, 0:wpc],
                                               hh_v[:, sub, pc * wpc:(pc + 1) * wpc],
                                               tmp1[:, 3:4], None, ALU.mult))
                    em.pre("v", v, st["zfb_rd"])
                    em.do("v", v.tensor_copy(zf_b[:, 0:wpc], qi_b[:, 0:wpc]))
                    em.pre("v", v)
                    em.do("v", v.scalar_tensor_tensor(
                        hh_v[:, sub, pc * wpc:(pc + 1) * wpc],
                        hh_v[:, sub, pc * wpc:(pc + 1) * wpc], tmp1[:, 3:4],
                        zf_b[:, 0:wpc], ALU.mult, ALU.subtract))
                    em.pre("v", v, st["qtm_rd"])
                    em.do("v", v.tensor_scalar(qtm_b[:, 0:wpc],
                                               hh_v[:, sub, pc * wpc:(pc + 1) * wpc],
                                               0.5, None, ALU.is_gt))
                    em.pre("v", v)
                    em.do("v", v.tensor_scalar(hh_v[:, sub, pc * wpc:(pc + 1) * wpc],
                                               hh_v[:, sub, pc * wpc:(pc + 1) * wpc],
                                               -0.5, None, ALU.is_lt))
                    em.pre("v", v)
                    em.do("v", v.tensor_tensor(zf_b[:, 0:wpc], zf_b[:, 0:wpc],
                                               qtm_b[:, 0:wpc], ALU.add))
                    em.pre("v", v)
                    em.do("v", v.tensor_tensor(qtm_b[:, 0:wpc], zf_b[:, 0:wpc],
                                               hh_v[:, sub, pc * wpc:(pc + 1) * wpc],
                                               ALU.subtract))
                block.vector(b_hp)
                st["xqi_rd"] = em.tok("v")
                st["zfb_rd"] = em.tok("v")
                t_hqc = em.tok("v")

                def b_hqtp(t):
                    em.wait("t", t, t_hqc, st["ptp_rd"])
                    for j in range(ntp):
                        ins = t.transpose(ptp[:, j * 128:(j + 1) * 128],
                                          qtm_b[:, j * 128:(j + 1) * 128], ident[:])
                    em.do("t", ins)
                block.tensor(b_hqtp)
                st["qtm_rd"] = em.tok("t")
                t_htp = em.tok("t")

                def b_hqev(v, sub=sub, pc=pc):
                    em.pre("v", v, t_htp,
                           st["hqfm_rd"] if (sub == 0 and pc == 0) else None)
                    em.do("v", v.tensor_copy(
                        hq_fv[:, pc * ntp:(pc + 1) * ntp, sub * 128:(sub + 1) * 128],
                        ptp[:, 0:ntp * 128].rearrange("p (k t) -> p k t", k=ntp)))
                block.vector(b_hqev)
                st["ptp_rd"] = em.tok("v")
            st["hq_rdy"][sub] = em.tok("v")
            if sub == SUBS - 1:
                st["hh_rd"] = em.tok("v")

        def emit_E_mm(mg, on, sub):
            sl = sub % 2
            dsl = (mg * NON + on) % 2

            def b_d_mm(t, on=on, sub=sub, sl=sl, dsl=dsl):
                em.wait("t", t, st["wd_tok"][dsl], st["hq_rdy"][sub], pbrd[2 + sl])
                for ic in range(c.IC):
                    ins = t.matmul(pb[2 + sl][:, 0:c.NS],
                                   hq_fv[:, ic, sub * 128:(sub + 1) * 128],
                                   wdsl_v[dsl][:, ic, :], start=(ic == 0),
                                   stop=(ic == c.IC - 1))
                tk = em.do("t", ins)
                if sub == SUBS - 1:
                    st["wd_use"][dsl] = tk
                st["hqfm_rd"] = tk
            block.tensor(b_d_mm)
            return em.tok("t")

        def make_ev(mg, on, sub, t_dm):
            sl = sub % 2
            par = mg % 2

            def run():
                def b_d_ev(v, on=on, sub=sub, sl=sl, par=par):
                    em.pre("v", v, t_dm, st["outb"][sl], st["gsl_rd"][sl])
                    em.do("v", v.scalar_tensor_tensor(
                        gsl[sl][:], pb[2 + sl][:, 0:c.NS], gam[:, sub:sub + 1],
                        res1v[par][:, sub, on * c.NS:(on + 1) * c.NS],
                        ALU.mult, ALU.add))
                block.vector(b_d_ev)
                pbrd[2 + sl] = em.tok("v")
                if on == NON - 1 and sub == SUBS - 1:
                    st["res1_rd"][par] = em.tok("v")
                t_de = em.tok("v")

                def b_out(sy, on=on, sub=sub, sl=sl):
                    em.wait("qd", sy, t_de)
                    t0 = mg * c.MEGA + sub * 128
                    st["outb"][sl] = em.dma_on(f"qo{sl}", sy.dma_start(
                        out=out[t0:t0 + 128, on * c.NS:(on + 1) * c.NS],
                        in_=gsl[sl][:]))
                block.sync(b_out)
            return run

        emit_wgu_load()
        emit_wgu_load()
        emit_wd_load()
        emit_wd_load()
        for sub in range(SUBS):
            emit_A_load(0, sub)
            emit_A_main(0, sub)
            emit_A_tp(0, sub)
        st["xq_rdy"][0] = em.tok("v")

        for mg in range(c.NMEGA):
            nxt = mg + 1

            for isl in range(NW):
                emit_BC_isl(mg, isl)
                if nxt < c.NMEGA:
                    if isl == 0:
                        emit_A_load(nxt, 0)
                    elif isl % 2 == 1:
                        s = isl // 2
                        if s < SUBS:
                            if s >= 1:
                                emit_A_tp(nxt, s - 1)
                            emit_A_main(nxt, s)
                            if s + 1 < SUBS:
                                emit_A_load(nxt, s + 1)
            if nxt < c.NMEGA:
                emit_A_tp(nxt, SUBS - 1)
                st["xq_rdy"][nxt % 2] = em.tok("v")

            for on in range(NON):
                prev = None
                for sub in range(SUBS):
                    if on == 0:
                        emit_D_sub(mg, sub)
                    t_dm = emit_E_mm(mg, on, sub)
                    if prev is not None:
                        prev()
                    prev = make_ev(mg, on, sub, t_dm)
                prev()
                emit_wd_load()

        if dbg:
            def b_dbg(sy):
                em.wait("qd", sy, *em.all_tokens())
                em.dma("qd", sy.dma_start(out=d_sv[:, 0:3], in_=sv_m[:]))
                em.dma("qd", sy.dma_start(out=d_sv[:, 3:6], in_=mv_m[:]))
                em.dma("qd", sy.dma_start(out=d_wgt, in_=wgt_s[:, 0:512]))
                em.dma("qd", sy.dma_start(out=d_wdt, in_=wdt_s[0:512, :]))
                em.dma("qd", sy.dma_start(out=d_hh, in_=hh[:, 0:c.INNER]))
                em.dma("qd", sy.dma_start(out=d_r1, in_=res1b[0][:, 0:c.DM]))
                em.dma("qd", sy.dma_start(out=d_al[:, 0:c.MEGA // 128], in_=alg[:, 0:c.MEGA // 128]))
                em.dma("qd", sy.dma_start(out=d_al[:, c.MEGA // 128:2 * (c.MEGA // 128)], in_=alu_[:, 0:c.MEGA // 128]))
                em.dma("qd", sy.dma_start(out=d_al[:, 2 * (c.MEGA // 128):], in_=gam[:]))
                em.dma("qd", sy.dma_start(out=d_lnx, in_=lnx_s))
                for h in range(c.NHL):
                    em.dma("qd", sy.dma_start(out=d_qcs[h * 64:(h + 1) * 64, :],
                                              in_=qs_s[h * 128:h * 128 + 64, :]))
                    em.dma("qd", sy.dma_start(out=d_qrs[h * 32:(h + 1) * 32, :],
                                              in_=qs_s[h * 128 + 64:h * 128 + 96, :]))
                em.dma("qd", sy.dma_start(out=d_aop, in_=aop_s))
                em.dma("qd", sy.dma_start(out=d_aog, in_=aog_s))
                em.dma("qd", sy.dma_start(out=d_cq, in_=cq_s[:, 0:512]))
            block.sync(b_dbg)

            def b_dbg2(g):
                em.wait("gd", g, *em.all_tokens())
                em.dma("gd", g.dma_start(out=d_maxk, in_=maxk[:]))
            block.gpsimd(b_dbg2)

        barrier(em, block)
        p3_es.close()

    return nc


def _shard_inputs(c, inputs):
    x = np.ascontiguousarray(inputs["L0_attn_x"])
    W = {k: np.ascontiguousarray(v) for k, v in inputs.items()}
    maps = []
    for core in range(2 * c.n_pairs):
        p, hc = core // 2, core % 2
        m = {
            "x_seq": x[p],
            "x_own": x[p, hc * c.OWN:(hc + 1) * c.OWN],
            "wdq": W["L0_attn_W_DQ"],
            "wuq": W["L0_attn_W_UQ"][:, hc * c.YD:(hc + 1) * c.YD],
            "wqr": W["L0_attn_W_QR"][:, hc * c.NHL * 32:(hc + 1) * c.NHL * 32],
            "wdkv": W["L0_attn_W_DKV"],
            "wuk": W["L0_attn_W_UK"][:, hc * c.YD:(hc + 1) * c.YD],
            "wuv": W["L0_attn_W_UV"][:, hc * c.YD:(hc + 1) * c.YD],
            "wkr": W["L0_attn_W_KR"],
            "wo": W["L0_attn_W_O"][hc * c.YD:(hc + 1) * c.YD, :],
            "wg": W["L0_ffn_w_gate"],
            "wu": W["L0_ffn_w_up"],
            "wd": W["L0_ffn_w_down"],
        }
        maps.append({k: np.ascontiguousarray(v) for k, v in m.items()})
    return maps


def kernel(**inputs):
    c = FULL
    nc = build_nc(c)
    in_maps = _shard_inputs(c, inputs)
    res = run_bass_kernel_spmd(nc, in_maps, core_ids=list(range(2 * c.n_pairs)))
    B = c.n_pairs
    outp = np.empty((B, c.T, c.DM), dtype=np.float32)
    for core in range(2 * B):
        p, hc = core // 2, core % 2
        outp[p, hc * c.OWN:(hc + 1) * c.OWN] = res.results[core]["out"]
    return outp


# version 26
# speedup vs baseline: 1.2161x; 1.0041x over previous
import math
from contextlib import ExitStack

import numpy as np
import ml_dtypes

import concourse.bass as bass
import concourse.mybir as mybir
from concourse.bass_utils import run_bass_kernel_spmd

F32 = mybir.dt.float32
BF16 = mybir.dt.bfloat16
I32 = mybir.dt.int32
AF = mybir.ActivationFunctionType
ALU = mybir.AluOpType
AX = mybir.AxisListType

RS96 = 1.0 / math.sqrt(96.0)
EPS = 1e-5
TWO_PI = 2.0 * math.pi


class CFG:
    def __init__(self, T, DM, CQ, INNER, NHL, n_pairs, SLOT, NS, MEGA=None):
        self.T, self.DM, self.CQ, self.INNER = T, DM, CQ, INNER
        self.NHL, self.n_pairs, self.SLOT, self.NS = NHL, n_pairs, SLOT, NS
        self.DC = 256
        self.KC = DM // 128
        self.CQC = CQ // 128
        self.NKT = T // 128
        self.NSLOT = T // SLOT
        self.OWN = T // 2
        self.YD = NHL * 64
        self.YC = self.YD // 128
        self.QRC = NHL * 32 // 128
        self.IC = INNER // 128
        self.MEGA = MEGA if MEGA is not None else min(512, self.OWN)
        self.NMEGA = self.OWN // self.MEGA
        assert NHL % 2 == 0 and self.YC >= 1 and self.QRC >= 1
        assert SLOT % 128 == 0 and NS <= 512 and T % SLOT == 0 and T % NS == 0


FULL = CFG(T=4096, DM=1024, CQ=4096, INNER=4096, NHL=8, n_pairs=4, SLOT=512, NS=512,
           MEGA=256)
MINI = CFG(T=512, DM=512, CQ=512, INNER=512, NHL=4, n_pairs=1, SLOT=128, NS=128,
           MEGA=128)


class Em:
    STREAMS = ("t", "v", "s", "g", "qd", "gd", "cc")

    def __init__(self, nc, es):
        self.nc = nc
        self.es = es
        self.sem = {k: es.enter_context(nc.semaphore("sem_" + k)) for k in self.STREAMS}
        self.cnt = {k: 0 for k in self.STREAMS}
        self.waited = {}

    def dma_on(self, name, instr):
        if name not in self.sem:
            self.sem[name] = self.es.enter_context(self.nc.semaphore("sem_" + name))
            self.cnt[name] = 0
        return self.do(name, instr, 16)

    def tok(self, s):
        return (s, self.cnt[s])

    def _wait(self, ename, eng, s, c):
        if c <= 0:
            return
        key = (ename, s)
        if self.waited.get(key, 0) >= c:
            return
        eng.wait_ge(self.sem[s], c)
        self.waited[key] = c

    def wait(self, ename, eng, *tokens):
        need = {}
        for t in tokens:
            if t is None:
                continue
            s, cv = t
            if cv > need.get(s, 0):
                need[s] = cv
        for s in sorted(need):
            self._wait(ename, eng, s, need[s])

    def pre(self, ename, eng, *tokens, selfsync=True):
        if selfsync and ename in ("v", "s", "g"):
            self._wait(ename, eng, ename, self.cnt[ename])
        self.wait(ename, eng, *tokens)

    def do(self, s, instr, n=1):
        instr.then_inc(self.sem[s], n)
        self.cnt[s] += n
        return (s, self.cnt[s])

    def dma(self, s, instr):
        return self.do(s, instr, 16)

    def all_tokens(self):
        return [(s, cv) for s, cv in self.cnt.items() if cv > 0]


def barrier(em, block):
    toks = em.all_tokens()
    for name, dec in (("t", block.tensor), ("v", block.vector), ("s", block.scalar),
                      ("g", block.gpsimd), ("qd", block.sync)):
        def body(eng, _t=tuple(toks), _n=name):
            em.wait(_n, eng, *_t)
        dec(body)


def build_nc(cfg, dbg=False):
    c = cfg
    nc = bass.Bass("TRN2", target_bir_lowering=False)

    x_seq = nc.dram_tensor("x_seq", [c.T, c.DM], F32, kind="ExternalInput").ap()
    x_own = nc.dram_tensor("x_own", [c.OWN, c.DM], F32, kind="ExternalInput").ap()
    wdq = nc.dram_tensor("wdq", [c.DM, c.CQ], F32, kind="ExternalInput").ap()
    wuq = nc.dram_tensor("wuq", [c.CQ, c.YD], F32, kind="ExternalInput").ap()
    wqr = nc.dram_tensor("wqr", [c.CQ, c.NHL * 32], F32, kind="ExternalInput").ap()
    wdkv = nc.dram_tensor("wdkv", [c.DM, c.DC], F32, kind="ExternalInput").ap()
    wuk = nc.dram_tensor("wuk", [c.DC, c.YD], F32, kind="ExternalInput").ap()
    wuv = nc.dram_tensor("wuv", [c.DC, c.YD], F32, kind="ExternalInput").ap()
    wkr = nc.dram_tensor("wkr", [c.DM, 32], F32, kind="ExternalInput").ap()
    wo = nc.dram_tensor("wo", [c.YD, c.DM], F32, kind="ExternalInput").ap()
    wg = nc.dram_tensor("wg", [c.INNER, c.DM], F32, kind="ExternalInput").ap()
    wu = nc.dram_tensor("wu", [c.INNER, c.DM], F32, kind="ExternalInput").ap()
    wd = nc.dram_tensor("wd", [c.DM, c.INNER], F32, kind="ExternalInput").ap()
    out = nc.dram_tensor("out", [c.OWN, c.DM], F32, kind="ExternalOutput").ap()
    if dbg:
        d_lnx = nc.dram_tensor("d_lnx", [c.DM, c.T], BF16, kind="ExternalOutput").ap()
        d_qcs = nc.dram_tensor("d_qcs", [c.YD, c.T], BF16, kind="ExternalOutput").ap()
        d_qrs = nc.dram_tensor("d_qrs", [c.NHL * 32, c.T], BF16, kind="ExternalOutput").ap()
        d_aop = nc.dram_tensor("d_aop", [2, c.DM, c.OWN], F32, kind="ExternalOutput").ap()
        d_aog = nc.dram_tensor("d_aog", [c.DM, c.OWN], F32, kind="ExternalOutput").ap()
        d_maxk = nc.dram_tensor("d_maxk", [c.NHL, 1], F32, kind="ExternalOutput").ap()
        d_cq = nc.dram_tensor("d_cq", [c.CQ, 512], BF16, kind="ExternalOutput").ap()
        d_scale = nc.dram_tensor("d_scale", [32, c.T], F32, kind="ExternalOutput").ap()
        d_ini = nc.dram_tensor("d_ini", [32, 2050], F32, kind="ExternalOutput").ap()
        d_sv = nc.dram_tensor("d_sv", [128, 6], F32, kind="ExternalOutput").ap()
        d_wgt = nc.dram_tensor("d_wgt", [c.DM, 512], BF16, kind="ExternalOutput").ap()
        d_wdt = nc.dram_tensor("d_wdt", [512, c.DM], BF16, kind="ExternalOutput").ap()
        d_hh = nc.dram_tensor("d_hh", [128, c.INNER], F32, kind="ExternalOutput").ap()
        d_r1 = nc.dram_tensor("d_r1", [128, c.DM], F32, kind="ExternalOutput").ap()
        d_al = nc.dram_tensor("d_al", [128, 3 * (c.MEGA // 128)], F32, kind="ExternalOutput").ap()
        d_scp4 = nc.dram_tensor("d_scp4", [128, c.T], F32, kind="ExternalOutput").ap()

    lnx_s = nc.dram_tensor("lnx_s", [c.DM, c.T], BF16, kind="Internal").ap()
    cq_s = nc.dram_tensor("cq_s", [c.CQ, c.T], BF16, kind="Internal").ap()
    qs_s = nc.dram_tensor("qs_s", [c.NHL * 128, c.T], BF16, kind="Internal").ap()
    aop_s = nc.dram_tensor("aop_s", [2, c.DM, c.OWN], F32, kind="Internal").ap()
    aog_s = nc.dram_tensor("aog_s", [c.DM, c.OWN], F32, kind="Internal").ap()
    wdq_bf = nc.dram_tensor("wdq_bf", [c.DM, c.CQ], BF16, kind="Internal").ap()
    wuq_bf = nc.dram_tensor("wuq_bf", [c.CQ, c.YD], BF16, kind="Internal").ap()
    wqr_bf = nc.dram_tensor("wqr_bf", [c.CQ, c.NHL * 32], BF16, kind="Internal").ap()
    wo_bf = nc.dram_tensor("wo_bf", [c.YD, c.DM], BF16, kind="Internal").ap()
    wdkv_bf = nc.dram_tensor("wdkv_bf", [c.DM, c.DC], BF16, kind="Internal").ap()
    wuk_bf = nc.dram_tensor("wuk_bf", [c.DC, c.YD], BF16, kind="Internal").ap()
    wuv_bf = nc.dram_tensor("wuv_bf", [c.DC, c.YD], BF16, kind="Internal").ap()
    wkr_bf = nc.dram_tensor("wkr_bf", [c.DM, 32], BF16, kind="Internal").ap()
    wgt_s = nc.dram_tensor("wgt_s", [c.DM, c.INNER], BF16, kind="Internal").ap()
    wut_s = nc.dram_tensor("wut_s", [c.DM, c.INNER], BF16, kind="Internal").ap()
    wdt_s = nc.dram_tensor("wdt_s", [c.INNER, c.DM], BF16, kind="Internal").ap()

    lnx_sv = lnx_s.rearrange("(c p) t -> p c t", p=128)
    cq_sv = cq_s.rearrange("(o p) t -> p o t", p=128)
    qs_sv = qs_s.rearrange("(o p) t -> p o t", p=128)
    groups = [[2 * p, 2 * p + 1] for p in range(c.n_pairs)]

    es = ExitStack()
    sb = lambda n, sh, dt, **kw: es.enter_context(nc.sbuf_tensor(n, sh, dt, **kw))
    ps = lambda n, sh, dt: es.enter_context(nc.psum_tensor(n, sh, dt))
    with es:
        em = Em(nc, es)
        pb = [ps(f"pb{i}", [128, 512], F32) for i in range(6)]
        ptp = ps("ptp", [128, 1024], BF16)
        ptp2 = ps("ptp2", [128, 1024], BF16)
        PT = 6

        ident = sb("ident", [128, 128], BF16)
        tri = sb("tri", [128, 128], BF16)
        ones_f = sb("ones_f", [128, 128], F32)
        ident_f = sb("ident_f", [128, 128], F32)
        sv_m = sb("sv_m", [128, 3], F32)
        mv_m = sb("mv_m", [128, 3], F32)
        att_es = ExitStack()
        sba = lambda n, sh, dt, **kw: att_es.enter_context(nc.sbuf_tensor(n, sh, dt, **kw))
        ehot = sba("ehot", [c.NHL, c.NHL * 128], BF16)
        ehot_v = ehot[:].rearrange("p (h f) -> p h f", f=128)
        hkh = sba("hkh", [128, c.NHL * c.NHL], BF16)
        maxk = sba("maxk", [c.NHL, 1], F32)
        Kh = [sba(f"Kh{i}", [128, c.T], BF16) for i in range(c.NHL)]
        Vp = sba("Vp", [128, c.NKT * c.NHL * 65], BF16)
        Vp_v = Vp[:].rearrange("p (k h d) -> p k h d", h=c.NHL, d=65)
        scp_c = sba("scp_c", [32, c.T], F32)
        QW = 2048
        quant_es = ExitStack()
        sbq = lambda n, sh, dt: quant_es.enter_context(nc.sbuf_tensor(n, sh, dt))
        wnat3 = [sbq(f"wnat3{i}", [128, QW], F32) for i in range(1)]
        zf3 = sbq("zf3", [128, 1024], BF16)
        wabs3 = sbq("wabs3", [128, 4], F32)
        wsc3 = sbq("wsc3", [1, 2], F32)
        wstg3 = sbq("wstg3", [128, 1024], BF16)
        scale_es = ExitStack()
        scale_t = scale_es.enter_context(nc.sbuf_tensor("scale_t", [32, c.T], F32))
        wn2_es = ExitStack()
        wnat3.append(wn2_es.enter_context(nc.sbuf_tensor("wnat3x", [128, QW], F32)))

        block = es.enter_context(nc.Block())

        ini_es = ExitStack()
        sbi = lambda n, sh, dt: ini_es.enter_context(nc.sbuf_tensor(n, sh, dt))
        io_pf = sbi("io_pf", [128, 128], I32)
        io_j = sbi("io_j", [32, 1], I32)
        io_pos = sbi("io_pos", [1, 512], I32)
        pos_f = sbi("pos_f", [1, 512], F32)
        jf = sbi("jf", [32, 2], F32)
        invf = sbi("invf", [32, 1], F32)
        emb = sbi("emb", [32, 512], F32)
        rr = sbi("rr", [32, 512], F32)
        u32 = sbi("u32", [32, 512], I32)
        uf = sbi("uf", [32, 512], F32)
        sint = sbi("sint", [32, 512], F32)
        cpl = sbi("cpl", [32, 512], F32)
        pi4 = sbi("pi4", [32, 1], F32)
        io_e = sbi("io_e", [c.NHL, 128], I32)

        def ini_g(g):
            g.iota(io_pf[:], pattern=[[-1, 128]], base=0, channel_multiplier=1)
            g.iota(io_j[:], pattern=[[0, 1]], base=0, channel_multiplier=1)
            em.do("g", g.iota(io_e[:], pattern=[[0, 128]], base=0, channel_multiplier=1))
        block.gpsimd(ini_g)
        t_iota = em.tok("g")

        def ini_v(v):
            em.pre("v", v, t_iota)
            em.do("v", v.tensor_scalar(ident[:], io_pf[:], 0, None, ALU.is_equal))
            em.pre("v", v)
            em.do("v", v.memset(pi4[:], math.pi / 4.0))
            em.pre("v", v)
            em.do("v", v.tensor_scalar(tri[:], io_pf[:], 0, None, ALU.is_le))
            em.pre("v", v)
            em.do("v", v.tensor_scalar(ident_f[:], io_pf[:], 0, None, ALU.is_equal))
            em.pre("v", v)
            em.do("v", v.memset(ones_f[:], 1.0))
            em.pre("v", v)
            em.do("v", v.tensor_copy(jf[:, 0:1], io_j[:]))
            em.pre("v", v)
            em.do("v", v.tensor_scalar(jf[:, 1:2], jf[:, 0:1], 16.0, None, ALU.is_ge))
            em.pre("v", v)
            em.do("v", v.scalar_tensor_tensor(jf[:, 0:1], jf[:, 1:2], -16.0, jf[:, 0:1],
                                              ALU.mult, ALU.add))
            for h in range(c.NHL):
                em.pre("v", v)
                em.do("v", v.tensor_scalar(ehot_v[:, h, :], io_e[:], h, None, ALU.is_equal))
            em.pre("v", v)
            em.do("v", v.memset(maxk[:], 0.0))
            em.pre("v", v)
            em.do("v", v.memset(hkh[:], 0.0))
            for h in range(c.NHL):
                em.pre("v", v)
                em.do("v", v.memset(hkh[0:96, h * c.NHL + h: h * c.NHL + h + 1], 1.0))
            for h in range(c.NHL):
                em.pre("v", v)
                em.do("v", v.memset(Kh[h][:], 0.0))
        block.vector(ini_v)
        t_vini = em.tok("v")

        def ini_s(s):
            em.pre("s", s, t_vini)
            em.do("s", s.activation(invf[:], jf[:, 0:1], AF.Exp,
                                    scale=-math.log(10000.0) / 16.0))
        block.scalar(ini_s)
        t_invf = em.tok("s")

        wsl = min(512, c.T)
        nsl = c.T // wsl
        tok_posf_rd = None
        tok_iop_rd = None
        for i in range(nsl):
            def ini_gp(g, i=i):
                em.pre("g", g, tok_iop_rd)
                em.do("g", g.iota(io_pos[0:1, 0:wsl], pattern=[[1, wsl]], base=i * wsl,
                                  channel_multiplier=0))
            block.gpsimd(ini_gp)
            t_iop = em.tok("g")

            def ini_vp(v, i=i):
                em.pre("v", v, t_iop, tok_posf_rd)
                em.do("v", v.tensor_copy(pos_f[0:1, 0:wsl], io_pos[0:1, 0:wsl]))
            block.vector(ini_vp)
            tok_iop_rd = em.tok("v")
            t_pf = em.tok("v")

            def ini_tp(t, i=i):
                em.wait("t", t, t_invf, t_vini, t_pf)
                em.do("t", t.matmul(pb[0][0:32, 0:wsl], ones_f[0:1, 0:32],
                                    pos_f[0:1, 0:wsl], start=True, stop=True))
            block.tensor(ini_tp)
            tok_posf_rd = em.tok("t")
            t_posbc = em.tok("t")

            def ini_v3(v, i=i):
                em.pre("v", v, t_posbc)
                em.do("v", v.tensor_scalar(emb[:, 0:wsl], pb[0][0:32, 0:wsl],
                                           invf[:], None, ALU.mult))
                em.pre("v", v)
                em.do("v", v.tensor_scalar(u32[:, 0:wsl], emb[:, 0:wsl], 1.0 / TWO_PI, 0.125,
                                           ALU.mult, ALU.add))
                em.pre("v", v)
                em.do("v", v.tensor_copy(uf[:, 0:wsl], u32[:, 0:wsl]))
                em.pre("v", v)
                em.do("v", v.scalar_tensor_tensor(rr[:, 0:wsl], emb[:, 0:wsl], 1.0 / TWO_PI,
                                                  uf[:, 0:wsl], ALU.mult, ALU.subtract))
                em.pre("v", v)
                em.do("v", v.tensor_scalar(cpl[:, 0:wsl], rr[:, 0:wsl], 0.375, None, ALU.is_gt))
                em.pre("v", v)
                em.do("v", v.tensor_scalar(rr[:, 0:wsl], rr[:, 0:wsl], -0.625, None, ALU.is_lt))
                em.pre("v", v)
                em.do("v", v.scalar_tensor_tensor(cpl[:, 0:wsl], rr[:, 0:wsl], -1.0,
                                                  cpl[:, 0:wsl], ALU.mult, ALU.add))
                em.pre("v", v)
                em.do("v", v.tensor_tensor(uf[:, 0:wsl], uf[:, 0:wsl], cpl[:, 0:wsl], ALU.add))
                em.pre("v", v)
                em.do("v", v.scalar_tensor_tensor(rr[:, 0:wsl], uf[:, 0:wsl], -TWO_PI,
                                                  emb[:, 0:wsl], ALU.mult, ALU.add))
            block.vector(ini_v3)
            t_rr = em.tok("v")

            def ini_s2(s, i=i):
                em.pre("s", s, t_rr)
                em.do("s", s.activation(sint[:, 0:wsl], rr[:, 0:wsl], AF.Sin,
                                        bias=pi4[:], scale=1.0))
            block.scalar(ini_s2)
            t_sin = em.tok("s")

            def ini_v4(v, i=i):
                em.pre("v", v, t_sin)
                em.do("v", v.tensor_scalar(scale_t[:, i * wsl:(i + 1) * wsl], sint[:, 0:wsl],
                                           math.sqrt(2.0), None, ALU.mult))
                em.pre("v", v)
                em.do("v", v.tensor_scalar(scp_c[:, i * wsl:(i + 1) * wsl], sint[:, 0:wsl],
                                           math.sqrt(2.0) * RS96, None, ALU.mult))
            block.vector(ini_v4)
            if dbg and i == nsl - 1:
                def ini_dbg(g, i=i):
                    em.wait("gd", g, em.tok("v"))
                    em.dma("gd", g.dma_start(out=d_ini[:, 0:512], in_=emb[:, 0:512]))
                    em.dma("gd", g.dma_start(out=d_ini[:, 512:1024], in_=rr[:, 0:512]))
                    em.dma("gd", g.dma_start(out=d_ini[:, 1024:1536], in_=uf[:, 0:512]))
                    em.dma("gd", g.dma_start(out=d_ini[:, 1536:2048], in_=sint[:, 0:512]))
                    em.dma("gd", g.dma_start(out=d_ini[:, 2048:2049], in_=invf[:], allow_slow_non_contiguous=True))
                    em.dma("gd", g.dma_start(out=d_ini[:, 2049:2050], in_=jf[:, 0:1], allow_slow_non_contiguous=True))
                block.gpsimd(ini_dbg)
        t_scp = em.tok("v")

        def ini_g2(g):
            em.wait("gd", g, t_scp)
            em.pre("g", g)
            em.do("g", g.memset(Vp_v[:, :, :, 64:65], 1.0))
            em.dma_on("wbs", g.dma_start(out=wdkv_bf, in_=wdkv))
            em.dma_on("wbs", g.dma_start(out=wuk_bf, in_=wuk))
            em.dma_on("wbs", g.dma_start(out=wuv_bf, in_=wuv))
            em.dma_on("wbs", g.dma_start(out=wkr_bf, in_=wkr))
            em.dma_on("wbl", g.dma_start(out=wdq_bf, in_=wdq))
            em.dma_on("wbl", g.dma_start(out=wuq_bf, in_=wuq))
            em.dma_on("wbl", g.dma_start(out=wqr_bf, in_=wqr))
            em.dma_on("wbl", g.dma_start(out=wo_bf, in_=wo))
        block.gpsimd(ini_g2)
        t_wbs = em.tok("wbs")
        t_wbl = em.tok("wbl")
        t_scp4 = em.tok("gd")
        t_vones = em.tok("g")

        barrier(em, block)
        ini_es.close()

        q_state = {"wn": [None, None], "wn_use": [None, None], "zf_rd": None,
                   "zb_rd": [None, None], "stg": [None] * 4, "pb3": None,
                   "i": 0}
        wgt_v3 = wgt_s.rearrange("(j p) r -> p j r", p=128)
        wut_v3 = wut_s.rearrange("(j p) r -> p j r", p=128)
        wdt_v3 = wdt_s.rearrange("(j p) r -> p j r", p=128)

        def _mk_units():
            units = []
            specs = [(wg, wgt_v3, c.INNER, c.DM, 0), (wu, wut_v3, c.INNER, c.DM, 1),
                     (wd, wdt_v3, c.DM, c.INNER, 2)]

            def u_clear():
                def f():
                    def b(v):
                        em.pre("v", v)
                        em.do("v", v.memset(wabs3[:, 0:3], 0.0))
                    block.vector(b)
                return (None, f)

            def u_p1(wsrc, r0, nr, cc0, ncol, sl, mi):
                def fl():
                    wn1b = wnat3[0] if sl == 0 else wnat3[-1]

                    def b_l(sy):
                        em.wait("qd", sy, q_state["wn_use"][sl])
                        q_state["wn"][sl] = em.dma_on(f"qn{sl}", sy.dma_start(
                            out=wn1b[:, 0:nr * ncol].rearrange("p (j f) -> p j f", j=nr),
                            in_=wsrc[r0 * 128:(r0 + nr) * 128, cc0:cc0 + ncol].rearrange(
                                "(j p) f -> p j f", p=128)))
                    block.sync(b_l)

                def f():
                    wn1b = wnat3[0] if sl == 0 else wnat3[-1]

                    def b_a(v):
                        em.pre("v", v, q_state["wn"][sl])
                        em.do("v", v.reduce_sum(wabs3[:, 3:4], wn1b[:, 0:nr * ncol],
                                                axis=AX.X, apply_absolute_value=True))
                        em.pre("v", v)
                        q_state["wn_use"][sl] = em.do("v", v.tensor_tensor(
                            wabs3[:, mi:mi + 1], wabs3[:, mi:mi + 1], wabs3[:, 3:4],
                            ALU.add))
                    block.vector(b_a)
                return (fl, f)

            def u_fin(mi, RC):
                def f():
                    def b_s(t):
                        em.wait("t", t, em.tok("v"), q_state["pb3"])
                        em.do("t", t.matmul(pb[4][0:1, 0:1], wabs3[:, mi:mi + 1],
                                            ones_f[:, 0:1], start=True, stop=True))
                    block.tensor(b_s)
                    t_sm = em.tok("t")

                    def b_m(v):
                        em.pre("v", v, t_sm)
                        em.do("v", v.tensor_scalar(wsc3[0:1, 0:1], pb[4][0:1, 0:1],
                                                   1.0 / RC, 1e-5, ALU.mult, ALU.max))
                        em.pre("v", v)
                        em.do("v", v.reciprocal(wsc3[0:1, 1:2], wsc3[0:1, 0:1]))
                    block.vector(b_m)
                    t_mn = em.tok("v")

                    def b_b(t):
                        em.wait("t", t, t_mn)
                        t.matmul(pb[4][:, 0:1], ones_f[0:1, :], wsc3[0:1, 1:2],
                                 start=True, stop=True)
                        em.do("t", t.matmul(pb[4][:, 1:2], ones_f[0:1, :], wsc3[0:1, 0:1],
                                            start=True, stop=True))
                    block.tensor(b_b)
                    t_b1 = em.tok("t")

                    def b_c(v):
                        em.pre("v", v, t_b1)
                        em.do("v", v.tensor_copy(sv_m[:, mi:mi + 1], pb[4][:, 0:1]))
                        em.pre("v", v)
                        em.do("v", v.tensor_copy(mv_m[:, mi:mi + 1], pb[4][:, 1:2]))
                    block.vector(b_c)
                    q_state["pb3"] = em.tok("v")
                return (None, f)

            def u_p2(wsrc, wdst_v, mi, r0, nr, cc0, ncol, cc_total, sl):
                def fl():
                    wn3b = wnat3[0] if sl == 0 else wnat3[-1]

                    def b_l(sy):
                        em.wait("qd", sy, q_state["wn_use"][sl])
                        q_state["wn"][sl] = em.dma_on(f"qn{sl}", sy.dma_start(
                            out=wn3b[:, 0:nr * ncol].rearrange("p (j f) -> p j f", j=nr),
                            in_=wsrc[r0 * 128:(r0 + nr) * 128, cc0:cc0 + ncol].rearrange(
                                "(j p) f -> p j f", p=128)))
                    block.sync(b_l)

                def f():
                    wn3b = wnat3[0] if sl == 0 else wnat3[-1]
                    nrd = max(1, (nr * ncol) // 1024)
                    strip = min(1024, nr * ncol)
                    ccn = max(1, strip // ncol)
                    jbn = min(ncol, strip) // 128
                    for rd in range(nrd):
                        goff = cc0 + (rd * strip) % ncol
                        gch = r0 + (rd * strip) // ncol
                        jblk = goff // 128
                        ws_ = rd % 2
                        pdst = ptp if ws_ == 0 else ptp2

                        def b_q(v, rd=rd, strip=strip):
                            em.pre("v", v, q_state["wn"][sl], q_state["zb_rd"][0])
                            em.do("v", v.tensor_scalar(
                                zf3[:, 0:strip], wn3b[:, rd * strip:(rd + 1) * strip],
                                sv_m[:, mi:mi + 1], 0.5, ALU.mult, ALU.is_gt))
                            em.pre("v", v)
                            tk = em.do("v", v.tensor_scalar(
                                wn3b[:, rd * strip:(rd + 1) * strip],
                                wn3b[:, rd * strip:(rd + 1) * strip],
                                sv_m[:, mi:mi + 1], -0.5, ALU.mult, ALU.is_lt))
                            if rd == nrd - 1:
                                q_state["wn_use"][sl] = tk
                            em.pre("v", v)
                            em.do("v", v.tensor_tensor(
                                zf3[:, 0:strip], zf3[:, 0:strip],
                                wn3b[:, rd * strip:(rd + 1) * strip], ALU.subtract))
                        block.vector(b_q)
                        t_zb = em.tok("v")

                        def b_t(t, strip=strip, ws_=ws_, pdst=pdst, t_zb=t_zb):
                            em.wait("t", t, t_zb, t_vini, q_state["tp_rd"][ws_])
                            for j in range(strip // 128):
                                ins = t.transpose(pdst[:, j * 128:(j + 1) * 128],
                                                  zf3[:, j * 128:(j + 1) * 128], ident[:])
                            em.do("t", ins)
                        block.tensor(b_t)
                        q_state["zb_rd"][0] = em.tok("t")
                        t_t3 = em.tok("t")

                        def b_e(v, ws_=ws_, pdst=pdst, t_t3=t_t3, strip=strip):
                            em.pre("v", v, t_t3, q_state["stg"][0])
                            em.do("v", v.tensor_copy(wstg3[:, 0:strip], pdst[:, 0:strip]))
                        block.vector(b_e)
                        q_state["tp_rd"][ws_] = em.tok("v")
                        t_e3 = em.tok("v")

                        def b_w(sy, jblk=jblk, gch=gch, t_e3=t_e3,
                                ccn=ccn, jbn=jbn, strip=strip):
                            em.wait("qd", sy, t_e3)
                            tk = None
                            for cc_i in range(ccn):
                                tk = em.dma_on("qw0", sy.dma_start(
                                    out=wdst_v[:, jblk:jblk + jbn,
                                               (gch + cc_i) * 128:(gch + cc_i + 1) * 128],
                                    in_=wstg3[:, cc_i * jbn * 128:(cc_i + 1) * jbn * 128
                                              ].rearrange("p (j r) -> p j r", r=128)))
                            q_state["stg"][0] = tk
                        block.sync(b_w)
                    q_state["i"] += 1
                return (fl, f)

            si = [0]
            units.append(u_clear())
            for (wsrc, wdst_v, R, Cc, mi) in specs:
                nr = max(1, QW // Cc)
                ncol = min(Cc, QW)
                for r0 in range(0, R // 128, nr):
                    for cc0 in range(0, Cc, ncol):
                        units.append(u_p1(wsrc, r0, nr, cc0, ncol, si[0] % 2, mi))
                        si[0] += 1
            fin_start = len(units)
            for (wsrc, wdst_v, R, Cc, mi) in specs:
                units.append(u_fin(mi, float(R) * Cc))
            p2start = len(units)
            for (wsrc, wdst_v, R, Cc, mi) in specs:
                nr = max(1, QW // Cc)
                ncol = min(Cc, QW)
                for r0 in range(0, R // 128, nr):
                    for cc0 in range(0, Cc, ncol):
                        units.append(u_p2(wsrc, wdst_v, mi, r0, nr, cc0, ncol, Cc,
                                          si[0] % 2))
                        si[0] += 1
            return units, fin_start, p2start

        q_state["tp_rd"] = [None, None]
        p3p_units, p3p_fin, p3p_p2start = _mk_units()
        p3p_pos = {"cp": 0, "ld": 0}
        P3P_LA = 2

        def emit_p3p_flush():
            while p3p_pos["cp"] < p3p_pos["ld"]:
                fc = p3p_units[p3p_pos["cp"]][1]
                if fc is not None:
                    fc()
                p3p_pos["cp"] += 1

        def emit_p3p(n, cap=None):
            lim = len(p3p_units) if cap is None else cap
            while n > 0 and p3p_pos["cp"] < lim:
                while (p3p_pos["ld"] < lim
                       and p3p_pos["ld"] < p3p_pos["cp"] + P3P_LA):
                    fl = p3p_units[p3p_pos["ld"]][0]
                    if fl is not None:
                        fl()
                    p3p_pos["ld"] += 1
                fc = p3p_units[p3p_pos["cp"]][1]
                fc()
                p3p_pos["cp"] += 1
                n -= 1


        p1_es = ExitStack()
        sb1 = lambda n, sh, dt: p1_es.enter_context(nc.sbuf_tensor(n, sh, dt))
        wdkv_sb = sb1("wdkv_sb", [128, c.KC * 2 * 128], BF16)
        wdkv_v = wdkv_sb[:].rearrange("p (k o f) -> p k o f", k=c.KC, o=2)
        wuk_sb = sb1("wuk_sb", [128, 2 * c.YD], BF16)
        wuk_v = wuk_sb[:].rearrange("p (k f) -> p k f", k=2)
        wuv_sb = sb1("wuv_sb", [128, 2 * c.YD], BF16)
        wuv_v = wuv_sb[:].rearrange("p (k f) -> p k f", k=2)
        wkr_sb = sb1("wkr_sb", [128, c.KC * 32], BF16)
        wkr_v = wkr_sb[:].rearrange("p (k f) -> p k f", k=c.KC)
        lnt = [sb1(f"p1ln{i}", [128, c.KC * c.NS], BF16) for i in range(2)]
        lnt_v = [x[:].rearrange("p (k t) -> p k t", k=c.KC) for x in lnt]
        ckv = sb1("ckv", [128, 2 * c.NS], BF16)
        ckv_v = ckv[:].rearrange("p (k t) -> p k t", k=2)
        sqb = sb1("sqb", [128, c.NS], BF16)
        krt = sb1("krt", [32, c.NS], BF16)
        knm = sb1("knm", [c.NHL, 1], F32)

        def p1_wload(sy):
            em.wait("qd", sy, t_wbs)
            em.dma_on("p1w", sy.dma_start(out=wdkv_v, in_=wdkv_bf.rearrange(
                "(k p) (o f) -> p k o f", p=128, f=128)))
            em.dma_on("p1w", sy.dma_start(out=wuk_v, in_=wuk_bf.rearrange("(k p) f -> p k f", p=128)))
            em.dma_on("p1w", sy.dma_start(out=wuv_v, in_=wuv_bf.rearrange("(k p) f -> p k f", p=128)))
            em.dma_on("p1w", sy.dma_start(out=wkr_v, in_=wkr_bf.rearrange("(k p) f -> p k f", p=128)))
        block.sync(p1_wload)
        t_p1w = em.tok("p1w")

        tok_ln1 = [None, None]
        tok_lnuse = [None, None]
        tok_ckv_war = None
        tok_pb2_rd = None
        tok_pb3_rd = None
        tok_pb4_rd = None
        tok_pb5_rd = None
        tok_sq_rd = None
        tok_ckv_rd = None

        p0_st = []
        P1R = c.NS // 128

        def emit_P1_tile(it):
            nonlocal tok_ckv_war, tok_pb2_rd, tok_pb3_rd, tok_pb4_rd, tok_pb5_rd
            nonlocal tok_sq_rd, tok_ckv_rd
            sl = it % 2
            ts0 = it * c.NS

            def b_ld(sy, sl=sl, ts0=ts0, it=it):
                em.wait("qd", sy, tok_lnuse[sl], p0_st[it * P1R + P1R - 1],
                        p0_st[it * P1R + P1R - 2] if P1R >= 2 else None)
                tok_ln1[sl] = em.dma_on(f"ql{sl}", sy.dma_start(
                    out=lnt_v[sl], in_=lnx_sv[:, :, ts0:ts0 + c.NS]))
            block.sync(b_ld)

            def b_ckv_mm(t, sl=sl):
                em.wait("t", t, tok_ln1[sl], t_p1w, tok_ckv_war)
                for o in range(2):
                    for kc in range(c.KC):
                        ins = t.matmul(pb[o][:, 0:c.NS], wdkv_v[:, kc, o, :],
                                       lnt_v[sl][:, kc, :],
                                       start=(kc == 0), stop=(kc == c.KC - 1))
                em.do("t", ins)
            block.tensor(b_ckv_mm)
            t_ckvmm = em.tok("t")

            def b_ckv_ev(v):
                em.pre("v", v, t_ckvmm, tok_ckv_rd)
                em.do("v", v.tensor_copy(ckv_v[:, 0, :], pb[0][:, 0:c.NS]))
                em.pre("v", v)
                em.do("v", v.tensor_copy(ckv_v[:, 1, :], pb[1][:, 0:c.NS]))
            block.vector(b_ckv_ev)
            tok_ckv_war = em.tok("v")
            t_ckv = em.tok("v")

            for oc in range(c.NHL // 2):
                def b_kc_mm(t, oc=oc):
                    em.wait("t", t, t_ckv, tok_pb2_rd)
                    for kc in range(2):
                        ins = t.matmul(pb[2][:, 0:c.NS], wuk_v[:, kc, oc * 128:(oc + 1) * 128],
                                       ckv_v[:, kc, :], start=(kc == 0), stop=(kc == 1))
                    em.do("t", ins)
                block.tensor(b_kc_mm)
                t_kcm = em.tok("t")

                def b_kc_ev(v, oc=oc, ts0=ts0):
                    em.pre("v", v, t_kcm)
                    em.do("v", v.tensor_copy(Kh[2 * oc][0:64, ts0:ts0 + c.NS],
                                             pb[2][0:64, 0:c.NS]))
                    em.pre("v", v)
                    em.do("v", v.tensor_copy(Kh[2 * oc + 1][0:64, ts0:ts0 + c.NS],
                                             pb[2][64:128, 0:c.NS]))
                block.vector(b_kc_ev)
                tok_pb2_rd = em.tok("v")

            def b_kr_mm(t, sl=sl):
                em.wait("t", t, tok_ln1[sl], tok_pb4_rd)
                for kc in range(c.KC):
                    ins = t.matmul(pb[4][0:32, 0:c.NS], wkr_v[:, kc, :], lnt_v[sl][:, kc, :],
                                   start=(kc == 0), stop=(kc == c.KC - 1))
                tok_lnuse[sl] = em.do("t", ins)
            block.tensor(b_kr_mm)
            t_krm = em.tok("t")

            def b_kr_ev(v, ts0=ts0):
                em.pre("v", v, t_krm, t_scp)
                em.do("v", v.tensor_tensor(krt[:], pb[4][0:32, 0:c.NS],
                                           scale_t[:, ts0:ts0 + c.NS], ALU.mult))
                for h in range(c.NHL):
                    em.pre("v", v)
                    em.do("v", v.tensor_copy(Kh[h][64:96, ts0:ts0 + c.NS], krt[:]))
            block.vector(b_kr_ev)
            tok_pb4_rd = em.tok("v")
            t_krope = em.tok("v")

            for h in range(c.NHL):
                def b_sq(s, h=h, ts0=ts0):
                    em.pre("s", s, t_krope, tok_sq_rd)
                    em.do("s", s.activation(sqb[:], Kh[h][:, ts0:ts0 + c.NS], AF.Square))
                block.scalar(b_sq)
                t_sq = em.tok("s")

                def b_kn_mm(t, h=h):
                    em.wait("t", t, t_sq, tok_pb3_rd if h == 0 else None)
                    em.do("t", t.matmul(pb[3][0:c.NHL, 0:c.NS],
                                        hkh[:, h * c.NHL:(h + 1) * c.NHL], sqb[:],
                                        start=(h == 0), stop=(h == c.NHL - 1)))
                block.tensor(b_kn_mm)
                tok_sq_rd = em.tok("t")
            t_knmm = em.tok("t")

            def b_kn_max(v):
                em.pre("v", v, t_knmm, t_vini)
                em.do("v", v.reduce_max(knm[:], pb[3][0:c.NHL, 0:c.NS], axis=AX.X))
                em.pre("v", v)
                em.do("v", v.tensor_tensor(maxk[:], maxk[:], knm[:], ALU.max))
            block.vector(b_kn_max)
            tok_pb3_rd = em.tok("v")

            for sub in range(c.NS // 128):
                tc = it * (c.NS // 128) + sub

                def b_v_mm(t, sub=sub):
                    em.wait("t", t, t_ckv, t_vones, tok_pb5_rd)
                    for kc in range(2):
                        ins = t.matmul(pb[5][:, 0:c.YD],
                                       ckv_v[:, kc, sub * 128:(sub + 1) * 128],
                                       wuv_v[:, kc, :], start=(kc == 0), stop=(kc == 1))
                    em.do("t", ins)
                block.tensor(b_v_mm)
                tok_ckv_rd = em.tok("t")
                t_vm = em.tok("t")

                def b_v_ev(v, tc=tc):
                    em.pre("v", v, t_vm)
                    em.do("v", v.tensor_copy(
                        Vp_v[:, tc, :, 0:64],
                        pb[5][:, 0:c.YD].rearrange("p (h d) -> p h d", d=64)))
                block.vector(b_v_ev)
                tok_pb5_rd = em.tok("v")

        p0_es = ExitStack()
        sb0 = lambda n, sh, dt: p0_es.enter_context(nc.sbuf_tensor(n, sh, dt))
        xb = [sb0(f"p0x{i}", [128, c.DM], F32) for i in range(2)]
        lnb = [sb0(f"p0ln{i}", [128, c.DM], BF16) for i in range(2)]
        fmb = [sb0(f"p0fm{i}", [128, c.DM], BF16) for i in range(2)]
        nst = max(1, c.DM // 512)
        wst = c.DM // nst
        st6 = sb0("p0st6", [128, 6 * nst], F32)
        mv2 = [sb0(f"p0mv{i}", [128, 2], F32) for i in range(2)]
        sdv = [sb0(f"p0sd{i}", [128, 3], F32) for i in range(2)]

        tok_xuse = [None, None]
        tok_x = [None, None]
        tok_lnwar = [None, None]
        tok_fmwar = [None, None]
        tok_ptp_ev = None
        t_sd2 = [None, None]

        def p0_front(i):
            sl = i % 2

            def b_load(sy, i=i, sl=sl):
                em.wait("qd", sy, tok_xuse[sl])
                tok_x[sl] = em.dma_on(f"qx{sl}", sy.dma_start(
                    out=xb[sl][:], in_=x_seq[i * 128:(i + 1) * 128, :]))
            block.sync(b_load)

            def b_stats(v, sl=sl):
                em.pre("v", v, tok_x[sl])
                for j in range(nst):
                    em.pre("v", v)
                    em.do("v", v.bn_stats(st6[:, 6 * j:6 * j + 6],
                                          xb[sl][:, j * wst:(j + 1) * wst]))
                em.pre("v", v)
                em.do("v", v.bn_aggr(mv2[sl][:], st6[:]))
                em.pre("v", v)
                em.do("v", v.tensor_scalar(sdv[sl][:, 0:1], mv2[sl][:, 1:2], EPS,
                                           None, ALU.add))
            block.vector(b_stats)
            t_veps = em.tok("v")

            def b_sd(s, sl=sl):
                em.pre("s", s, t_veps)
                em.do("s", s.activation(sdv[sl][:, 0:1], sdv[sl][:, 0:1], AF.Sqrt))
            block.scalar(b_sd)
            t_sd2[sl] = em.tok("s")

        def p0_back(i):
            nonlocal tok_ptp_ev
            sl = i % 2

            def b_rstd(v, sl=sl):
                em.pre("v", v, t_sd2[sl])
                em.do("v", v.reciprocal(sdv[sl][:, 1:2], sdv[sl][:, 0:1]))
                em.pre("v", v)
                em.do("v", v.tensor_scalar(sdv[sl][:, 2:3], mv2[sl][:, 0:1],
                                           sdv[sl][:, 1:2], -1.0, ALU.mult, ALU.mult))
            block.vector(b_rstd)
            t_rstd = em.tok("v")

            def b_apply(v, sl=sl):
                em.pre("v", v, t_rstd, tok_lnwar[sl])
                tok_xuse[sl] = em.do("v", v.tensor_scalar(
                    lnb[sl][:], xb[sl][:], sdv[sl][:, 1:2], sdv[sl][:, 2:3],
                    ALU.mult, ALU.add))
            block.vector(b_apply)
            t_ln = em.tok("v")

            def b_tp(t, sl=sl):
                em.wait("t", t, t_ln, t_vini, tok_ptp_ev)
                for cc_ in range(c.KC):
                    ins = t.transpose(ptp[:, cc_ * 128:(cc_ + 1) * 128],
                                      lnb[sl][:, cc_ * 128:(cc_ + 1) * 128], ident[:])
                tok_lnwar[sl] = em.do("t", ins)
            block.tensor(b_tp)
            t_tp = em.tok("t")

            def b_evac(v, sl=sl):
                em.pre("v", v, t_tp, tok_fmwar[sl])
                em.do("v", v.tensor_copy(fmb[sl][:], ptp[:, 0:c.DM]))
            block.vector(b_evac)
            tok_ptp_ev = em.tok("v")
            t_ev = em.tok("v")

            def b_store(sy, i=i, sl=sl):
                em.wait("qd", sy, t_ev)
                tok_fmwar[sl] = em.dma_on(f"qf{sl}", sy.dma_start(
                    out=lnx_sv[:, :, i * 128:(i + 1) * 128],
                    in_=fmb[sl][:].rearrange("p (c t) -> p c t", c=c.KC)))
            block.sync(b_store)
            p0_st.append(tok_fmwar[sl])
            emit_p3p(1, cap=p3p_fin)
            if (i + 1) % P1R == 0:
                emit_P1_tile((i + 1) // P1R - 1)

        for i in range(c.T // 128):
            p0_front(i)
            if i > 0:
                p0_back(i - 1)
        p0_back(c.T // 128 - 1)
        emit_p3p_flush()

        barrier(em, block)
        p0_es.close()
        p1_es.close()
        wn2_es.close()
        scale_es.close()



        p2a_es = ExitStack()
        sb2a = lambda n, sh, dt: p2a_es.enter_context(nc.sbuf_tensor(n, sh, dt))
        lnxf = sb2a("lnxf", [128, c.KC * c.T], BF16)
        lnxf_v = lnxf[:].rearrange("p (k t) -> p k t", k=c.KC)
        wdq_b = [sb2a(f"wdqb{i}", [128, c.KC * 128], BF16) for i in range(2)]
        wdq_bv = [x[:].rearrange("p (k f) -> p k f", k=c.KC) for x in wdq_b]
        cqt = [sb2a(f"cqt{i}", [128, c.NS], BF16) for i in range(2)]
        wnat3.append(sb2a("wnat3y", [128, QW], F32))

        def p2a_lnx(sy):
            em.dma("qd", sy.dma_start(out=lnxf_v, in_=lnx_sv))
        block.sync(p2a_lnx)
        t_lnxf = em.tok("qd")

        tok_wdq = [None, None]
        tok_wdq_use = [None, None]
        tok_cqt_st = [None] * 4
        tok_pbsc_rd = [None] * 4
        n2 = c.T // c.NS
        for oc in range(c.CQC):
            sl = oc % 2

            def b_w(sy, oc=oc, sl=sl):
                em.wait("qd", sy, tok_wdq_use[sl], t_wbl)
                tok_wdq[sl] = em.dma_on(f"gw{sl}", sy.dma_start(
                    out=wdq_bv[sl],
                    in_=wdq_bf[:, oc * 128:(oc + 1) * 128].rearrange("(k p) f -> p k f", p=128)))
            block.sync(b_w)

            for ns in range(n2):
                psl = ns % 4

                def b_mm(t, oc=oc, sl=sl, ns=ns, psl=psl):
                    em.wait("t", t, tok_wdq[sl], t_lnxf, tok_pbsc_rd[psl])
                    for kc in range(c.KC):
                        ins = t.matmul(pb[psl][:, 0:c.NS], wdq_bv[sl][:, kc, :],
                                       lnxf_v[:, kc, ns * c.NS:(ns + 1) * c.NS],
                                       start=(kc == 0), stop=(kc == c.KC - 1))
                    tk = em.do("t", ins)
                    if ns == n2 - 1:
                        tok_wdq_use[sl] = tk
                block.tensor(b_mm)
                t_mm = em.tok("t")

                def b_ev(v, psl=psl, csl=ns % 2):
                    em.pre("v", v, t_mm, tok_cqt_st[csl])
                    em.do("v", v.tensor_copy(cqt[csl][:], pb[psl][:, 0:c.NS]))
                block.vector(b_ev)
                tok_pbsc_rd[psl] = em.tok("v")
                t_e = em.tok("v")

                def b_st(sy, oc=oc, ns=ns, csl=ns % 2):
                    em.wait("qd", sy, t_e)
                    tok_cqt_st[csl] = em.dma_on(f"qc{csl}", sy.dma_start(
                        out=cq_sv[:, oc, ns * c.NS:(ns + 1) * c.NS], in_=cqt[csl][:]))
                block.sync(b_st)
            emit_p3p(3)
        emit_p3p(len(p3p_units))
        barrier(em, block)
        p2a_es.close()
        quant_es.close()

        p2b_es = ExitStack()
        sb2b = lambda n, sh, dt: p2b_es.enter_context(nc.sbuf_tensor(n, sh, dt))
        wuq_sb = sb2b("wuq_sb", [128, c.CQC * c.YD], BF16)
        wuq_v = wuq_sb[:].rearrange("p (k f) -> p k f", k=c.CQC)
        wqr_sb = sb2b("wqr_sb", [128, c.CQC * c.NHL * 32], BF16)
        wqr_v = wqr_sb[:].rearrange("p (k f) -> p k f", k=c.CQC)
        cq_t = sb2b("cq_t", [128, c.CQC * c.NS], BF16)
        cq_tv = cq_t[:].rearrange("p (k t) -> p k t", k=c.CQC)
        qoutb = [sb2b(f"qout{i}", [128, c.NS], BF16) for i in range(2)]
        scp4 = sb2b("scp4", [128, c.T], BF16)

        def p2b_dup(g):
            em.dma_on("scp", g.dma_start(out=scp4[0:32, :], in_=scp_c[:]))
            for b0 in (32, 64, 96):
                em.dma_on("scp", g.dma_start(out=scp4[b0:b0 + 32, :], in_=scp_c[:]))
        block.gpsimd(p2b_dup)

        def p2b_w(sy):
            em.wait("qd", sy, t_wbl)
            em.dma_on("p2bw", sy.dma_start(out=wuq_v, in_=wuq_bf.rearrange("(k p) f -> p k f", p=128)))
            em.dma_on("p2bw", sy.dma_start(out=wqr_v, in_=wqr_bf.rearrange("(k p) f -> p k f", p=128)))
        block.sync(p2b_w)
        t_p2bw = em.tok("p2bw")

        tok_cqld = None
        tok_qst = [None, None]
        tok_pbq_rd = [None, None]
        nqc = c.NHL // 2
        for ns in range(n2):
            def b_cq(sy, ns=ns):
                em.wait("qd", sy, tok_cqld)
                em.dma_on("ql2", sy.dma_start(out=cq_tv, in_=cq_sv[:, :, ns * c.NS:(ns + 1) * c.NS]))
            block.sync(b_cq)
            t_cq = em.tok("ql2")

            for j in range(nqc + c.QRC):
                psl = j % 2
                is_qr = j >= nqc
                oc = j - nqc if is_qr else j

                def b_mm(t, j=j, oc=oc, is_qr=is_qr, psl=psl):
                    em.wait("t", t, t_cq, t_p2bw, tok_pbq_rd[psl])
                    w_v = wqr_v if is_qr else wuq_v
                    for kc in range(c.CQC):
                        ins = t.matmul(pb[psl][:, 0:c.NS], w_v[:, kc, oc * 128:(oc + 1) * 128],
                                       cq_tv[:, kc, :], start=(kc == 0), stop=(kc == c.CQC - 1))
                    tk = em.do("t", ins)
                    if j == nqc + c.QRC - 1:
                        nonlocal tok_cqld
                        tok_cqld = tk
                block.tensor(b_mm)
                t_mm = em.tok("t")

                if is_qr:
                    def b_ev(v, psl=psl, ns=ns):
                        em.pre("v", v, t_mm, tok_qst[psl], em.tok("scp"))
                        em.do("v", v.tensor_tensor(qoutb[psl][:], pb[psl][:, 0:c.NS],
                                                   scp4[:, ns * c.NS:(ns + 1) * c.NS], ALU.mult))
                    block.vector(b_ev)
                    tok_pbq_rd[psl] = em.tok("v")
                    t_e = em.tok("v")
                else:
                    def b_ev(s, psl=psl):
                        em.pre("s", s, t_mm, tok_qst[psl])
                        em.do("s", s.activation(qoutb[psl][:], pb[psl][:, 0:c.NS],
                                                AF.Identity, scale=RS96))
                    block.scalar(b_ev)
                    tok_pbq_rd[psl] = em.tok("s")
                    t_e = em.tok("s")

                def b_st(sy, oc=oc, is_qr=is_qr, psl=psl, ns=ns):
                    em.wait("qd", sy, t_e)
                    tk = None
                    if is_qr:
                        for j in range(4):
                            tk = em.dma_on(f"qs{psl}", sy.dma_start(
                                out=qs_s[(4 * oc + j) * 128 + 64:(4 * oc + j) * 128 + 96,
                                         ns * c.NS:(ns + 1) * c.NS],
                                in_=qoutb[psl][j * 32:(j + 1) * 32, :]))
                    else:
                        for j in range(2):
                            tk = em.dma_on(f"qs{psl}", sy.dma_start(
                                out=qs_s[(2 * oc + j) * 128:(2 * oc + j) * 128 + 64,
                                         ns * c.NS:(ns + 1) * c.NS],
                                in_=qoutb[psl][j * 64:(j + 1) * 64, :]))
                    tok_qst[psl] = tk
                block.sync(b_st)
        barrier(em, block)
        p2b_es.close()

        p2c_es = ExitStack()
        sb2c = lambda n, sh, dt: p2c_es.enter_context(nc.sbuf_tensor(n, sh, dt))
        qsl = sb2c("qsl", [128, c.NHL * c.SLOT], BF16)
        qh_v = qsl[:].rearrange("p (o t) -> p o t", o=c.NHL)
        sq2 = sb2c("sq2", [128, c.SLOT], BF16)
        qn = sb2c("qn", [c.NHL, c.SLOT], F32)
        qmx = sb2c("qmx", [c.NHL, 1], F32)
        ngc = sb2c("ngc", [c.NHL, 1], F32)
        ngb = sb2c("ngb", [c.NHL, 1], BF16)
        bias_sb = sb2c("bias_sb", [128, c.NHL], F32)
        expA = sb2c("expA", [128, c.NKT * c.SLOT], BF16)
        expA_v = expA[:].rearrange("p (k t) -> p k t", k=c.NKT)
        yun_sb = [sb2c(f"yun{i}", [128, c.SLOT], BF16) for i in range(c.YC)]
        den_f = sb2c("den_f", [1, c.NHL * c.SLOT], F32)
        y_sb = [sb2c(f"ysb{i}", [128, c.SLOT], BF16) for i in range(c.YC)]
        wop_sb = sb2c("wop_sb", [128, c.YC * c.DM], BF16)
        wop_v = wop_sb[:].rearrange("p (k f) -> p k f", k=c.YC)
        aop_t = [sb2c(f"aopt{i}", [128, c.SLOT], F32) for i in range(2)]
        aop_sv = aop_s.rearrange("j (o p) t -> p j o t", p=128)

        def p2c_wo(sy):
            em.wait("qd", sy, t_wbl)
            em.dma_on("wop", sy.dma_start(out=wop_v, in_=wo_bf.rearrange("(k p) f -> p k f", p=128)))
        block.sync(p2c_wo)
        t_wop = em.tok("wop")

        def p2c_zero(v):
            em.pre("v", v)
            em.do("v", v.memset(qsl[:], 0.0))
        block.vector(p2c_zero)
        t_qrz = em.tok("v")

        tok_qsl_use = None
        tok_avprev = None
        tok_yun_rd = None
        pb01_rd = [None, None]
        tok_py_rd = [None, None]
        tok_pb3q_rd = None
        tok_pb4e_rd = None
        tok_ngb_rd = None
        tok_ywr = None
        tok_sq2_rd = None
        tok_aop_st = [None, None]
        for slot in range(c.NSLOT):
            q0 = slot * c.SLOT
            nkt = (q0 + c.SLOT) // 128

            def b_qld(sy, q0=q0):
                em.wait("qd", sy, tok_qsl_use, t_qrz)
                em.dma_on("qq", sy.dma_start(out=qh_v[0:96, :, :],
                                             in_=qs_sv[0:96, :, q0:q0 + c.SLOT]))
            block.sync(b_qld)
            t_qc_ld = em.tok("qq")
            t_qr_ld = t_qc_ld

            for j in range(c.NHL):
                def b_sqq(v, j=j):
                    em.pre("v", v, t_qc_ld, tok_sq2_rd)
                    em.do("v", v.tensor_tensor(sq2[:], qh_v[:, j, :], qh_v[:, j, :], ALU.mult))
                block.vector(b_sqq)
                t_s1 = em.tok("v")

                def b_nm(t, j=j):
                    em.wait("t", t, t_s1, tok_pb3q_rd if j == 0 else None)
                    em.do("t", t.matmul(pb[3][0:c.NHL, 0:c.SLOT],
                                        hkh[:, j * c.NHL:(j + 1) * c.NHL], sq2[:],
                                        start=(j == 0), stop=(j == c.NHL - 1)))
                block.tensor(b_nm)
                tok_sq2_rd = em.tok("t")
            t_qn_mm = em.tok("t")

            def b_ngc(v):
                em.pre("v", v, t_qn_mm, tok_ngb_rd)
                em.do("v", v.reduce_max(qmx[:], pb[3][0:c.NHL, 0:c.SLOT], axis=AX.X))
                em.pre("v", v)
                em.do("v", v.tensor_scalar(ngc[:], qmx[:], maxk[:], None, ALU.mult))
            block.vector(b_ngc)
            tok_pb3q_rd = em.tok("v")
            t_c2 = em.tok("v")

            def b_csq(s):
                em.pre("s", s, t_c2)
                em.do("s", s.activation(ngc[:], ngc[:], AF.Sqrt))
            block.scalar(b_csq)
            t_cs = em.tok("s")

            def b_ng2(v):
                em.pre("v", v, t_cs)
                em.do("v", v.tensor_scalar(ngc[:], ngc[:], -1.0, None, ALU.mult))
                em.pre("v", v)
                em.do("v", v.tensor_copy(ngb[:], ngc[:]))
            block.vector(b_ng2)
            t_ngb = em.tok("v")

            def b_ext(t):
                em.wait("t", t, t_ngb, tok_pb4e_rd)
                for h in range(c.NHL):
                    ins = t.matmul(pb[4][:, h:h + 1], ehot_v[:, h, :], ngb[:],
                                   start=True, stop=True)
                em.do("t", ins)
            block.tensor(b_ext)
            tok_ngb_rd = em.tok("t")
            t_ext = em.tok("t")

            def b_bias(v):
                em.pre("v", v, t_ext)
                em.do("v", v.tensor_copy(bias_sb[:, 0:c.NHL], pb[4][:, 0:c.NHL]))
            block.vector(b_bias)
            tok_pb4e_rd = em.tok("v")
            t_bias = em.tok("v")

            offs = [max(0, kt * 128 - q0) for kt in range(nkt)]
            for h in range(c.NHL):
                pby = pb[5] if h % 2 == 0 else pb[2]
                ybk = h % 2
                sbase = em.cnt["s"]

                def b_sc_all(t, h=h, nkt=nkt, sbase=sbase):
                    em.wait("t", t, t_qc_ld, t_bias, ("s", sbase),
                            pb01_rd[0], pb01_rd[1], tok_pb4e_rd)
                    for kt in range(nkt):
                        off = offs[kt]
                        psl = (0, 1, 4)[kt % 3]
                        if kt >= 3:
                            em.wait("t", t, ("s", sbase + kt - 2))
                        ins = t.matmul(pb[psl][:, off:c.SLOT],
                                       Kh[h][0:96, kt * 128:(kt + 1) * 128],
                                       qh_v[0:96, h, off:c.SLOT], start=True, stop=True)
                        em.do("t", ins)
                block.tensor(b_sc_all)
                tbase = em.cnt["t"] - nkt

                def b_exp_all(s, h=h, nkt=nkt, tbase=tbase):
                    em.pre("s", s, tok_avprev)
                    for kt in range(nkt):
                        off = offs[kt]
                        em.wait("s", s, ("t", tbase + kt + 1))
                        em.do("s", s.activation(expA_v[:, kt, off:c.SLOT],
                                                pb[(0, 1, 4)[kt % 3]][:, off:c.SLOT],
                                                AF.Exp, bias=bias_sb[:, h:h + 1],
                                                scale=1.0))
                block.scalar(b_exp_all)
                assert em.cnt["s"] == sbase + nkt

                ndiag = c.SLOT // 128
                vbase = em.cnt["v"]

                def b_msk_all(v, nkt=nkt, sbase=sbase, ndiag=ndiag):
                    for i_, kt in enumerate(range(nkt - ndiag, nkt)):
                        off = offs[kt]
                        em.pre("v", v, ("s", sbase + kt + 1))
                        em.do("v", v.tensor_tensor(expA_v[:, kt, off:off + 128],
                                                   expA_v[:, kt, off:off + 128],
                                                   tri[:], ALU.mult))
                block.vector(b_msk_all)
                assert em.cnt["v"] == vbase + ndiag

                def b_av_all(t, h=h, nkt=nkt, sbase=sbase, vbase=vbase, ndiag=ndiag,
                             pby=pby, ybk=ybk):
                    em.wait("t", t, tok_py_rd[ybk])
                    for kt in range(nkt):
                        off = offs[kt]
                        if kt >= nkt - ndiag:
                            em.wait("t", t, ("v", vbase + ndiag - (nkt - kt) + 1))
                        else:
                            em.wait("t", t, ("s", sbase + kt + 1))
                        ins = t.matmul(pby[0:65, off:c.SLOT], Vp_v[:, kt, h, :],
                                       expA_v[:, kt, off:c.SLOT],
                                       start=(kt == 0), stop=(kt == nkt - 1))
                    em.do("t", ins)
                block.tensor(b_av_all)
                tok_avprev = em.tok("t")
                t_y = em.tok("t")
                if h == c.NHL - 1:
                    tok_qsl_use = t_y

                def b_yev(v, h=h, ch=h // 2, b0=(h % 2) * 64, pby=pby, ybk=ybk):
                    em.pre("v", v, t_y, tok_yun_rd)
                    em.do("v", v.tensor_copy(yun_sb[ch][b0:b0 + 64, 0:c.SLOT],
                                             pby[0:64, 0:c.SLOT]))
                    em.pre("v", v)
                    em.do("v", v.tensor_copy(den_f[0:1, h * c.SLOT:(h + 1) * c.SLOT],
                                             pby[64:65, 0:c.SLOT]))
                block.vector(b_yev)
                tok_py_rd[ybk] = em.tok("v")
            t_yun = em.tok("v")

            t_rcp = t_yun

            def b_bc_all(t):
                em.wait("t", t, t_rcp, pb01_rd[0], pb01_rd[1], tok_pb3q_rd, tok_pb4e_rd)
                for h in range(c.NHL):
                    bank = (pb[0], pb[1], pb[3], pb[4])[h // 2]
                    ins = t.matmul(bank[(h % 2) * 64:(h % 2) * 64 + 64, 0:c.SLOT],
                                   ones_f[0:1, 0:64],
                                   den_f[0:1, h * c.SLOT:(h + 1) * c.SLOT],
                                   start=True, stop=True)
                em.do("t", ins)
            block.tensor(b_bc_all)
            t_bc2 = em.tok("t")

            def b_recipb(v):
                em.pre("v", v, t_bc2)
                for bi in range(c.NHL // 2):
                    bank = (pb[0], pb[1], pb[3], pb[4])[bi]
                    rows = 128
                    em.pre("v", v)
                    em.do("v", v.reciprocal(bank[0:rows, 0:c.SLOT],
                                            bank[0:rows, 0:c.SLOT]))
            block.vector(b_recipb)
            t_bcr = em.tok("v")

            def b_norm(v):
                em.pre("v", v, t_bcr, tok_ywr)
                for h in range(c.NHL):
                    bank = (pb[0], pb[1], pb[3], pb[4])[h // 2]
                    em.pre("v", v)
                    em.do("v", v.tensor_tensor(
                        y_sb[h // 2][(h % 2) * 64:(h % 2) * 64 + 64, 0:c.SLOT],
                        yun_sb[h // 2][(h % 2) * 64:(h % 2) * 64 + 64, 0:c.SLOT],
                        bank[(h % 2) * 64:(h % 2) * 64 + 64, 0:c.SLOT], ALU.mult))
            block.vector(b_norm)
            tok_yun_rd = em.tok("v")
            tok_pb3q_rd = em.tok("v")
            tok_pb4e_rd = em.tok("v")
            pb01_rd[0] = em.tok("v")
            pb01_rd[1] = em.tok("v")
            t_yall = em.tok("v")

            for oc in range(c.KC):
                psl2 = oc % 2

                def b_wo_mm(t, oc=oc, psl2=psl2):
                    em.wait("t", t, t_yall, t_wop, pb01_rd[psl2])
                    for kc in range(c.YC):
                        ins = t.matmul(pb[psl2][:, 0:c.SLOT],
                                       wop_v[:, kc, oc * 128:(oc + 1) * 128],
                                       y_sb[kc][:, 0:c.SLOT],
                                       start=(kc == 0), stop=(kc == c.YC - 1))
                    tk = em.do("t", ins)
                    if oc == c.KC - 1:
                        nonlocal tok_ywr
                        tok_ywr = tk
                block.tensor(b_wo_mm)
                t_wm = em.tok("t")

                def b_wo_ev(v, oc=oc, psl2=psl2):
                    em.pre("v", v, t_wm, tok_aop_st[psl2])
                    em.do("v", v.tensor_copy(aop_t[psl2][:], pb[psl2][:, 0:c.SLOT]))
                block.vector(b_wo_ev)
                pb01_rd[psl2] = em.tok("v")
                t_ae = em.tok("v")

                def b_wo_st(sy, q0=q0, oc=oc, psl2=psl2):
                    em.wait("qd", sy, t_ae)
                    th = q0 // c.OWN
                    c0 = q0 % c.OWN
                    tok_aop_st[psl2] = em.dma_on(f"qa{psl2}", sy.dma_start(
                        out=aop_sv[:, th, oc, c0:c0 + c.SLOT], in_=aop_t[psl2][:]))
                block.sync(b_wo_st)

        barrier(em, block)

        def b_cc(g):
            em.do("cc", g.collective_compute(
                "ReduceScatter", mybir.AluOpType.add, groups, [aop_s.opt()], [aog_s.opt()]), 1)
        block.gpsimd(b_cc)

        barrier(em, block)
        p2c_es.close()
        att_es.close()


        p3_es = ExitStack()
        sb3 = lambda n, sh, dt, **kw: p3_es.enter_context(nc.sbuf_tensor(n, sh, dt, **kw))
        SUBS = c.MEGA // 128
        KC2 = c.KC // 2
        RW = KC2 * 128
        NW = c.INNER // c.NS
        NON = c.DM // c.NS
        assert NW >= 3 * SUBS + 1 and RW <= 512 and c.DM == 2 * RW
        aofm = sb3("aofm", [128, KC2 * 128], F32)
        aofm_v = aofm[:].rearrange("p (k t) -> p k t", k=KC2)
        xt = [sb3(f"xt{i}", [128, c.DM], F32) for i in range(2)]
        res1b = [sb3(f"res1{i}", [128, SUBS * c.DM], F32) for i in range(2)]
        res1v = [x[:].rearrange("p (u f) -> p u f", u=SUBS) for x in res1b]
        ln2x = sb3("ln2x", [128, c.DM], F32)
        qi_b = sb3("qi_b", [128, c.DM], I32)
        qtm_b = sb3("qtm_b", [128, c.DM], BF16)
        zf_b = sb3("zf_b", [128, c.DM], BF16)
        xq_fm = [sb3(f"xq_fm{i}", [128, c.KC * c.MEGA], BF16) for i in range(2)]
        xq_fv = [x[:].rearrange("p (k t) -> p k t", k=c.KC) for x in xq_fm]
        alg = sb3("alg", [128, 2 * SUBS], F32)
        alu_ = sb3("alu", [128, 2 * SUBS], F32)
        gam = sb3("gam", [128, SUBS], F32)
        tmp1 = sb3("tmp1", [128, 4], F32)
        st6b = sb3("st6b", [128, 6 * nst], F32)
        mv2b = sb3("mv2b", [128, 2], F32)
        sdvb = sb3("sdvb", [128, 3], F32)
        gsl = [sb3(f"gsl{i}", [128, c.NS], F32) for i in range(2)]
        wgsl = [sb3(f"wgsl{i}", [128, c.KC * c.NS], BF16) for i in range(2)]
        wgsl_v = [x[:].rearrange("p (k t) -> p k t", k=c.KC) for x in wgsl]
        wusl = [sb3(f"wusl{i}", [128, c.KC * c.NS], BF16) for i in range(2)]
        wusl_v = [x[:].rearrange("p (k t) -> p k t", k=c.KC) for x in wusl]
        hq_fm = sb3("hq_fm", [128, c.IC * c.MEGA], BF16)
        hq_fv = hq_fm[:].rearrange("p (k t) -> p k t", k=c.IC)
        wdsl = [sb3(f"wdsl{i}", [128, c.IC * c.NS], BF16) for i in range(2)]
        wdsl_v = [x[:].rearrange("p (k t) -> p k t", k=c.IC) for x in wdsl]
        hh = sb3("hh", [128, SUBS * c.INNER], F32, side="right")
        hh_v = hh[:].rearrange("p (u f) -> p u f", u=SUBS)

        wgt_sv = wgt_s.rearrange("(k p) t -> p k t", p=128)
        wut_sv = wut_s.rearrange("(k p) t -> p k t", p=128)
        wdt_sv = wdt_s.rearrange("(k p) t -> p k t", p=128)
        aog_sv = aog_s.rearrange("(o p) t -> p o t", p=128)

        st = {
            "yg": None, "yg_use": None,
            "xt": [None, None], "xt_use": [None, None],
            "res1_rd": [None, None],
            "ptp2_rd": None, "ptp_rd": None,
            "xqi_rd": None, "zfb_rd": None, "qtm_rd": None,
            "xq_rd": [None, None], "xq_rdy": [None, None],
            "a_qtm": None, "l2b": None,
            "hq_rdy": [None] * SUBS, "hqfm_rd": None,
            "wgu_tok": [None, None], "wgu_use": [None, None], "wg_next": 0,
            "wd_tok": [None, None], "wd_use": [None, None], "wd_next": 0,
            "wd_last": None,
            "hh_rd": None, "hh_done": None,
            "gsl_rd": [None, None], "outb": [None, None],
        }
        pbrd = {i: None for i in range(6)}

        def emit_A_load(mg, sub):
            t0 = mg * c.MEGA + sub * 128
            sl = sub % 2

            def b_ao(sy, t0=t0):
                em.wait("qd", sy, st["yg_use"])
                st["yg"] = em.dma_on("qao", sy.dma_start(
                    out=aofm_v, in_=aog_sv[:, 0:KC2, t0:t0 + 128]))
            block.sync(b_ao)

            def b_xl(sy, t0=t0, sl=sl):
                em.wait("qd", sy, st["xt_use"][sl])
                st["xt"][sl] = em.dma_on("qxo", sy.dma_start(
                    out=xt[sl][:], in_=x_own[t0:t0 + 128, :]))
            block.sync(b_xl)

        def emit_A_main1(mg, sub):
            par = mg % 2
            sl = sub % 2
            for r in range(2):
                if r == 1:
                    def b_ao1(sy, mg=mg, sub=sub):
                        em.wait("qd", sy, st["yg_use"])
                        t0 = mg * c.MEGA + sub * 128
                        st["yg"] = em.dma_on("qao", sy.dma_start(
                            out=aofm_v, in_=aog_sv[:, KC2:c.KC, t0:t0 + 128]))
                    block.sync(b_ao1)

                def b_aotp(t):
                    em.wait("t", t, st["yg"], pbrd[4])
                    for oc in range(KC2):
                        ins = t.transpose(pb[4][:, oc * 128:(oc + 1) * 128],
                                          aofm_v[:, oc, :], ident_f[:])
                    st["yg_use"] = em.do("t", ins)
                block.tensor(b_aotp)
                t_aotp = em.tok("t")

                def b_res1(v, r=r, sub=sub, sl=sl, par=par):
                    em.pre("v", v, t_aotp, st["xt"][sl], st["res1_rd"][par])
                    em.do("v", v.tensor_tensor(
                        res1v[par][:, sub, r * RW:(r + 1) * RW],
                        pb[4][:, 0:RW], xt[sl][:, r * RW:(r + 1) * RW], ALU.add))
                block.vector(b_res1)
                pbrd[4] = em.tok("v")
            st["xt_use"][sl] = em.tok("v")
            t_r1 = em.tok("v")

            def b_ln2a(v, sub=sub, par=par):
                em.pre("v", v, t_r1)
                for j in range(nst):
                    em.pre("v", v)
                    em.do("v", v.bn_stats(st6b[:, 6 * j:6 * j + 6],
                                          res1v[par][:, sub, j * wst:(j + 1) * wst]))
                em.pre("v", v)
                em.do("v", v.bn_aggr(mv2b[:], st6b[:]))
                em.pre("v", v)
                em.do("v", v.tensor_scalar(sdvb[:, 0:1], mv2b[:, 1:2], EPS, None, ALU.add))
            block.vector(b_ln2a)
            t_l2a = em.tok("v")

            def b_ln2b(s):
                em.pre("s", s, t_l2a)
                em.do("s", s.activation(sdvb[:, 0:1], sdvb[:, 0:1], AF.Sqrt))
            block.scalar(b_ln2b)
            st["l2b"] = em.tok("s")

        def emit_A_main2(mg, sub):
            par = mg % 2
            t_l2b = st["l2b"]
            cidx = par * SUBS + sub

            def b_aq(v, sub=sub, par=par, cidx=cidx):
                em.pre("v", v, t_l2b)
                em.do("v", v.reciprocal(sdvb[:, 1:2], sdvb[:, 0:1]))
                em.pre("v", v)
                em.do("v", v.tensor_scalar(sdvb[:, 2:3], mv2b[:, 0:1], sdvb[:, 1:2],
                                           -1.0, ALU.mult, ALU.mult))
                em.pre("v", v)
                em.do("v", v.tensor_scalar(ln2x[:], res1v[par][:, sub, :],
                                           sdvb[:, 1:2], sdvb[:, 2:3], ALU.mult, ALU.add))
                em.pre("v", v)
                em.do("v", v.reduce_max(tmp1[:, 0:1], ln2x[:], axis=AX.X,
                                        apply_absolute_value=True))
                em.pre("v", v)
                em.do("v", v.tensor_scalar(tmp1[:, 1:2], tmp1[:, 0:1], 1e-5, None, ALU.max))
                em.pre("v", v)
                em.do("v", v.reciprocal(tmp1[:, 2:3], tmp1[:, 1:2]))
                em.pre("v", v)
                em.do("v", v.tensor_scalar(tmp1[:, 3:4], tmp1[:, 2:3], 127.0, None, ALU.mult))
                em.pre("v", v)
                em.do("v", v.tensor_scalar(alg[:, cidx:cidx + 1], tmp1[:, 1:2],
                                           mv_m[:, 0:1], 1.0 / 127.0, ALU.mult, ALU.mult))
                em.pre("v", v)
                em.do("v", v.tensor_scalar(alu_[:, cidx:cidx + 1], tmp1[:, 1:2],
                                           mv_m[:, 1:2], 1.0 / 127.0, ALU.mult, ALU.mult))
                em.pre("v", v, st["xqi_rd"])
                em.do("v", v.tensor_scalar(qi_b[:], ln2x[:], tmp1[:, 3:4], None, ALU.mult))
                em.pre("v", v, st["zfb_rd"])
                em.do("v", v.tensor_copy(zf_b[:], qi_b[:]))
                em.pre("v", v)
                em.do("v", v.scalar_tensor_tensor(ln2x[:], ln2x[:], tmp1[:, 3:4],
                                                  zf_b[:], ALU.mult, ALU.subtract))
                em.pre("v", v, st["qtm_rd"])
                em.do("v", v.tensor_scalar(qtm_b[:], ln2x[:], 0.5, None, ALU.is_gt))
                em.pre("v", v)
                em.do("v", v.tensor_scalar(ln2x[:], ln2x[:], -0.5, None, ALU.is_lt))
                em.pre("v", v)
                em.do("v", v.tensor_tensor(zf_b[:], zf_b[:], qtm_b[:], ALU.add))
                em.pre("v", v)
                em.do("v", v.tensor_tensor(qtm_b[:], zf_b[:], ln2x[:], ALU.subtract))
            block.vector(b_aq)
            st["xqi_rd"] = em.tok("v")
            st["zfb_rd"] = em.tok("v")
            st["a_qtm"] = em.tok("v")

        def emit_A_tp(mg, sub):
            par = mg % 2

            def b_xqtp(t):
                em.wait("t", t, st["a_qtm"], st["ptp2_rd"])
                for oc in range(c.KC):
                    ins = t.transpose(ptp2[:, oc * 128:(oc + 1) * 128],
                                      qtm_b[:, oc * 128:(oc + 1) * 128], ident[:])
                em.do("t", ins)
            block.tensor(b_xqtp)
            st["qtm_rd"] = em.tok("t")
            t_xtp = em.tok("t")

            def b_xqev(v, sub=sub, par=par):
                em.pre("v", v, t_xtp, st["xq_rd"][par])
                em.do("v", v.tensor_copy(
                    xq_fv[par][:, :, sub * 128:(sub + 1) * 128],
                    ptp2[:, 0:c.DM].rearrange("p (k t) -> p k t", k=c.KC)))
            block.vector(b_xqev)
            st["ptp2_rd"] = em.tok("v")

        def emit_wgu_load():
            g = st["wg_next"]
            if g >= c.NMEGA * NW:
                return
            st["wg_next"] = g + 1
            isl = g % NW
            sl = g % 2

            def b_wgu(sy, isl=isl, sl=sl):
                em.wait("qd", sy, st["wgu_use"][sl])
                tg = em.dma_on(f"qg{sl}", sy.dma_start(
                    out=wgsl_v[sl], in_=wgt_sv[:, :, isl * c.NS:(isl + 1) * c.NS]))
                tu = em.dma_on(f"qg2{sl}", sy.dma_start(
                    out=wusl_v[sl], in_=wut_sv[:, :, isl * c.NS:(isl + 1) * c.NS]))
                st["wgu_tok"][sl] = (tg, tu)
            block.sync(b_wgu)

        def emit_wd_load():
            g = st["wd_next"]
            if g >= c.NMEGA * NON:
                return
            st["wd_next"] = g + 1
            on = g % NON
            sl = g % 2

            def b_wd(sy, on=on, sl=sl):
                em.wait("qd", sy, st["wd_use"][sl])
                st["wd_tok"][sl] = em.dma_on(f"qwd{sl}", sy.dma_start(
                    out=wdsl_v[sl], in_=wdt_sv[:, :, on * c.NS:(on + 1) * c.NS]))
            block.sync(b_wd)

        def emit_BC_isl(mg, isl):
            par = mg % 2
            wsl = (mg * NW + isl) % 2

            sfut = em.cnt["s"]
            vfut0 = em.cnt["v"]
            tmm = []
            for sub in range(SUBS):
                gb = 2 * (sub % 2)
                ub = gb + 1

                def b_gu_mm(t, sub=sub, gb=gb, ub=ub, par=par, wsl=wsl):
                    em.wait("t", t, st["wgu_tok"][wsl][0], st["wgu_tok"][wsl][1],
                            st["xq_rdy"][par])
                    if sub < 2:
                        em.wait("t", t, pbrd[gb], pbrd[ub])
                    else:
                        em.wait("t", t, ("s", sfut + (sub - 2) + 1),
                                ("v", vfut0 + (sub - 2) + 1))
                    for kc in range(c.KC):
                        t.matmul(pb[gb][:, 0:c.NS],
                                 xq_fv[par][:, kc, sub * 128:(sub + 1) * 128],
                                 wgsl_v[wsl][:, kc, :], start=(kc == 0),
                                 stop=(kc == c.KC - 1))
                    for kc in range(c.KC):
                        ins = t.matmul(pb[ub][:, 0:c.NS],
                                       xq_fv[par][:, kc, sub * 128:(sub + 1) * 128],
                                       wusl_v[wsl][:, kc, :], start=(kc == 0),
                                       stop=(kc == c.KC - 1))
                    tk = em.do("t", ins)
                    if sub == SUBS - 1:
                        st["wgu_use"][wsl] = tk
                    st["xq_rd"][par] = tk
                block.tensor(b_gu_mm)
                tmm.append(em.tok("t"))
            emit_wgu_load()

            def b_silu(s, par=par):
                for sub in range(SUBS):
                    gb = 2 * (sub % 2)
                    cidx = par * SUBS + sub
                    em.pre("s", s, tmm[sub], st["gsl_rd"][sub % 2],
                           st["outb"][sub % 2])
                    em.do("s", s.activation(gsl[sub % 2][:], pb[gb][:, 0:c.NS], AF.Silu,
                                            scale=alg[:, cidx:cidx + 1]))
            block.scalar(b_silu)
            assert em.cnt["s"] == sfut + SUBS
            for sub in range(SUBS):
                pbrd[2 * (sub % 2)] = ("s", sfut + sub + 1)

            def b_hh(v, isl=isl):
                for sub in range(SUBS):
                    ub = 2 * (sub % 2) + 1
                    em.pre("v", v, ("s", sfut + sub + 1), st["hh_rd"])
                    em.do("v", v.tensor_tensor(
                        hh_v[:, sub, isl * c.NS:(isl + 1) * c.NS], gsl[sub % 2][:],
                        pb[ub][:, 0:c.NS], ALU.mult))
            block.vector(b_hh)
            assert em.cnt["v"] == vfut0 + SUBS
            for sub in range(SUBS):
                pbrd[2 * (sub % 2) + 1] = ("v", vfut0 + sub + 1)
                st["gsl_rd"][sub % 2] = ("v", vfut0 + sub + 1)
            st["hh_done"] = ("v", vfut0 + SUBS)

        npc = max(1, c.INNER // c.DM)
        wpc = c.INNER // npc
        ntp = wpc // 128

        def emit_D_sub(mg, sub):
            par = mg % 2
            cidx = par * SUBS + sub

            def b_hq(v, sub=sub, cidx=cidx):
                em.pre("v", v, st["hh_done"])
                em.do("v", v.reduce_max(tmp1[:, 0:1], hh_v[:, sub, :], axis=AX.X,
                                        apply_absolute_value=True))
                em.pre("v", v)
                em.do("v", v.tensor_scalar(tmp1[:, 1:2], tmp1[:, 0:1],
                                           alu_[:, cidx:cidx + 1], None, ALU.mult))
                em.pre("v", v)
                em.do("v", v.tensor_scalar(tmp1[:, 1:2], tmp1[:, 1:2], 1e-5, None, ALU.max))
                em.pre("v", v)
                em.do("v", v.reciprocal(tmp1[:, 2:3], tmp1[:, 1:2]))
                em.pre("v", v)
                em.do("v", v.tensor_scalar(tmp1[:, 3:4], tmp1[:, 2:3],
                                           alu_[:, cidx:cidx + 1], 127.0, ALU.mult, ALU.mult))
                em.pre("v", v)
                em.do("v", v.tensor_scalar(gam[:, sub:sub + 1], tmp1[:, 1:2],
                                           mv_m[:, 2:3], 1.0 / 127.0, ALU.mult, ALU.mult))
            block.vector(b_hq)

            for pc in range(npc):
                def b_hp(v, sub=sub, pc=pc):
                    em.pre("v", v, st["xqi_rd"])
                    em.do("v", v.tensor_scalar(qi_b[:, 0:wpc],
                                               hh_v[:, sub, pc * wpc:(pc + 1) * wpc],
                                               tmp1[:, 3:4], None, ALU.mult))
                    em.pre("v", v, st["zfb_rd"])
                    em.do("v", v.tensor_copy(zf_b[:, 0:wpc], qi_b[:, 0:wpc]))
                    em.pre("v", v)
                    em.do("v", v.scalar_tensor_tensor(
                        hh_v[:, sub, pc * wpc:(pc + 1) * wpc],
                        hh_v[:, sub, pc * wpc:(pc + 1) * wpc], tmp1[:, 3:4],
                        zf_b[:, 0:wpc], ALU.mult, ALU.subtract))
                    em.pre("v", v, st["qtm_rd"])
                    em.do("v", v.tensor_scalar(qtm_b[:, 0:wpc],
                                               hh_v[:, sub, pc * wpc:(pc + 1) * wpc],
                                               0.5, None, ALU.is_gt))
                    em.pre("v", v)
                    em.do("v", v.tensor_scalar(hh_v[:, sub, pc * wpc:(pc + 1) * wpc],
                                               hh_v[:, sub, pc * wpc:(pc + 1) * wpc],
                                               -0.5, None, ALU.is_lt))
                    em.pre("v", v)
                    em.do("v", v.tensor_tensor(zf_b[:, 0:wpc], zf_b[:, 0:wpc],
                                               qtm_b[:, 0:wpc], ALU.add))
                    em.pre("v", v)
                    em.do("v", v.tensor_tensor(qtm_b[:, 0:wpc], zf_b[:, 0:wpc],
                                               hh_v[:, sub, pc * wpc:(pc + 1) * wpc],
                                               ALU.subtract))
                block.vector(b_hp)
                st["xqi_rd"] = em.tok("v")
                st["zfb_rd"] = em.tok("v")
                t_hqc = em.tok("v")

                def b_hqtp(t):
                    em.wait("t", t, t_hqc, st["ptp_rd"])
                    for j in range(ntp):
                        ins = t.transpose(ptp[:, j * 128:(j + 1) * 128],
                                          qtm_b[:, j * 128:(j + 1) * 128], ident[:])
                    em.do("t", ins)
                block.tensor(b_hqtp)
                st["qtm_rd"] = em.tok("t")
                t_htp = em.tok("t")

                def b_hqev(v, sub=sub, pc=pc):
                    em.pre("v", v, t_htp,
                           st["hqfm_rd"] if (sub == 0 and pc == 0) else None)
                    em.do("v", v.tensor_copy(
                        hq_fv[:, pc * ntp:(pc + 1) * ntp, sub * 128:(sub + 1) * 128],
                        ptp[:, 0:ntp * 128].rearrange("p (k t) -> p k t", k=ntp)))
                block.vector(b_hqev)
                st["ptp_rd"] = em.tok("v")
            st["hq_rdy"][sub] = em.tok("v")
            if sub == SUBS - 1:
                st["hh_rd"] = em.tok("v")

        def emit_E_mm(mg, on, sub):
            sl = sub % 2
            dsl = (mg * NON + on) % 2

            def b_d_mm(t, on=on, sub=sub, sl=sl, dsl=dsl):
                em.wait("t", t, st["wd_tok"][dsl], st["hq_rdy"][sub], pbrd[2 + sl])
                for ic in range(c.IC):
                    ins = t.matmul(pb[2 + sl][:, 0:c.NS],
                                   hq_fv[:, ic, sub * 128:(sub + 1) * 128],
                                   wdsl_v[dsl][:, ic, :], start=(ic == 0),
                                   stop=(ic == c.IC - 1))
                tk = em.do("t", ins)
                if sub == SUBS - 1:
                    st["wd_use"][dsl] = tk
                st["hqfm_rd"] = tk
            block.tensor(b_d_mm)
            return em.tok("t")

        def make_ev(mg, on, sub, t_dm):
            sl = sub % 2
            par = mg % 2

            def run():
                def b_d_ev(v, on=on, sub=sub, sl=sl, par=par):
                    em.pre("v", v, t_dm, st["outb"][sl], st["gsl_rd"][sl])
                    em.do("v", v.scalar_tensor_tensor(
                        gsl[sl][:], pb[2 + sl][:, 0:c.NS], gam[:, sub:sub + 1],
                        res1v[par][:, sub, on * c.NS:(on + 1) * c.NS],
                        ALU.mult, ALU.add))
                block.vector(b_d_ev)
                pbrd[2 + sl] = em.tok("v")
                if on == NON - 1 and sub == SUBS - 1:
                    st["res1_rd"][par] = em.tok("v")
                t_de = em.tok("v")

                def b_out(sy, on=on, sub=sub, sl=sl):
                    em.wait("qd", sy, t_de)
                    t0 = mg * c.MEGA + sub * 128
                    st["outb"][sl] = em.dma_on(f"qo{sl}", sy.dma_start(
                        out=out[t0:t0 + 128, on * c.NS:(on + 1) * c.NS],
                        in_=gsl[sl][:]))
                block.sync(b_out)
            return run

        emit_wgu_load()
        emit_wgu_load()
        emit_wd_load()
        emit_wd_load()
        for sub in range(SUBS):
            emit_A_load(0, sub)
            emit_A_main1(0, sub)
            emit_A_main2(0, sub)
            emit_A_tp(0, sub)
        st["xq_rdy"][0] = em.tok("v")

        for mg in range(c.NMEGA):
            nxt = mg + 1

            for isl in range(NW):
                emit_BC_isl(mg, isl)
                if nxt < c.NMEGA:
                    if isl == 0:
                        emit_A_load(nxt, 0)
                    else:
                        s, r = divmod(isl - 1, 3)
                        if s < SUBS:
                            if r == 0:
                                emit_A_main1(nxt, s)
                            elif r == 1:
                                emit_A_main2(nxt, s)
                            else:
                                emit_A_tp(nxt, s)
                                if s + 1 < SUBS:
                                    emit_A_load(nxt, s + 1)
            if nxt < c.NMEGA:
                st["xq_rdy"][nxt % 2] = em.tok("v")

            for on in range(NON):
                prev = None
                for sub in range(SUBS):
                    if on == 0:
                        emit_D_sub(mg, sub)
                    t_dm = emit_E_mm(mg, on, sub)
                    if prev is not None:
                        prev()
                    prev = make_ev(mg, on, sub, t_dm)
                prev()
                emit_wd_load()

        if dbg:
            def b_dbg(sy):
                em.wait("qd", sy, *em.all_tokens())
                em.dma("qd", sy.dma_start(out=d_sv[:, 0:3], in_=sv_m[:]))
                em.dma("qd", sy.dma_start(out=d_sv[:, 3:6], in_=mv_m[:]))
                em.dma("qd", sy.dma_start(out=d_wgt, in_=wgt_s[:, 0:512]))
                em.dma("qd", sy.dma_start(out=d_wdt, in_=wdt_s[0:512, :]))
                em.dma("qd", sy.dma_start(out=d_hh, in_=hh[:, 0:c.INNER]))
                em.dma("qd", sy.dma_start(out=d_r1, in_=res1b[0][:, 0:c.DM]))
                em.dma("qd", sy.dma_start(out=d_al[:, 0:c.MEGA // 128], in_=alg[:, 0:c.MEGA // 128]))
                em.dma("qd", sy.dma_start(out=d_al[:, c.MEGA // 128:2 * (c.MEGA // 128)], in_=alu_[:, 0:c.MEGA // 128]))
                em.dma("qd", sy.dma_start(out=d_al[:, 2 * (c.MEGA // 128):], in_=gam[:]))
                em.dma("qd", sy.dma_start(out=d_lnx, in_=lnx_s))
                for h in range(c.NHL):
                    em.dma("qd", sy.dma_start(out=d_qcs[h * 64:(h + 1) * 64, :],
                                              in_=qs_s[h * 128:h * 128 + 64, :]))
                    em.dma("qd", sy.dma_start(out=d_qrs[h * 32:(h + 1) * 32, :],
                                              in_=qs_s[h * 128 + 64:h * 128 + 96, :]))
                em.dma("qd", sy.dma_start(out=d_aop, in_=aop_s))
                em.dma("qd", sy.dma_start(out=d_aog, in_=aog_s))
                em.dma("qd", sy.dma_start(out=d_cq, in_=cq_s[:, 0:512]))
            block.sync(b_dbg)

            def b_dbg2(g):
                em.wait("gd", g, *em.all_tokens())
                em.dma("gd", g.dma_start(out=d_maxk, in_=maxk[:]))
            block.gpsimd(b_dbg2)

        barrier(em, block)
        p3_es.close()

    return nc


def _shard_inputs(c, inputs):
    x = np.ascontiguousarray(inputs["L0_attn_x"])
    W = {k: np.ascontiguousarray(v) for k, v in inputs.items()}
    maps = []
    for core in range(2 * c.n_pairs):
        p, hc = core // 2, core % 2
        m = {
            "x_seq": x[p],
            "x_own": x[p, hc * c.OWN:(hc + 1) * c.OWN],
            "wdq": W["L0_attn_W_DQ"],
            "wuq": W["L0_attn_W_UQ"][:, hc * c.YD:(hc + 1) * c.YD],
            "wqr": W["L0_attn_W_QR"][:, hc * c.NHL * 32:(hc + 1) * c.NHL * 32],
            "wdkv": W["L0_attn_W_DKV"],
            "wuk": W["L0_attn_W_UK"][:, hc * c.YD:(hc + 1) * c.YD],
            "wuv": W["L0_attn_W_UV"][:, hc * c.YD:(hc + 1) * c.YD],
            "wkr": W["L0_attn_W_KR"],
            "wo": W["L0_attn_W_O"][hc * c.YD:(hc + 1) * c.YD, :],
            "wg": W["L0_ffn_w_gate"],
            "wu": W["L0_ffn_w_up"],
            "wd": W["L0_ffn_w_down"],
        }
        maps.append({k: np.ascontiguousarray(v) for k, v in m.items()})
    return maps


def kernel(**inputs):
    c = FULL
    nc = build_nc(c)
    in_maps = _shard_inputs(c, inputs)
    res = run_bass_kernel_spmd(nc, in_maps, core_ids=list(range(2 * c.n_pairs)))
    B = c.n_pairs
    outp = np.empty((B, c.T, c.DM), dtype=np.float32)
    for core in range(2 * B):
        p, hc = core // 2, core % 2
        outp[p, hc * c.OWN:(hc + 1) * c.OWN] = res.results[core]["out"]
    return outp


# version 27 (final)
# speedup vs baseline: 1.2448x; 1.0235x over previous
import math
from contextlib import ExitStack

import numpy as np
import ml_dtypes

import concourse.bass as bass
import concourse.mybir as mybir
from concourse.bass_utils import run_bass_kernel_spmd

F32 = mybir.dt.float32
BF16 = mybir.dt.bfloat16
I32 = mybir.dt.int32
AF = mybir.ActivationFunctionType
ALU = mybir.AluOpType
AX = mybir.AxisListType

RS96 = 1.0 / math.sqrt(96.0)
EPS = 1e-5
TWO_PI = 2.0 * math.pi


class CFG:
    def __init__(self, T, DM, CQ, INNER, NHL, n_pairs, SLOT, NS, MEGA=None):
        self.T, self.DM, self.CQ, self.INNER = T, DM, CQ, INNER
        self.NHL, self.n_pairs, self.SLOT, self.NS = NHL, n_pairs, SLOT, NS
        self.DC = 256
        self.KC = DM // 128
        self.CQC = CQ // 128
        self.NKT = T // 128
        self.NSLOT = T // SLOT
        self.OWN = T // 2
        self.YD = NHL * 64
        self.YC = self.YD // 128
        self.QRC = NHL * 32 // 128
        self.IC = INNER // 128
        self.MEGA = MEGA if MEGA is not None else min(512, self.OWN)
        self.NMEGA = self.OWN // self.MEGA
        assert NHL % 2 == 0 and self.YC >= 1 and self.QRC >= 1
        assert SLOT % 128 == 0 and NS <= 512 and T % SLOT == 0 and T % NS == 0


FULL = CFG(T=4096, DM=1024, CQ=4096, INNER=4096, NHL=8, n_pairs=4, SLOT=512, NS=512,
           MEGA=256)
MINI = CFG(T=512, DM=512, CQ=512, INNER=512, NHL=4, n_pairs=1, SLOT=128, NS=128,
           MEGA=128)


class Em:
    STREAMS = ("t", "v", "s", "g", "qd", "gd", "cc")

    def __init__(self, nc, es):
        self.nc = nc
        self.es = es
        self.sem = {k: es.enter_context(nc.semaphore("sem_" + k)) for k in self.STREAMS}
        self.cnt = {k: 0 for k in self.STREAMS}
        self.waited = {}

    def dma_on(self, name, instr):
        if name not in self.sem:
            self.sem[name] = self.es.enter_context(self.nc.semaphore("sem_" + name))
            self.cnt[name] = 0
        return self.do(name, instr, 16)

    def tok(self, s):
        return (s, self.cnt[s])

    def _wait(self, ename, eng, s, c):
        if c <= 0:
            return
        key = (ename, s)
        if self.waited.get(key, 0) >= c:
            return
        eng.wait_ge(self.sem[s], c)
        self.waited[key] = c

    def wait(self, ename, eng, *tokens):
        need = {}
        for t in tokens:
            if t is None:
                continue
            s, cv = t
            if cv > need.get(s, 0):
                need[s] = cv
        for s in sorted(need):
            self._wait(ename, eng, s, need[s])

    def pre(self, ename, eng, *tokens, selfsync=True):
        if selfsync and ename in ("v", "s", "g"):
            self._wait(ename, eng, ename, self.cnt[ename])
        self.wait(ename, eng, *tokens)

    def do(self, s, instr, n=1):
        instr.then_inc(self.sem[s], n)
        self.cnt[s] += n
        return (s, self.cnt[s])

    def dma(self, s, instr):
        return self.do(s, instr, 16)

    def all_tokens(self):
        return [(s, cv) for s, cv in self.cnt.items() if cv > 0]


def barrier(em, block):
    toks = em.all_tokens()
    for name, dec in (("t", block.tensor), ("v", block.vector), ("s", block.scalar),
                      ("g", block.gpsimd), ("qd", block.sync)):
        def body(eng, _t=tuple(toks), _n=name):
            em.wait(_n, eng, *_t)
        dec(body)


def build_nc(cfg, dbg=False):
    c = cfg
    nc = bass.Bass("TRN2", target_bir_lowering=False)

    x_seq = nc.dram_tensor("x_seq", [c.T, c.DM], F32, kind="ExternalInput").ap()
    x_own = nc.dram_tensor("x_own", [c.OWN, c.DM], F32, kind="ExternalInput").ap()
    wdq = nc.dram_tensor("wdq", [c.DM, c.CQ], F32, kind="ExternalInput").ap()
    wuq = nc.dram_tensor("wuq", [c.CQ, c.YD], F32, kind="ExternalInput").ap()
    wqr = nc.dram_tensor("wqr", [c.CQ, c.NHL * 32], F32, kind="ExternalInput").ap()
    wdkv = nc.dram_tensor("wdkv", [c.DM, c.DC], F32, kind="ExternalInput").ap()
    wuk = nc.dram_tensor("wuk", [c.DC, c.YD], F32, kind="ExternalInput").ap()
    wuv = nc.dram_tensor("wuv", [c.DC, c.YD], F32, kind="ExternalInput").ap()
    wkr = nc.dram_tensor("wkr", [c.DM, 32], F32, kind="ExternalInput").ap()
    wo = nc.dram_tensor("wo", [c.YD, c.DM], F32, kind="ExternalInput").ap()
    wg = nc.dram_tensor("wg", [c.INNER, c.DM], F32, kind="ExternalInput").ap()
    wu = nc.dram_tensor("wu", [c.INNER, c.DM], F32, kind="ExternalInput").ap()
    wd = nc.dram_tensor("wd", [c.DM, c.INNER], F32, kind="ExternalInput").ap()
    out = nc.dram_tensor("out", [c.OWN, c.DM], F32, kind="ExternalOutput").ap()
    if dbg:
        d_lnx = nc.dram_tensor("d_lnx", [c.DM, c.T], BF16, kind="ExternalOutput").ap()
        d_qcs = nc.dram_tensor("d_qcs", [c.YD, c.T], BF16, kind="ExternalOutput").ap()
        d_qrs = nc.dram_tensor("d_qrs", [c.NHL * 32, c.T], BF16, kind="ExternalOutput").ap()
        d_aop = nc.dram_tensor("d_aop", [2, c.DM, c.OWN], F32, kind="ExternalOutput").ap()
        d_aog = nc.dram_tensor("d_aog", [c.DM, c.OWN], F32, kind="ExternalOutput").ap()
        d_maxk = nc.dram_tensor("d_maxk", [c.NHL, 1], F32, kind="ExternalOutput").ap()
        d_cq = nc.dram_tensor("d_cq", [c.CQ, 512], BF16, kind="ExternalOutput").ap()
        d_scale = nc.dram_tensor("d_scale", [32, c.T], F32, kind="ExternalOutput").ap()
        d_ini = nc.dram_tensor("d_ini", [32, 2050], F32, kind="ExternalOutput").ap()
        d_sv = nc.dram_tensor("d_sv", [128, 6], F32, kind="ExternalOutput").ap()
        d_wgt = nc.dram_tensor("d_wgt", [c.DM, 512], BF16, kind="ExternalOutput").ap()
        d_wdt = nc.dram_tensor("d_wdt", [512, c.DM], BF16, kind="ExternalOutput").ap()
        d_hh = nc.dram_tensor("d_hh", [128, c.INNER], F32, kind="ExternalOutput").ap()
        d_r1 = nc.dram_tensor("d_r1", [128, c.DM], F32, kind="ExternalOutput").ap()
        d_al = nc.dram_tensor("d_al", [128, 3 * (c.MEGA // 128)], F32, kind="ExternalOutput").ap()
        d_scp4 = nc.dram_tensor("d_scp4", [128, c.T], F32, kind="ExternalOutput").ap()

    lnx_s = nc.dram_tensor("lnx_s", [c.DM, c.T], BF16, kind="Internal").ap()
    cq_s = nc.dram_tensor("cq_s", [c.CQ, c.T], BF16, kind="Internal").ap()
    qs_s = nc.dram_tensor("qs_s", [c.NHL * 128, c.T], BF16, kind="Internal").ap()
    NRS = c.NSLOT // 2
    aop_p = [nc.dram_tensor(f"aop_p{j}", [2, c.DM, c.SLOT], F32, kind="Internal").ap()
             for j in range(NRS)]
    aog_p = [nc.dram_tensor(f"aog_p{j}", [c.DM, c.SLOT], F32, kind="Internal").ap()
             for j in range(NRS)]
    wdq_bf = nc.dram_tensor("wdq_bf", [c.DM, c.CQ], BF16, kind="Internal").ap()
    wuq_bf = nc.dram_tensor("wuq_bf", [c.CQ, c.YD], BF16, kind="Internal").ap()
    wqr_bf = nc.dram_tensor("wqr_bf", [c.CQ, c.NHL * 32], BF16, kind="Internal").ap()
    wo_bf = nc.dram_tensor("wo_bf", [c.YD, c.DM], BF16, kind="Internal").ap()
    wdkv_bf = nc.dram_tensor("wdkv_bf", [c.DM, c.DC], BF16, kind="Internal").ap()
    wuk_bf = nc.dram_tensor("wuk_bf", [c.DC, c.YD], BF16, kind="Internal").ap()
    wuv_bf = nc.dram_tensor("wuv_bf", [c.DC, c.YD], BF16, kind="Internal").ap()
    wkr_bf = nc.dram_tensor("wkr_bf", [c.DM, 32], BF16, kind="Internal").ap()
    wgt_s = nc.dram_tensor("wgt_s", [c.DM, c.INNER], BF16, kind="Internal").ap()
    wut_s = nc.dram_tensor("wut_s", [c.DM, c.INNER], BF16, kind="Internal").ap()
    wdt_s = nc.dram_tensor("wdt_s", [c.INNER, c.DM], BF16, kind="Internal").ap()

    lnx_sv = lnx_s.rearrange("(c p) t -> p c t", p=128)
    cq_sv = cq_s.rearrange("(o p) t -> p o t", p=128)
    qs_sv = qs_s.rearrange("(o p) t -> p o t", p=128)
    groups = [[2 * p, 2 * p + 1] for p in range(c.n_pairs)]

    es = ExitStack()
    sb = lambda n, sh, dt, **kw: es.enter_context(nc.sbuf_tensor(n, sh, dt, **kw))
    ps = lambda n, sh, dt: es.enter_context(nc.psum_tensor(n, sh, dt))
    with es:
        em = Em(nc, es)
        pb = [ps(f"pb{i}", [128, 512], F32) for i in range(6)]
        ptp = ps("ptp", [128, 1024], BF16)
        ptp2 = ps("ptp2", [128, 1024], BF16)
        PT = 6

        ident = sb("ident", [128, 128], BF16)
        tri = sb("tri", [128, 128], BF16)
        ones_f = sb("ones_f", [128, 128], F32)
        ident_f = sb("ident_f", [128, 128], F32)
        sv_m = sb("sv_m", [128, 3], F32)
        mv_m = sb("mv_m", [128, 3], F32)
        att_es = ExitStack()
        sba = lambda n, sh, dt, **kw: att_es.enter_context(nc.sbuf_tensor(n, sh, dt, **kw))
        ehot = sba("ehot", [c.NHL, c.NHL * 128], BF16)
        ehot_v = ehot[:].rearrange("p (h f) -> p h f", f=128)
        hkh = sba("hkh", [128, c.NHL * c.NHL], BF16)
        maxk = sba("maxk", [c.NHL, 1], F32)
        Kh = [sba(f"Kh{i}", [128, c.T], BF16) for i in range(c.NHL)]
        Vp = sba("Vp", [128, c.NKT * c.NHL * 65], BF16)
        Vp_v = Vp[:].rearrange("p (k h d) -> p k h d", h=c.NHL, d=65)
        scp_c = sba("scp_c", [32, c.T], F32)
        QW = 2048
        quant_es = ExitStack()
        sbq = lambda n, sh, dt: quant_es.enter_context(nc.sbuf_tensor(n, sh, dt))
        wnat3 = [sbq(f"wnat3{i}", [128, QW], F32) for i in range(1)]
        zf3 = sbq("zf3", [128, 1024], BF16)
        wabs3 = sbq("wabs3", [128, 4], F32)
        wsc3 = sbq("wsc3", [1, 2], F32)
        wstg3 = sbq("wstg3", [128, 1024], BF16)
        scale_es = ExitStack()
        scale_t = scale_es.enter_context(nc.sbuf_tensor("scale_t", [32, c.T], F32))
        wn2_es = ExitStack()
        wnat3.append(wn2_es.enter_context(nc.sbuf_tensor("wnat3x", [128, QW], F32)))

        block = es.enter_context(nc.Block())

        ini_es = ExitStack()
        sbi = lambda n, sh, dt: ini_es.enter_context(nc.sbuf_tensor(n, sh, dt))
        io_pf = sbi("io_pf", [128, 128], I32)
        io_j = sbi("io_j", [32, 1], I32)
        io_pos = sbi("io_pos", [1, 512], I32)
        pos_f = sbi("pos_f", [1, 512], F32)
        jf = sbi("jf", [32, 2], F32)
        invf = sbi("invf", [32, 1], F32)
        emb = sbi("emb", [32, 512], F32)
        rr = sbi("rr", [32, 512], F32)
        u32 = sbi("u32", [32, 512], I32)
        uf = sbi("uf", [32, 512], F32)
        sint = sbi("sint", [32, 512], F32)
        cpl = sbi("cpl", [32, 512], F32)
        pi4 = sbi("pi4", [32, 1], F32)
        io_e = sbi("io_e", [c.NHL, 128], I32)

        def ini_g(g):
            g.iota(io_pf[:], pattern=[[-1, 128]], base=0, channel_multiplier=1)
            g.iota(io_j[:], pattern=[[0, 1]], base=0, channel_multiplier=1)
            em.do("g", g.iota(io_e[:], pattern=[[0, 128]], base=0, channel_multiplier=1))
        block.gpsimd(ini_g)
        t_iota = em.tok("g")

        def ini_v(v):
            em.pre("v", v, t_iota)
            em.do("v", v.tensor_scalar(ident[:], io_pf[:], 0, None, ALU.is_equal))
            em.pre("v", v)
            em.do("v", v.memset(pi4[:], math.pi / 4.0))
            em.pre("v", v)
            em.do("v", v.tensor_scalar(tri[:], io_pf[:], 0, None, ALU.is_le))
            em.pre("v", v)
            em.do("v", v.tensor_scalar(ident_f[:], io_pf[:], 0, None, ALU.is_equal))
            em.pre("v", v)
            em.do("v", v.memset(ones_f[:], 1.0))
            em.pre("v", v)
            em.do("v", v.tensor_copy(jf[:, 0:1], io_j[:]))
            em.pre("v", v)
            em.do("v", v.tensor_scalar(jf[:, 1:2], jf[:, 0:1], 16.0, None, ALU.is_ge))
            em.pre("v", v)
            em.do("v", v.scalar_tensor_tensor(jf[:, 0:1], jf[:, 1:2], -16.0, jf[:, 0:1],
                                              ALU.mult, ALU.add))
            for h in range(c.NHL):
                em.pre("v", v)
                em.do("v", v.tensor_scalar(ehot_v[:, h, :], io_e[:], h, None, ALU.is_equal))
            em.pre("v", v)
            em.do("v", v.memset(maxk[:], 0.0))
            em.pre("v", v)
            em.do("v", v.memset(hkh[:], 0.0))
            for h in range(c.NHL):
                em.pre("v", v)
                em.do("v", v.memset(hkh[0:96, h * c.NHL + h: h * c.NHL + h + 1], 1.0))
            for h in range(c.NHL):
                em.pre("v", v)
                em.do("v", v.memset(Kh[h][:], 0.0))
        block.vector(ini_v)
        t_vini = em.tok("v")

        def ini_s(s):
            em.pre("s", s, t_vini)
            em.do("s", s.activation(invf[:], jf[:, 0:1], AF.Exp,
                                    scale=-math.log(10000.0) / 16.0))
        block.scalar(ini_s)
        t_invf = em.tok("s")

        wsl = min(512, c.T)
        nsl = c.T // wsl
        tok_posf_rd = None
        tok_iop_rd = None
        for i in range(nsl):
            def ini_gp(g, i=i):
                em.pre("g", g, tok_iop_rd)
                em.do("g", g.iota(io_pos[0:1, 0:wsl], pattern=[[1, wsl]], base=i * wsl,
                                  channel_multiplier=0))
            block.gpsimd(ini_gp)
            t_iop = em.tok("g")

            def ini_vp(v, i=i):
                em.pre("v", v, t_iop, tok_posf_rd)
                em.do("v", v.tensor_copy(pos_f[0:1, 0:wsl], io_pos[0:1, 0:wsl]))
            block.vector(ini_vp)
            tok_iop_rd = em.tok("v")
            t_pf = em.tok("v")

            def ini_tp(t, i=i):
                em.wait("t", t, t_invf, t_vini, t_pf)
                em.do("t", t.matmul(pb[0][0:32, 0:wsl], ones_f[0:1, 0:32],
                                    pos_f[0:1, 0:wsl], start=True, stop=True))
            block.tensor(ini_tp)
            tok_posf_rd = em.tok("t")
            t_posbc = em.tok("t")

            def ini_v3(v, i=i):
                em.pre("v", v, t_posbc)
                em.do("v", v.tensor_scalar(emb[:, 0:wsl], pb[0][0:32, 0:wsl],
                                           invf[:], None, ALU.mult))
                em.pre("v", v)
                em.do("v", v.tensor_scalar(u32[:, 0:wsl], emb[:, 0:wsl], 1.0 / TWO_PI, 0.125,
                                           ALU.mult, ALU.add))
                em.pre("v", v)
                em.do("v", v.tensor_copy(uf[:, 0:wsl], u32[:, 0:wsl]))
                em.pre("v", v)
                em.do("v", v.scalar_tensor_tensor(rr[:, 0:wsl], emb[:, 0:wsl], 1.0 / TWO_PI,
                                                  uf[:, 0:wsl], ALU.mult, ALU.subtract))
                em.pre("v", v)
                em.do("v", v.tensor_scalar(cpl[:, 0:wsl], rr[:, 0:wsl], 0.375, None, ALU.is_gt))
                em.pre("v", v)
                em.do("v", v.tensor_scalar(rr[:, 0:wsl], rr[:, 0:wsl], -0.625, None, ALU.is_lt))
                em.pre("v", v)
                em.do("v", v.scalar_tensor_tensor(cpl[:, 0:wsl], rr[:, 0:wsl], -1.0,
                                                  cpl[:, 0:wsl], ALU.mult, ALU.add))
                em.pre("v", v)
                em.do("v", v.tensor_tensor(uf[:, 0:wsl], uf[:, 0:wsl], cpl[:, 0:wsl], ALU.add))
                em.pre("v", v)
                em.do("v", v.scalar_tensor_tensor(rr[:, 0:wsl], uf[:, 0:wsl], -TWO_PI,
                                                  emb[:, 0:wsl], ALU.mult, ALU.add))
            block.vector(ini_v3)
            t_rr = em.tok("v")

            def ini_s2(s, i=i):
                em.pre("s", s, t_rr)
                em.do("s", s.activation(sint[:, 0:wsl], rr[:, 0:wsl], AF.Sin,
                                        bias=pi4[:], scale=1.0))
            block.scalar(ini_s2)
            t_sin = em.tok("s")

            def ini_v4(v, i=i):
                em.pre("v", v, t_sin)
                em.do("v", v.tensor_scalar(scale_t[:, i * wsl:(i + 1) * wsl], sint[:, 0:wsl],
                                           math.sqrt(2.0), None, ALU.mult))
                em.pre("v", v)
                em.do("v", v.tensor_scalar(scp_c[:, i * wsl:(i + 1) * wsl], sint[:, 0:wsl],
                                           math.sqrt(2.0) * RS96, None, ALU.mult))
            block.vector(ini_v4)
            if dbg and i == nsl - 1:
                def ini_dbg(g, i=i):
                    em.wait("gd", g, em.tok("v"))
                    em.dma("gd", g.dma_start(out=d_ini[:, 0:512], in_=emb[:, 0:512]))
                    em.dma("gd", g.dma_start(out=d_ini[:, 512:1024], in_=rr[:, 0:512]))
                    em.dma("gd", g.dma_start(out=d_ini[:, 1024:1536], in_=uf[:, 0:512]))
                    em.dma("gd", g.dma_start(out=d_ini[:, 1536:2048], in_=sint[:, 0:512]))
                    em.dma("gd", g.dma_start(out=d_ini[:, 2048:2049], in_=invf[:], allow_slow_non_contiguous=True))
                    em.dma("gd", g.dma_start(out=d_ini[:, 2049:2050], in_=jf[:, 0:1], allow_slow_non_contiguous=True))
                block.gpsimd(ini_dbg)
        t_scp = em.tok("v")

        def ini_g2(g):
            em.wait("gd", g, t_scp)
            em.pre("g", g)
            em.do("g", g.memset(Vp_v[:, :, :, 64:65], 1.0))
            em.dma_on("wbs", g.dma_start(out=wdkv_bf, in_=wdkv))
            em.dma_on("wbs", g.dma_start(out=wuk_bf, in_=wuk))
            em.dma_on("wbs", g.dma_start(out=wuv_bf, in_=wuv))
            em.dma_on("wbs", g.dma_start(out=wkr_bf, in_=wkr))
            em.dma_on("wbl", g.dma_start(out=wdq_bf, in_=wdq))
            em.dma_on("wbl", g.dma_start(out=wuq_bf, in_=wuq))
            em.dma_on("wbl", g.dma_start(out=wqr_bf, in_=wqr))
            em.dma_on("wbl", g.dma_start(out=wo_bf, in_=wo))
        block.gpsimd(ini_g2)
        t_wbs = em.tok("wbs")
        t_wbl = em.tok("wbl")
        t_scp4 = em.tok("gd")
        t_vones = em.tok("g")

        barrier(em, block)
        ini_es.close()

        q_state = {"wn": [None, None], "wn_use": [None, None], "zf_rd": None,
                   "zb_rd": [None, None], "stg": [None] * 4, "pb3": None,
                   "i": 0}
        wgt_v3 = wgt_s.rearrange("(j p) r -> p j r", p=128)
        wut_v3 = wut_s.rearrange("(j p) r -> p j r", p=128)
        wdt_v3 = wdt_s.rearrange("(j p) r -> p j r", p=128)

        def _mk_units():
            units = []
            specs = [(wg, wgt_v3, c.INNER, c.DM, 0), (wu, wut_v3, c.INNER, c.DM, 1),
                     (wd, wdt_v3, c.DM, c.INNER, 2)]

            def u_clear():
                def f():
                    def b(v):
                        em.pre("v", v)
                        em.do("v", v.memset(wabs3[:, 0:3], 0.0))
                    block.vector(b)
                return (None, f)

            def u_p1(wsrc, r0, nr, cc0, ncol, sl, mi):
                def fl():
                    wn1b = wnat3[0] if sl == 0 else wnat3[-1]

                    def b_l(sy):
                        em.wait("qd", sy, q_state["wn_use"][sl])
                        q_state["wn"][sl] = em.dma_on(f"qn{sl}", sy.dma_start(
                            out=wn1b[:, 0:nr * ncol].rearrange("p (j f) -> p j f", j=nr),
                            in_=wsrc[r0 * 128:(r0 + nr) * 128, cc0:cc0 + ncol].rearrange(
                                "(j p) f -> p j f", p=128)))
                    block.sync(b_l)

                def f():
                    wn1b = wnat3[0] if sl == 0 else wnat3[-1]

                    def b_a(v):
                        em.pre("v", v, q_state["wn"][sl])
                        em.do("v", v.reduce_sum(wabs3[:, 3:4], wn1b[:, 0:nr * ncol],
                                                axis=AX.X, apply_absolute_value=True))
                        em.pre("v", v)
                        q_state["wn_use"][sl] = em.do("v", v.tensor_tensor(
                            wabs3[:, mi:mi + 1], wabs3[:, mi:mi + 1], wabs3[:, 3:4],
                            ALU.add))
                    block.vector(b_a)
                return (fl, f)

            def u_fin(mi, RC):
                def f():
                    def b_s(t):
                        em.wait("t", t, em.tok("v"), q_state["pb3"])
                        em.do("t", t.matmul(pb[4][0:1, 0:1], wabs3[:, mi:mi + 1],
                                            ones_f[:, 0:1], start=True, stop=True))
                    block.tensor(b_s)
                    t_sm = em.tok("t")

                    def b_m(v):
                        em.pre("v", v, t_sm)
                        em.do("v", v.tensor_scalar(wsc3[0:1, 0:1], pb[4][0:1, 0:1],
                                                   1.0 / RC, 1e-5, ALU.mult, ALU.max))
                        em.pre("v", v)
                        em.do("v", v.reciprocal(wsc3[0:1, 1:2], wsc3[0:1, 0:1]))
                    block.vector(b_m)
                    t_mn = em.tok("v")

                    def b_b(t):
                        em.wait("t", t, t_mn)
                        t.matmul(pb[4][:, 0:1], ones_f[0:1, :], wsc3[0:1, 1:2],
                                 start=True, stop=True)
                        em.do("t", t.matmul(pb[4][:, 1:2], ones_f[0:1, :], wsc3[0:1, 0:1],
                                            start=True, stop=True))
                    block.tensor(b_b)
                    t_b1 = em.tok("t")

                    def b_c(v):
                        em.pre("v", v, t_b1)
                        em.do("v", v.tensor_copy(sv_m[:, mi:mi + 1], pb[4][:, 0:1]))
                        em.pre("v", v)
                        em.do("v", v.tensor_copy(mv_m[:, mi:mi + 1], pb[4][:, 1:2]))
                    block.vector(b_c)
                    q_state["pb3"] = em.tok("v")
                return (None, f)

            def u_p2(wsrc, wdst_v, mi, r0, nr, cc0, ncol, cc_total, sl):
                def fl():
                    wn3b = wnat3[0] if sl == 0 else wnat3[-1]

                    def b_l(sy):
                        em.wait("qd", sy, q_state["wn_use"][sl])
                        q_state["wn"][sl] = em.dma_on(f"qn{sl}", sy.dma_start(
                            out=wn3b[:, 0:nr * ncol].rearrange("p (j f) -> p j f", j=nr),
                            in_=wsrc[r0 * 128:(r0 + nr) * 128, cc0:cc0 + ncol].rearrange(
                                "(j p) f -> p j f", p=128)))
                    block.sync(b_l)

                def f():
                    wn3b = wnat3[0] if sl == 0 else wnat3[-1]
                    nrd = max(1, (nr * ncol) // 1024)
                    strip = min(1024, nr * ncol)
                    ccn = max(1, strip // ncol)
                    jbn = min(ncol, strip) // 128
                    for rd in range(nrd):
                        goff = cc0 + (rd * strip) % ncol
                        gch = r0 + (rd * strip) // ncol
                        jblk = goff // 128
                        ws_ = rd % 2
                        pdst = ptp if ws_ == 0 else ptp2

                        def b_q(v, rd=rd, strip=strip):
                            em.pre("v", v, q_state["wn"][sl], q_state["zb_rd"][0])
                            em.do("v", v.tensor_scalar(
                                zf3[:, 0:strip], wn3b[:, rd * strip:(rd + 1) * strip],
                                sv_m[:, mi:mi + 1], 0.5, ALU.mult, ALU.is_gt))
                            em.pre("v", v)
                            tk = em.do("v", v.tensor_scalar(
                                wn3b[:, rd * strip:(rd + 1) * strip],
                                wn3b[:, rd * strip:(rd + 1) * strip],
                                sv_m[:, mi:mi + 1], -0.5, ALU.mult, ALU.is_lt))
                            if rd == nrd - 1:
                                q_state["wn_use"][sl] = tk
                            em.pre("v", v)
                            em.do("v", v.tensor_tensor(
                                zf3[:, 0:strip], zf3[:, 0:strip],
                                wn3b[:, rd * strip:(rd + 1) * strip], ALU.subtract))
                        block.vector(b_q)
                        t_zb = em.tok("v")

                        def b_t(t, strip=strip, ws_=ws_, pdst=pdst, t_zb=t_zb):
                            em.wait("t", t, t_zb, t_vini, q_state["tp_rd"][ws_])
                            for j in range(strip // 128):
                                ins = t.transpose(pdst[:, j * 128:(j + 1) * 128],
                                                  zf3[:, j * 128:(j + 1) * 128], ident[:])
                            em.do("t", ins)
                        block.tensor(b_t)
                        q_state["zb_rd"][0] = em.tok("t")
                        t_t3 = em.tok("t")

                        def b_e(v, ws_=ws_, pdst=pdst, t_t3=t_t3, strip=strip):
                            em.pre("v", v, t_t3, q_state["stg"][0])
                            em.do("v", v.tensor_copy(wstg3[:, 0:strip], pdst[:, 0:strip]))
                        block.vector(b_e)
                        q_state["tp_rd"][ws_] = em.tok("v")
                        t_e3 = em.tok("v")

                        def b_w(sy, jblk=jblk, gch=gch, t_e3=t_e3,
                                ccn=ccn, jbn=jbn, strip=strip):
                            em.wait("qd", sy, t_e3)
                            tk = None
                            for cc_i in range(ccn):
                                tk = em.dma_on("qw0", sy.dma_start(
                                    out=wdst_v[:, jblk:jblk + jbn,
                                               (gch + cc_i) * 128:(gch + cc_i + 1) * 128],
                                    in_=wstg3[:, cc_i * jbn * 128:(cc_i + 1) * jbn * 128
                                              ].rearrange("p (j r) -> p j r", r=128)))
                            q_state["stg"][0] = tk
                        block.sync(b_w)
                    q_state["i"] += 1
                return (fl, f)

            si = [0]
            units.append(u_clear())
            for (wsrc, wdst_v, R, Cc, mi) in specs:
                nr = max(1, QW // Cc)
                ncol = min(Cc, QW)
                for r0 in range(0, R // 128, nr):
                    for cc0 in range(0, Cc, ncol):
                        units.append(u_p1(wsrc, r0, nr, cc0, ncol, si[0] % 2, mi))
                        si[0] += 1
            fin_start = len(units)
            for (wsrc, wdst_v, R, Cc, mi) in specs:
                units.append(u_fin(mi, float(R) * Cc))
            p2start = len(units)
            for (wsrc, wdst_v, R, Cc, mi) in specs:
                nr = max(1, QW // Cc)
                ncol = min(Cc, QW)
                for r0 in range(0, R // 128, nr):
                    for cc0 in range(0, Cc, ncol):
                        units.append(u_p2(wsrc, wdst_v, mi, r0, nr, cc0, ncol, Cc,
                                          si[0] % 2))
                        si[0] += 1
            return units, fin_start, p2start

        q_state["tp_rd"] = [None, None]
        p3p_units, p3p_fin, p3p_p2start = _mk_units()
        p3p_pos = {"cp": 0, "ld": 0}
        P3P_LA = 2

        def emit_p3p_flush():
            while p3p_pos["cp"] < p3p_pos["ld"]:
                fc = p3p_units[p3p_pos["cp"]][1]
                if fc is not None:
                    fc()
                p3p_pos["cp"] += 1

        def emit_p3p(n, cap=None):
            lim = len(p3p_units) if cap is None else cap
            while n > 0 and p3p_pos["cp"] < lim:
                while (p3p_pos["ld"] < lim
                       and p3p_pos["ld"] < p3p_pos["cp"] + P3P_LA):
                    fl = p3p_units[p3p_pos["ld"]][0]
                    if fl is not None:
                        fl()
                    p3p_pos["ld"] += 1
                fc = p3p_units[p3p_pos["cp"]][1]
                fc()
                p3p_pos["cp"] += 1
                n -= 1


        p1_es = ExitStack()
        sb1 = lambda n, sh, dt: p1_es.enter_context(nc.sbuf_tensor(n, sh, dt))
        wdkv_sb = sb1("wdkv_sb", [128, c.KC * 2 * 128], BF16)
        wdkv_v = wdkv_sb[:].rearrange("p (k o f) -> p k o f", k=c.KC, o=2)
        wuk_sb = sb1("wuk_sb", [128, 2 * c.YD], BF16)
        wuk_v = wuk_sb[:].rearrange("p (k f) -> p k f", k=2)
        wuv_sb = sb1("wuv_sb", [128, 2 * c.YD], BF16)
        wuv_v = wuv_sb[:].rearrange("p (k f) -> p k f", k=2)
        wkr_sb = sb1("wkr_sb", [128, c.KC * 32], BF16)
        wkr_v = wkr_sb[:].rearrange("p (k f) -> p k f", k=c.KC)
        lnt = [sb1(f"p1ln{i}", [128, c.KC * c.NS], BF16) for i in range(2)]
        lnt_v = [x[:].rearrange("p (k t) -> p k t", k=c.KC) for x in lnt]
        ckv = sb1("ckv", [128, 2 * c.NS], BF16)
        ckv_v = ckv[:].rearrange("p (k t) -> p k t", k=2)
        sqb = sb1("sqb", [128, c.NS], BF16)
        krt = sb1("krt", [32, c.NS], BF16)
        knm = sb1("knm", [c.NHL, 1], F32)

        def p1_wload(sy):
            em.wait("qd", sy, t_wbs)
            em.dma_on("p1w", sy.dma_start(out=wdkv_v, in_=wdkv_bf.rearrange(
                "(k p) (o f) -> p k o f", p=128, f=128)))
            em.dma_on("p1w", sy.dma_start(out=wuk_v, in_=wuk_bf.rearrange("(k p) f -> p k f", p=128)))
            em.dma_on("p1w", sy.dma_start(out=wuv_v, in_=wuv_bf.rearrange("(k p) f -> p k f", p=128)))
            em.dma_on("p1w", sy.dma_start(out=wkr_v, in_=wkr_bf.rearrange("(k p) f -> p k f", p=128)))
        block.sync(p1_wload)
        t_p1w = em.tok("p1w")

        tok_ln1 = [None, None]
        tok_lnuse = [None, None]
        tok_ckv_war = None
        tok_pb2_rd = None
        tok_pb3_rd = None
        tok_pb4_rd = None
        tok_pb5_rd = None
        tok_sq_rd = None
        tok_ckv_rd = None

        p0_st = []
        P1R = c.NS // 128

        def emit_P1_tile(it):
            nonlocal tok_ckv_war, tok_pb2_rd, tok_pb3_rd, tok_pb4_rd, tok_pb5_rd
            nonlocal tok_sq_rd, tok_ckv_rd
            sl = it % 2
            ts0 = it * c.NS

            def b_ld(sy, sl=sl, ts0=ts0, it=it):
                em.wait("qd", sy, tok_lnuse[sl], p0_st[it * P1R + P1R - 1],
                        p0_st[it * P1R + P1R - 2] if P1R >= 2 else None)
                tok_ln1[sl] = em.dma_on(f"ql{sl}", sy.dma_start(
                    out=lnt_v[sl], in_=lnx_sv[:, :, ts0:ts0 + c.NS]))
            block.sync(b_ld)

            def b_ckv_mm(t, sl=sl):
                em.wait("t", t, tok_ln1[sl], t_p1w, tok_ckv_war)
                for o in range(2):
                    for kc in range(c.KC):
                        ins = t.matmul(pb[o][:, 0:c.NS], wdkv_v[:, kc, o, :],
                                       lnt_v[sl][:, kc, :],
                                       start=(kc == 0), stop=(kc == c.KC - 1))
                em.do("t", ins)
            block.tensor(b_ckv_mm)
            t_ckvmm = em.tok("t")

            def b_ckv_ev(v):
                em.pre("v", v, t_ckvmm, tok_ckv_rd)
                em.do("v", v.tensor_copy(ckv_v[:, 0, :], pb[0][:, 0:c.NS]))
                em.pre("v", v)
                em.do("v", v.tensor_copy(ckv_v[:, 1, :], pb[1][:, 0:c.NS]))
            block.vector(b_ckv_ev)
            tok_ckv_war = em.tok("v")
            t_ckv = em.tok("v")

            for oc in range(c.NHL // 2):
                def b_kc_mm(t, oc=oc):
                    em.wait("t", t, t_ckv, tok_pb2_rd)
                    for kc in range(2):
                        ins = t.matmul(pb[2][:, 0:c.NS], wuk_v[:, kc, oc * 128:(oc + 1) * 128],
                                       ckv_v[:, kc, :], start=(kc == 0), stop=(kc == 1))
                    em.do("t", ins)
                block.tensor(b_kc_mm)
                t_kcm = em.tok("t")

                def b_kc_ev(v, oc=oc, ts0=ts0):
                    em.pre("v", v, t_kcm)
                    em.do("v", v.tensor_copy(Kh[2 * oc][0:64, ts0:ts0 + c.NS],
                                             pb[2][0:64, 0:c.NS]))
                    em.pre("v", v)
                    em.do("v", v.tensor_copy(Kh[2 * oc + 1][0:64, ts0:ts0 + c.NS],
                                             pb[2][64:128, 0:c.NS]))
                block.vector(b_kc_ev)
                tok_pb2_rd = em.tok("v")

            def b_kr_mm(t, sl=sl):
                em.wait("t", t, tok_ln1[sl], tok_pb4_rd)
                for kc in range(c.KC):
                    ins = t.matmul(pb[4][0:32, 0:c.NS], wkr_v[:, kc, :], lnt_v[sl][:, kc, :],
                                   start=(kc == 0), stop=(kc == c.KC - 1))
                tok_lnuse[sl] = em.do("t", ins)
            block.tensor(b_kr_mm)
            t_krm = em.tok("t")

            def b_kr_ev(v, ts0=ts0):
                em.pre("v", v, t_krm, t_scp)
                em.do("v", v.tensor_tensor(krt[:], pb[4][0:32, 0:c.NS],
                                           scale_t[:, ts0:ts0 + c.NS], ALU.mult))
                for h in range(c.NHL):
                    em.pre("v", v)
                    em.do("v", v.tensor_copy(Kh[h][64:96, ts0:ts0 + c.NS], krt[:]))
            block.vector(b_kr_ev)
            tok_pb4_rd = em.tok("v")
            t_krope = em.tok("v")

            for h in range(c.NHL):
                def b_sq(s, h=h, ts0=ts0):
                    em.pre("s", s, t_krope, tok_sq_rd)
                    em.do("s", s.activation(sqb[:], Kh[h][:, ts0:ts0 + c.NS], AF.Square))
                block.scalar(b_sq)
                t_sq = em.tok("s")

                def b_kn_mm(t, h=h):
                    em.wait("t", t, t_sq, tok_pb3_rd if h == 0 else None)
                    em.do("t", t.matmul(pb[3][0:c.NHL, 0:c.NS],
                                        hkh[:, h * c.NHL:(h + 1) * c.NHL], sqb[:],
                                        start=(h == 0), stop=(h == c.NHL - 1)))
                block.tensor(b_kn_mm)
                tok_sq_rd = em.tok("t")
            t_knmm = em.tok("t")

            def b_kn_max(v):
                em.pre("v", v, t_knmm, t_vini)
                em.do("v", v.reduce_max(knm[:], pb[3][0:c.NHL, 0:c.NS], axis=AX.X))
                em.pre("v", v)
                em.do("v", v.tensor_tensor(maxk[:], maxk[:], knm[:], ALU.max))
            block.vector(b_kn_max)
            tok_pb3_rd = em.tok("v")

            for sub in range(c.NS // 128):
                tc = it * (c.NS // 128) + sub

                def b_v_mm(t, sub=sub):
                    em.wait("t", t, t_ckv, t_vones, tok_pb5_rd)
                    for kc in range(2):
                        ins = t.matmul(pb[5][:, 0:c.YD],
                                       ckv_v[:, kc, sub * 128:(sub + 1) * 128],
                                       wuv_v[:, kc, :], start=(kc == 0), stop=(kc == 1))
                    em.do("t", ins)
                block.tensor(b_v_mm)
                tok_ckv_rd = em.tok("t")
                t_vm = em.tok("t")

                def b_v_ev(v, tc=tc):
                    em.pre("v", v, t_vm)
                    em.do("v", v.tensor_copy(
                        Vp_v[:, tc, :, 0:64],
                        pb[5][:, 0:c.YD].rearrange("p (h d) -> p h d", d=64)))
                block.vector(b_v_ev)
                tok_pb5_rd = em.tok("v")

        p0_es = ExitStack()
        sb0 = lambda n, sh, dt: p0_es.enter_context(nc.sbuf_tensor(n, sh, dt))
        xb = [sb0(f"p0x{i}", [128, c.DM], F32) for i in range(2)]
        lnb = [sb0(f"p0ln{i}", [128, c.DM], BF16) for i in range(2)]
        fmb = [sb0(f"p0fm{i}", [128, c.DM], BF16) for i in range(2)]
        nst = max(1, c.DM // 512)
        wst = c.DM // nst
        st6 = sb0("p0st6", [128, 6 * nst], F32)
        mv2 = [sb0(f"p0mv{i}", [128, 2], F32) for i in range(2)]
        sdv = [sb0(f"p0sd{i}", [128, 3], F32) for i in range(2)]

        tok_xuse = [None, None]
        tok_x = [None, None]
        tok_lnwar = [None, None]
        tok_fmwar = [None, None]
        tok_ptp_ev = None
        t_sd2 = [None, None]

        def p0_front(i):
            sl = i % 2

            def b_load(sy, i=i, sl=sl):
                em.wait("qd", sy, tok_xuse[sl])
                tok_x[sl] = em.dma_on(f"qx{sl}", sy.dma_start(
                    out=xb[sl][:], in_=x_seq[i * 128:(i + 1) * 128, :]))
            block.sync(b_load)

            def b_stats(v, sl=sl):
                em.pre("v", v, tok_x[sl])
                for j in range(nst):
                    em.pre("v", v)
                    em.do("v", v.bn_stats(st6[:, 6 * j:6 * j + 6],
                                          xb[sl][:, j * wst:(j + 1) * wst]))
                em.pre("v", v)
                em.do("v", v.bn_aggr(mv2[sl][:], st6[:]))
                em.pre("v", v)
                em.do("v", v.tensor_scalar(sdv[sl][:, 0:1], mv2[sl][:, 1:2], EPS,
                                           None, ALU.add))
            block.vector(b_stats)
            t_veps = em.tok("v")

            def b_sd(s, sl=sl):
                em.pre("s", s, t_veps)
                em.do("s", s.activation(sdv[sl][:, 0:1], sdv[sl][:, 0:1], AF.Sqrt))
            block.scalar(b_sd)
            t_sd2[sl] = em.tok("s")

        def p0_back(i):
            nonlocal tok_ptp_ev
            sl = i % 2

            def b_rstd(v, sl=sl):
                em.pre("v", v, t_sd2[sl])
                em.do("v", v.reciprocal(sdv[sl][:, 1:2], sdv[sl][:, 0:1]))
                em.pre("v", v)
                em.do("v", v.tensor_scalar(sdv[sl][:, 2:3], mv2[sl][:, 0:1],
                                           sdv[sl][:, 1:2], -1.0, ALU.mult, ALU.mult))
            block.vector(b_rstd)
            t_rstd = em.tok("v")

            def b_apply(v, sl=sl):
                em.pre("v", v, t_rstd, tok_lnwar[sl])
                tok_xuse[sl] = em.do("v", v.tensor_scalar(
                    lnb[sl][:], xb[sl][:], sdv[sl][:, 1:2], sdv[sl][:, 2:3],
                    ALU.mult, ALU.add))
            block.vector(b_apply)
            t_ln = em.tok("v")

            def b_tp(t, sl=sl):
                em.wait("t", t, t_ln, t_vini, tok_ptp_ev)
                for cc_ in range(c.KC):
                    ins = t.transpose(ptp[:, cc_ * 128:(cc_ + 1) * 128],
                                      lnb[sl][:, cc_ * 128:(cc_ + 1) * 128], ident[:])
                tok_lnwar[sl] = em.do("t", ins)
            block.tensor(b_tp)
            t_tp = em.tok("t")

            def b_evac(v, sl=sl):
                em.pre("v", v, t_tp, tok_fmwar[sl])
                em.do("v", v.tensor_copy(fmb[sl][:], ptp[:, 0:c.DM]))
            block.vector(b_evac)
            tok_ptp_ev = em.tok("v")
            t_ev = em.tok("v")

            def b_store(sy, i=i, sl=sl):
                em.wait("qd", sy, t_ev)
                tok_fmwar[sl] = em.dma_on(f"qf{sl}", sy.dma_start(
                    out=lnx_sv[:, :, i * 128:(i + 1) * 128],
                    in_=fmb[sl][:].rearrange("p (c t) -> p c t", c=c.KC)))
            block.sync(b_store)
            p0_st.append(tok_fmwar[sl])
            emit_p3p(1, cap=p3p_fin)
            if (i + 1) % P1R == 0:
                emit_P1_tile((i + 1) // P1R - 1)

        for i in range(c.T // 128):
            p0_front(i)
            if i > 0:
                p0_back(i - 1)
        p0_back(c.T // 128 - 1)
        emit_p3p_flush()

        barrier(em, block)
        p0_es.close()
        p1_es.close()
        wn2_es.close()
        scale_es.close()



        p2a_es = ExitStack()
        sb2a = lambda n, sh, dt: p2a_es.enter_context(nc.sbuf_tensor(n, sh, dt))
        lnxf = sb2a("lnxf", [128, c.KC * c.T], BF16)
        lnxf_v = lnxf[:].rearrange("p (k t) -> p k t", k=c.KC)
        wdq_b = [sb2a(f"wdqb{i}", [128, c.KC * 128], BF16) for i in range(2)]
        wdq_bv = [x[:].rearrange("p (k f) -> p k f", k=c.KC) for x in wdq_b]
        cqt = [sb2a(f"cqt{i}", [128, c.NS], BF16) for i in range(2)]
        wnat3.append(sb2a("wnat3y", [128, QW], F32))

        def p2a_lnx(sy):
            em.dma("qd", sy.dma_start(out=lnxf_v, in_=lnx_sv))
        block.sync(p2a_lnx)
        t_lnxf = em.tok("qd")

        tok_wdq = [None, None]
        tok_wdq_use = [None, None]
        tok_cqt_st = [None] * 4
        tok_pbsc_rd = [None] * 4
        n2 = c.T // c.NS
        for oc in range(c.CQC):
            sl = oc % 2

            def b_w(sy, oc=oc, sl=sl):
                em.wait("qd", sy, tok_wdq_use[sl], t_wbl)
                tok_wdq[sl] = em.dma_on(f"gw{sl}", sy.dma_start(
                    out=wdq_bv[sl],
                    in_=wdq_bf[:, oc * 128:(oc + 1) * 128].rearrange("(k p) f -> p k f", p=128)))
            block.sync(b_w)

            for ns in range(n2):
                psl = ns % 4

                def b_mm(t, oc=oc, sl=sl, ns=ns, psl=psl):
                    em.wait("t", t, tok_wdq[sl], t_lnxf, tok_pbsc_rd[psl])
                    for kc in range(c.KC):
                        ins = t.matmul(pb[psl][:, 0:c.NS], wdq_bv[sl][:, kc, :],
                                       lnxf_v[:, kc, ns * c.NS:(ns + 1) * c.NS],
                                       start=(kc == 0), stop=(kc == c.KC - 1))
                    tk = em.do("t", ins)
                    if ns == n2 - 1:
                        tok_wdq_use[sl] = tk
                block.tensor(b_mm)
                t_mm = em.tok("t")

                def b_ev(v, psl=psl, csl=ns % 2):
                    em.pre("v", v, t_mm, tok_cqt_st[csl])
                    em.do("v", v.tensor_copy(cqt[csl][:], pb[psl][:, 0:c.NS]))
                block.vector(b_ev)
                tok_pbsc_rd[psl] = em.tok("v")
                t_e = em.tok("v")

                def b_st(sy, oc=oc, ns=ns, csl=ns % 2):
                    em.wait("qd", sy, t_e)
                    tok_cqt_st[csl] = em.dma_on(f"qc{csl}", sy.dma_start(
                        out=cq_sv[:, oc, ns * c.NS:(ns + 1) * c.NS], in_=cqt[csl][:]))
                block.sync(b_st)
            emit_p3p(3)
        emit_p3p(len(p3p_units))
        barrier(em, block)
        p2a_es.close()
        quant_es.close()

        p2b_es = ExitStack()
        sb2b = lambda n, sh, dt: p2b_es.enter_context(nc.sbuf_tensor(n, sh, dt))
        wuq_sb = sb2b("wuq_sb", [128, c.CQC * c.YD], BF16)
        wuq_v = wuq_sb[:].rearrange("p (k f) -> p k f", k=c.CQC)
        wqr_sb = sb2b("wqr_sb", [128, c.CQC * c.NHL * 32], BF16)
        wqr_v = wqr_sb[:].rearrange("p (k f) -> p k f", k=c.CQC)
        cq_t = sb2b("cq_t", [128, c.CQC * c.NS], BF16)
        cq_tv = cq_t[:].rearrange("p (k t) -> p k t", k=c.CQC)
        qoutb = [sb2b(f"qout{i}", [128, c.NS], BF16) for i in range(2)]
        scp4 = sb2b("scp4", [128, c.T], BF16)

        def p2b_dup(g):
            em.dma_on("scp", g.dma_start(out=scp4[0:32, :], in_=scp_c[:]))
            for b0 in (32, 64, 96):
                em.dma_on("scp", g.dma_start(out=scp4[b0:b0 + 32, :], in_=scp_c[:]))
        block.gpsimd(p2b_dup)

        def p2b_w(sy):
            em.wait("qd", sy, t_wbl)
            em.dma_on("p2bw", sy.dma_start(out=wuq_v, in_=wuq_bf.rearrange("(k p) f -> p k f", p=128)))
            em.dma_on("p2bw", sy.dma_start(out=wqr_v, in_=wqr_bf.rearrange("(k p) f -> p k f", p=128)))
        block.sync(p2b_w)
        t_p2bw = em.tok("p2bw")

        tok_cqld = None
        tok_qst = [None, None]
        tok_pbq_rd = [None, None]
        nqc = c.NHL // 2
        for ns in range(n2):
            def b_cq(sy, ns=ns):
                em.wait("qd", sy, tok_cqld)
                em.dma_on("ql2", sy.dma_start(out=cq_tv, in_=cq_sv[:, :, ns * c.NS:(ns + 1) * c.NS]))
            block.sync(b_cq)
            t_cq = em.tok("ql2")

            for j in range(nqc + c.QRC):
                psl = j % 2
                is_qr = j >= nqc
                oc = j - nqc if is_qr else j

                def b_mm(t, j=j, oc=oc, is_qr=is_qr, psl=psl):
                    em.wait("t", t, t_cq, t_p2bw, tok_pbq_rd[psl])
                    w_v = wqr_v if is_qr else wuq_v
                    for kc in range(c.CQC):
                        ins = t.matmul(pb[psl][:, 0:c.NS], w_v[:, kc, oc * 128:(oc + 1) * 128],
                                       cq_tv[:, kc, :], start=(kc == 0), stop=(kc == c.CQC - 1))
                    tk = em.do("t", ins)
                    if j == nqc + c.QRC - 1:
                        nonlocal tok_cqld
                        tok_cqld = tk
                block.tensor(b_mm)
                t_mm = em.tok("t")

                if is_qr:
                    def b_ev(v, psl=psl, ns=ns):
                        em.pre("v", v, t_mm, tok_qst[psl], em.tok("scp"))
                        em.do("v", v.tensor_tensor(qoutb[psl][:], pb[psl][:, 0:c.NS],
                                                   scp4[:, ns * c.NS:(ns + 1) * c.NS], ALU.mult))
                    block.vector(b_ev)
                    tok_pbq_rd[psl] = em.tok("v")
                    t_e = em.tok("v")
                else:
                    def b_ev(s, psl=psl):
                        em.pre("s", s, t_mm, tok_qst[psl])
                        em.do("s", s.activation(qoutb[psl][:], pb[psl][:, 0:c.NS],
                                                AF.Identity, scale=RS96))
                    block.scalar(b_ev)
                    tok_pbq_rd[psl] = em.tok("s")
                    t_e = em.tok("s")

                def b_st(sy, oc=oc, is_qr=is_qr, psl=psl, ns=ns):
                    em.wait("qd", sy, t_e)
                    tk = None
                    if is_qr:
                        for j in range(4):
                            tk = em.dma_on(f"qs{psl}", sy.dma_start(
                                out=qs_s[(4 * oc + j) * 128 + 64:(4 * oc + j) * 128 + 96,
                                         ns * c.NS:(ns + 1) * c.NS],
                                in_=qoutb[psl][j * 32:(j + 1) * 32, :]))
                    else:
                        for j in range(2):
                            tk = em.dma_on(f"qs{psl}", sy.dma_start(
                                out=qs_s[(2 * oc + j) * 128:(2 * oc + j) * 128 + 64,
                                         ns * c.NS:(ns + 1) * c.NS],
                                in_=qoutb[psl][j * 64:(j + 1) * 64, :]))
                    tok_qst[psl] = tk
                block.sync(b_st)
        barrier(em, block)
        p2b_es.close()

        p2c_es = ExitStack()
        sb2c = lambda n, sh, dt: p2c_es.enter_context(nc.sbuf_tensor(n, sh, dt))
        qsl = sb2c("qsl", [128, c.NHL * c.SLOT], BF16)
        qh_v = qsl[:].rearrange("p (o t) -> p o t", o=c.NHL)
        sq2 = sb2c("sq2", [128, c.SLOT], BF16)
        qn = sb2c("qn", [c.NHL, c.SLOT], F32)
        qmx = sb2c("qmx", [c.NHL, 1], F32)
        ngc = sb2c("ngc", [c.NHL, 1], F32)
        ngb = sb2c("ngb", [c.NHL, 1], BF16)
        bias_sb = sb2c("bias_sb", [128, c.NHL], F32)
        expA = sb2c("expA", [128, c.NKT * c.SLOT], BF16)
        expA_v = expA[:].rearrange("p (k t) -> p k t", k=c.NKT)
        yun_sb = [sb2c(f"yun{i}", [128, c.SLOT], BF16) for i in range(c.YC)]
        den_f = sb2c("den_f", [1, c.NHL * c.SLOT], F32)
        y_sb = [sb2c(f"ysb{i}", [128, c.SLOT], BF16) for i in range(c.YC)]
        wop_sb = sb2c("wop_sb", [128, c.YC * c.DM], BF16)
        wop_v = wop_sb[:].rearrange("p (k f) -> p k f", k=c.YC)
        aop_t = [sb2c(f"aopt{i}", [128, c.SLOT], F32) for i in range(2)]
        aop_pv = [x.rearrange("h (o p) t -> p h o t", p=128) for x in aop_p]
        wo_done = []

        def p2c_wo(sy):
            em.wait("qd", sy, t_wbl)
            em.dma_on("wop", sy.dma_start(out=wop_v, in_=wo_bf.rearrange("(k p) f -> p k f", p=128)))
        block.sync(p2c_wo)
        t_wop = em.tok("wop")

        def p2c_zero(v):
            em.pre("v", v)
            em.do("v", v.memset(qsl[:], 0.0))
        block.vector(p2c_zero)
        t_qrz = em.tok("v")

        tok_qsl_use = None
        tok_avprev = None
        tok_yun_rd = None
        pb01_rd = [None, None]
        tok_py_rd = [None, None]
        tok_pb3q_rd = None
        tok_pb4e_rd = None
        tok_ngb_rd = None
        tok_ywr = None
        tok_sq2_rd = None
        tok_aop_st = [None, None]
        for slot in range(c.NSLOT):
            q0 = slot * c.SLOT
            nkt = (q0 + c.SLOT) // 128

            def b_qld(sy, q0=q0):
                em.wait("qd", sy, tok_qsl_use, t_qrz)
                em.dma_on("qq", sy.dma_start(out=qh_v[0:96, :, :],
                                             in_=qs_sv[0:96, :, q0:q0 + c.SLOT]))
            block.sync(b_qld)
            t_qc_ld = em.tok("qq")
            t_qr_ld = t_qc_ld

            for j in range(c.NHL):
                def b_sqq(v, j=j):
                    em.pre("v", v, t_qc_ld, tok_sq2_rd)
                    em.do("v", v.tensor_tensor(sq2[:], qh_v[:, j, :], qh_v[:, j, :], ALU.mult))
                block.vector(b_sqq)
                t_s1 = em.tok("v")

                def b_nm(t, j=j):
                    em.wait("t", t, t_s1, tok_pb3q_rd if j == 0 else None)
                    em.do("t", t.matmul(pb[3][0:c.NHL, 0:c.SLOT],
                                        hkh[:, j * c.NHL:(j + 1) * c.NHL], sq2[:],
                                        start=(j == 0), stop=(j == c.NHL - 1)))
                block.tensor(b_nm)
                tok_sq2_rd = em.tok("t")
            t_qn_mm = em.tok("t")

            def b_ngc(v):
                em.pre("v", v, t_qn_mm, tok_ngb_rd)
                em.do("v", v.reduce_max(qmx[:], pb[3][0:c.NHL, 0:c.SLOT], axis=AX.X))
                em.pre("v", v)
                em.do("v", v.tensor_scalar(ngc[:], qmx[:], maxk[:], None, ALU.mult))
            block.vector(b_ngc)
            tok_pb3q_rd = em.tok("v")
            t_c2 = em.tok("v")

            def b_csq(s):
                em.pre("s", s, t_c2)
                em.do("s", s.activation(ngc[:], ngc[:], AF.Sqrt))
            block.scalar(b_csq)
            t_cs = em.tok("s")

            def b_ng2(v):
                em.pre("v", v, t_cs)
                em.do("v", v.tensor_scalar(ngc[:], ngc[:], -1.0, None, ALU.mult))
                em.pre("v", v)
                em.do("v", v.tensor_copy(ngb[:], ngc[:]))
            block.vector(b_ng2)
            t_ngb = em.tok("v")

            def b_ext(t):
                em.wait("t", t, t_ngb, tok_pb4e_rd)
                for h in range(c.NHL):
                    ins = t.matmul(pb[4][:, h:h + 1], ehot_v[:, h, :], ngb[:],
                                   start=True, stop=True)
                em.do("t", ins)
            block.tensor(b_ext)
            tok_ngb_rd = em.tok("t")
            t_ext = em.tok("t")

            def b_bias(v):
                em.pre("v", v, t_ext)
                em.do("v", v.tensor_copy(bias_sb[:, 0:c.NHL], pb[4][:, 0:c.NHL]))
            block.vector(b_bias)
            tok_pb4e_rd = em.tok("v")
            t_bias = em.tok("v")

            offs = [max(0, kt * 128 - q0) for kt in range(nkt)]
            for h in range(c.NHL):
                pby = pb[5] if h % 2 == 0 else pb[2]
                ybk = h % 2
                sbase = em.cnt["s"]

                def b_sc_all(t, h=h, nkt=nkt, sbase=sbase):
                    em.wait("t", t, t_qc_ld, t_bias, ("s", sbase),
                            pb01_rd[0], pb01_rd[1], tok_pb4e_rd)
                    for kt in range(nkt):
                        off = offs[kt]
                        psl = (0, 1, 4)[kt % 3]
                        if kt >= 3:
                            em.wait("t", t, ("s", sbase + kt - 2))
                        ins = t.matmul(pb[psl][:, off:c.SLOT],
                                       Kh[h][0:96, kt * 128:(kt + 1) * 128],
                                       qh_v[0:96, h, off:c.SLOT], start=True, stop=True)
                        em.do("t", ins)
                block.tensor(b_sc_all)
                tbase = em.cnt["t"] - nkt

                def b_exp_all(s, h=h, nkt=nkt, tbase=tbase):
                    em.pre("s", s, tok_avprev)
                    for kt in range(nkt):
                        off = offs[kt]
                        em.wait("s", s, ("t", tbase + kt + 1))
                        em.do("s", s.activation(expA_v[:, kt, off:c.SLOT],
                                                pb[(0, 1, 4)[kt % 3]][:, off:c.SLOT],
                                                AF.Exp, bias=bias_sb[:, h:h + 1],
                                                scale=1.0))
                block.scalar(b_exp_all)
                assert em.cnt["s"] == sbase + nkt

                ndiag = c.SLOT // 128
                vbase = em.cnt["v"]

                def b_msk_all(v, nkt=nkt, sbase=sbase, ndiag=ndiag):
                    for i_, kt in enumerate(range(nkt - ndiag, nkt)):
                        off = offs[kt]
                        em.pre("v", v, ("s", sbase + kt + 1))
                        em.do("v", v.tensor_tensor(expA_v[:, kt, off:off + 128],
                                                   expA_v[:, kt, off:off + 128],
                                                   tri[:], ALU.mult))
                block.vector(b_msk_all)
                assert em.cnt["v"] == vbase + ndiag

                def b_av_all(t, h=h, nkt=nkt, sbase=sbase, vbase=vbase, ndiag=ndiag,
                             pby=pby, ybk=ybk):
                    em.wait("t", t, tok_py_rd[ybk])
                    for kt in range(nkt):
                        off = offs[kt]
                        if kt >= nkt - ndiag:
                            em.wait("t", t, ("v", vbase + ndiag - (nkt - kt) + 1))
                        else:
                            em.wait("t", t, ("s", sbase + kt + 1))
                        ins = t.matmul(pby[0:65, off:c.SLOT], Vp_v[:, kt, h, :],
                                       expA_v[:, kt, off:c.SLOT],
                                       start=(kt == 0), stop=(kt == nkt - 1))
                    em.do("t", ins)
                block.tensor(b_av_all)
                tok_avprev = em.tok("t")
                t_y = em.tok("t")
                if h == c.NHL - 1:
                    tok_qsl_use = t_y

                def b_yev(v, h=h, ch=h // 2, b0=(h % 2) * 64, pby=pby, ybk=ybk):
                    em.pre("v", v, t_y, tok_yun_rd)
                    em.do("v", v.tensor_copy(yun_sb[ch][b0:b0 + 64, 0:c.SLOT],
                                             pby[0:64, 0:c.SLOT]))
                    em.pre("v", v)
                    em.do("v", v.tensor_copy(den_f[0:1, h * c.SLOT:(h + 1) * c.SLOT],
                                             pby[64:65, 0:c.SLOT]))
                block.vector(b_yev)
                tok_py_rd[ybk] = em.tok("v")
            t_yun = em.tok("v")

            t_rcp = t_yun

            def b_bc_all(t):
                em.wait("t", t, t_rcp, pb01_rd[0], pb01_rd[1], tok_pb3q_rd, tok_pb4e_rd)
                for h in range(c.NHL):
                    bank = (pb[0], pb[1], pb[3], pb[4])[h // 2]
                    ins = t.matmul(bank[(h % 2) * 64:(h % 2) * 64 + 64, 0:c.SLOT],
                                   ones_f[0:1, 0:64],
                                   den_f[0:1, h * c.SLOT:(h + 1) * c.SLOT],
                                   start=True, stop=True)
                em.do("t", ins)
            block.tensor(b_bc_all)
            t_bc2 = em.tok("t")

            def b_recipb(v):
                em.pre("v", v, t_bc2)
                for bi in range(c.NHL // 2):
                    bank = (pb[0], pb[1], pb[3], pb[4])[bi]
                    rows = 128
                    em.pre("v", v)
                    em.do("v", v.reciprocal(bank[0:rows, 0:c.SLOT],
                                            bank[0:rows, 0:c.SLOT]))
            block.vector(b_recipb)
            t_bcr = em.tok("v")

            def b_norm(v):
                em.pre("v", v, t_bcr, tok_ywr)
                for h in range(c.NHL):
                    bank = (pb[0], pb[1], pb[3], pb[4])[h // 2]
                    em.pre("v", v)
                    em.do("v", v.tensor_tensor(
                        y_sb[h // 2][(h % 2) * 64:(h % 2) * 64 + 64, 0:c.SLOT],
                        yun_sb[h // 2][(h % 2) * 64:(h % 2) * 64 + 64, 0:c.SLOT],
                        bank[(h % 2) * 64:(h % 2) * 64 + 64, 0:c.SLOT], ALU.mult))
            block.vector(b_norm)
            tok_yun_rd = em.tok("v")
            tok_pb3q_rd = em.tok("v")
            tok_pb4e_rd = em.tok("v")
            pb01_rd[0] = em.tok("v")
            pb01_rd[1] = em.tok("v")
            t_yall = em.tok("v")

            for oc in range(c.KC):
                psl2 = oc % 2

                def b_wo_mm(t, oc=oc, psl2=psl2):
                    em.wait("t", t, t_yall, t_wop, pb01_rd[psl2])
                    for kc in range(c.YC):
                        ins = t.matmul(pb[psl2][:, 0:c.SLOT],
                                       wop_v[:, kc, oc * 128:(oc + 1) * 128],
                                       y_sb[kc][:, 0:c.SLOT],
                                       start=(kc == 0), stop=(kc == c.YC - 1))
                    tk = em.do("t", ins)
                    if oc == c.KC - 1:
                        nonlocal tok_ywr
                        tok_ywr = tk
                block.tensor(b_wo_mm)
                t_wm = em.tok("t")

                def b_wo_ev(v, oc=oc, psl2=psl2):
                    em.pre("v", v, t_wm, tok_aop_st[psl2])
                    em.do("v", v.tensor_copy(aop_t[psl2][:], pb[psl2][:, 0:c.SLOT]))
                block.vector(b_wo_ev)
                pb01_rd[psl2] = em.tok("v")
                t_ae = em.tok("v")

                def b_wo_st(sy, q0=q0, oc=oc, psl2=psl2):
                    em.wait("qd", sy, t_ae)
                    th = q0 // c.OWN
                    jsl = (q0 % c.OWN) // c.SLOT
                    tok_aop_st[psl2] = em.dma_on(f"qa{psl2}", sy.dma_start(
                        out=aop_pv[jsl][:, th, oc, :], in_=aop_t[psl2][:]))
                block.sync(b_wo_st)

            wo_done.append((tok_aop_st[0], tok_aop_st[1]))
            s_idx = q0 // c.SLOT
            if s_idx >= NRS:
                jrs = s_idx - NRS

                def b_ccj(g, jrs=jrs):
                    em.wait("g", g, wo_done[jrs][0], wo_done[jrs][1],
                            wo_done[NRS + jrs][0], wo_done[NRS + jrs][1])
                    em.do("cc", g.collective_compute(
                        "ReduceScatter", mybir.AluOpType.add, groups,
                        [aop_p[jrs].opt()], [aog_p[jrs].opt()]), 1)
                block.gpsimd(b_ccj)

        barrier(em, block)
        p2c_es.close()
        att_es.close()


        p3_es = ExitStack()
        sb3 = lambda n, sh, dt, **kw: p3_es.enter_context(nc.sbuf_tensor(n, sh, dt, **kw))
        SUBS = c.MEGA // 128
        KC2 = c.KC // 2
        RW = KC2 * 128
        NW = c.INNER // c.NS
        NON = c.DM // c.NS
        assert NW >= 3 * SUBS + 1 and RW <= 512 and c.DM == 2 * RW
        aofm = sb3("aofm", [128, KC2 * 128], F32)
        aofm_v = aofm[:].rearrange("p (k t) -> p k t", k=KC2)
        xt = [sb3(f"xt{i}", [128, c.DM], F32) for i in range(2)]
        res1b = [sb3(f"res1{i}", [128, SUBS * c.DM], F32) for i in range(2)]
        res1v = [x[:].rearrange("p (u f) -> p u f", u=SUBS) for x in res1b]
        ln2x = sb3("ln2x", [128, c.DM], F32)
        qi_b = sb3("qi_b", [128, c.DM], I32)
        qtm_b = sb3("qtm_b", [128, c.DM], BF16)
        zf_b = sb3("zf_b", [128, c.DM], BF16)
        xq_fm = [sb3(f"xq_fm{i}", [128, c.KC * c.MEGA], BF16) for i in range(2)]
        xq_fv = [x[:].rearrange("p (k t) -> p k t", k=c.KC) for x in xq_fm]
        alg = sb3("alg", [128, 2 * SUBS], F32)
        alu_ = sb3("alu", [128, 2 * SUBS], F32)
        gam = sb3("gam", [128, SUBS], F32)
        tmp1 = sb3("tmp1", [128, 4], F32)
        st6b = sb3("st6b", [128, 6 * nst], F32)
        mv2b = sb3("mv2b", [128, 2], F32)
        sdvb = sb3("sdvb", [128, 3], F32)
        gsl = [sb3(f"gsl{i}", [128, c.NS], F32) for i in range(2)]
        wgsl = [sb3(f"wgsl{i}", [128, c.KC * c.NS], BF16) for i in range(2)]
        wgsl_v = [x[:].rearrange("p (k t) -> p k t", k=c.KC) for x in wgsl]
        wusl = [sb3(f"wusl{i}", [128, c.KC * c.NS], BF16) for i in range(2)]
        wusl_v = [x[:].rearrange("p (k t) -> p k t", k=c.KC) for x in wusl]
        hq_fm = sb3("hq_fm", [128, c.IC * c.MEGA], BF16)
        hq_fv = hq_fm[:].rearrange("p (k t) -> p k t", k=c.IC)
        wdsl = [sb3(f"wdsl{i}", [128, c.IC * c.NS], BF16) for i in range(2)]
        wdsl_v = [x[:].rearrange("p (k t) -> p k t", k=c.IC) for x in wdsl]
        hh = sb3("hh", [128, SUBS * c.INNER], F32, side="right")
        hh_v = hh[:].rearrange("p (u f) -> p u f", u=SUBS)

        wgt_sv = wgt_s.rearrange("(k p) t -> p k t", p=128)
        wut_sv = wut_s.rearrange("(k p) t -> p k t", p=128)
        wdt_sv = wdt_s.rearrange("(k p) t -> p k t", p=128)
        aog_pv = [x.rearrange("(o p) t -> p o t", p=128) for x in aog_p]

        st = {
            "yg": None, "yg_use": None,
            "xt": [None, None], "xt_use": [None, None],
            "res1_rd": [None, None],
            "ptp2_rd": None, "ptp_rd": None,
            "xqi_rd": None, "zfb_rd": None, "qtm_rd": None,
            "xq_rd": [None, None], "xq_rdy": [None, None],
            "a_qtm": None, "l2b": None,
            "hq_rdy": [None] * SUBS, "hqfm_rd": None,
            "wgu_tok": [None, None], "wgu_use": [None, None], "wg_next": 0,
            "wd_tok": [None, None], "wd_use": [None, None], "wd_next": 0,
            "wd_last": None,
            "hh_rd": None, "hh_done": None,
            "gsl_rd": [None, None], "outb": [None, None],
        }
        pbrd = {i: None for i in range(6)}

        def emit_A_load(mg, sub):
            t0 = mg * c.MEGA + sub * 128
            sl = sub % 2

            def b_ao(sy, t0=t0):
                em.wait("qd", sy, st["yg_use"], ("cc", t0 // c.SLOT + 1))
                st["yg"] = em.dma_on("qao", sy.dma_start(
                    out=aofm_v,
                    in_=aog_pv[t0 // c.SLOT][:, 0:KC2, t0 % c.SLOT:t0 % c.SLOT + 128]))
            block.sync(b_ao)

            def b_xl(sy, t0=t0, sl=sl):
                em.wait("qd", sy, st["xt_use"][sl])
                st["xt"][sl] = em.dma_on("qxo", sy.dma_start(
                    out=xt[sl][:], in_=x_own[t0:t0 + 128, :]))
            block.sync(b_xl)

        def emit_A_main1(mg, sub):
            par = mg % 2
            sl = sub % 2
            for r in range(2):
                if r == 1:
                    def b_ao1(sy, mg=mg, sub=sub):
                        em.wait("qd", sy, st["yg_use"])
                        t0 = mg * c.MEGA + sub * 128
                        st["yg"] = em.dma_on("qao", sy.dma_start(
                            out=aofm_v,
                            in_=aog_pv[t0 // c.SLOT][:, KC2:c.KC,
                                       t0 % c.SLOT:t0 % c.SLOT + 128]))
                    block.sync(b_ao1)

                def b_aotp(t):
                    em.wait("t", t, st["yg"], pbrd[4])
                    for oc in range(KC2):
                        ins = t.transpose(pb[4][:, oc * 128:(oc + 1) * 128],
                                          aofm_v[:, oc, :], ident_f[:])
                    st["yg_use"] = em.do("t", ins)
                block.tensor(b_aotp)
                t_aotp = em.tok("t")

                def b_res1(v, r=r, sub=sub, sl=sl, par=par):
                    em.pre("v", v, t_aotp, st["xt"][sl], st["res1_rd"][par])
                    em.do("v", v.tensor_tensor(
                        res1v[par][:, sub, r * RW:(r + 1) * RW],
                        pb[4][:, 0:RW], xt[sl][:, r * RW:(r + 1) * RW], ALU.add))
                block.vector(b_res1)
                pbrd[4] = em.tok("v")
            st["xt_use"][sl] = em.tok("v")
            t_r1 = em.tok("v")

            def b_ln2a(v, sub=sub, par=par):
                em.pre("v", v, t_r1)
                for j in range(nst):
                    em.pre("v", v)
                    em.do("v", v.bn_stats(st6b[:, 6 * j:6 * j + 6],
                                          res1v[par][:, sub, j * wst:(j + 1) * wst]))
                em.pre("v", v)
                em.do("v", v.bn_aggr(mv2b[:], st6b[:]))
                em.pre("v", v)
                em.do("v", v.tensor_scalar(sdvb[:, 0:1], mv2b[:, 1:2], EPS, None, ALU.add))
            block.vector(b_ln2a)
            t_l2a = em.tok("v")

            def b_ln2b(s):
                em.pre("s", s, t_l2a)
                em.do("s", s.activation(sdvb[:, 0:1], sdvb[:, 0:1], AF.Sqrt))
            block.scalar(b_ln2b)
            st["l2b"] = em.tok("s")

        def emit_A_main2(mg, sub):
            par = mg % 2
            t_l2b = st["l2b"]
            cidx = par * SUBS + sub

            def b_aq(v, sub=sub, par=par, cidx=cidx):
                em.pre("v", v, t_l2b)
                em.do("v", v.reciprocal(sdvb[:, 1:2], sdvb[:, 0:1]))
                em.pre("v", v)
                em.do("v", v.tensor_scalar(sdvb[:, 2:3], mv2b[:, 0:1], sdvb[:, 1:2],
                                           -1.0, ALU.mult, ALU.mult))
                em.pre("v", v)
                em.do("v", v.tensor_scalar(ln2x[:], res1v[par][:, sub, :],
                                           sdvb[:, 1:2], sdvb[:, 2:3], ALU.mult, ALU.add))
                em.pre("v", v)
                em.do("v", v.reduce_max(tmp1[:, 0:1], ln2x[:], axis=AX.X,
                                        apply_absolute_value=True))
                em.pre("v", v)
                em.do("v", v.tensor_scalar(tmp1[:, 1:2], tmp1[:, 0:1], 1e-5, None, ALU.max))
                em.pre("v", v)
                em.do("v", v.reciprocal(tmp1[:, 2:3], tmp1[:, 1:2]))
                em.pre("v", v)
                em.do("v", v.tensor_scalar(tmp1[:, 3:4], tmp1[:, 2:3], 127.0, None, ALU.mult))
                em.pre("v", v)
                em.do("v", v.tensor_scalar(alg[:, cidx:cidx + 1], tmp1[:, 1:2],
                                           mv_m[:, 0:1], 1.0 / 127.0, ALU.mult, ALU.mult))
                em.pre("v", v)
                em.do("v", v.tensor_scalar(alu_[:, cidx:cidx + 1], tmp1[:, 1:2],
                                           mv_m[:, 1:2], 1.0 / 127.0, ALU.mult, ALU.mult))
                em.pre("v", v, st["xqi_rd"])
                em.do("v", v.tensor_scalar(qi_b[:], ln2x[:], tmp1[:, 3:4], None, ALU.mult))
                em.pre("v", v, st["zfb_rd"])
                em.do("v", v.tensor_copy(zf_b[:], qi_b[:]))
                em.pre("v", v)
                em.do("v", v.scalar_tensor_tensor(ln2x[:], ln2x[:], tmp1[:, 3:4],
                                                  zf_b[:], ALU.mult, ALU.subtract))
                em.pre("v", v, st["qtm_rd"])
                em.do("v", v.tensor_scalar(qtm_b[:], ln2x[:], 0.5, None, ALU.is_gt))
                em.pre("v", v)
                em.do("v", v.tensor_scalar(ln2x[:], ln2x[:], -0.5, None, ALU.is_lt))
                em.pre("v", v)
                em.do("v", v.tensor_tensor(zf_b[:], zf_b[:], qtm_b[:], ALU.add))
                em.pre("v", v)
                em.do("v", v.tensor_tensor(qtm_b[:], zf_b[:], ln2x[:], ALU.subtract))
            block.vector(b_aq)
            st["xqi_rd"] = em.tok("v")
            st["zfb_rd"] = em.tok("v")
            st["a_qtm"] = em.tok("v")

        def emit_A_tp(mg, sub):
            par = mg % 2

            def b_xqtp(t):
                em.wait("t", t, st["a_qtm"], st["ptp2_rd"])
                for oc in range(c.KC):
                    ins = t.transpose(ptp2[:, oc * 128:(oc + 1) * 128],
                                      qtm_b[:, oc * 128:(oc + 1) * 128], ident[:])
                em.do("t", ins)
            block.tensor(b_xqtp)
            st["qtm_rd"] = em.tok("t")
            t_xtp = em.tok("t")

            def b_xqev(v, sub=sub, par=par):
                em.pre("v", v, t_xtp, st["xq_rd"][par])
                em.do("v", v.tensor_copy(
                    xq_fv[par][:, :, sub * 128:(sub + 1) * 128],
                    ptp2[:, 0:c.DM].rearrange("p (k t) -> p k t", k=c.KC)))
            block.vector(b_xqev)
            st["ptp2_rd"] = em.tok("v")

        def emit_wgu_load():
            g = st["wg_next"]
            if g >= c.NMEGA * NW:
                return
            st["wg_next"] = g + 1
            isl = g % NW
            sl = g % 2

            def b_wgu(sy, isl=isl, sl=sl):
                em.wait("qd", sy, st["wgu_use"][sl])
                tg = em.dma_on(f"qg{sl}", sy.dma_start(
                    out=wgsl_v[sl], in_=wgt_sv[:, :, isl * c.NS:(isl + 1) * c.NS]))
                tu = em.dma_on(f"qg2{sl}", sy.dma_start(
                    out=wusl_v[sl], in_=wut_sv[:, :, isl * c.NS:(isl + 1) * c.NS]))
                st["wgu_tok"][sl] = (tg, tu)
            block.sync(b_wgu)

        def emit_wd_load():
            g = st["wd_next"]
            if g >= c.NMEGA * NON:
                return
            st["wd_next"] = g + 1
            on = g % NON
            sl = g % 2

            def b_wd(sy, on=on, sl=sl):
                em.wait("qd", sy, st["wd_use"][sl])
                st["wd_tok"][sl] = em.dma_on(f"qwd{sl}", sy.dma_start(
                    out=wdsl_v[sl], in_=wdt_sv[:, :, on * c.NS:(on + 1) * c.NS]))
            block.sync(b_wd)

        def emit_BC_isl(mg, isl):
            par = mg % 2
            wsl = (mg * NW + isl) % 2

            sfut = em.cnt["s"]
            vfut0 = em.cnt["v"]
            tmm = []
            for sub in range(SUBS):
                gb = 2 * (sub % 2)
                ub = gb + 1

                def b_gu_mm(t, sub=sub, gb=gb, ub=ub, par=par, wsl=wsl):
                    em.wait("t", t, st["wgu_tok"][wsl][0], st["wgu_tok"][wsl][1],
                            st["xq_rdy"][par])
                    if sub < 2:
                        em.wait("t", t, pbrd[gb], pbrd[ub])
                    else:
                        em.wait("t", t, ("s", sfut + (sub - 2) + 1),
                                ("v", vfut0 + (sub - 2) + 1))
                    for kc in range(c.KC):
                        t.matmul(pb[gb][:, 0:c.NS],
                                 xq_fv[par][:, kc, sub * 128:(sub + 1) * 128],
                                 wgsl_v[wsl][:, kc, :], start=(kc == 0),
                                 stop=(kc == c.KC - 1))
                    for kc in range(c.KC):
                        ins = t.matmul(pb[ub][:, 0:c.NS],
                                       xq_fv[par][:, kc, sub * 128:(sub + 1) * 128],
                                       wusl_v[wsl][:, kc, :], start=(kc == 0),
                                       stop=(kc == c.KC - 1))
                    tk = em.do("t", ins)
                    if sub == SUBS - 1:
                        st["wgu_use"][wsl] = tk
                    st["xq_rd"][par] = tk
                block.tensor(b_gu_mm)
                tmm.append(em.tok("t"))
            emit_wgu_load()

            def b_silu(s, par=par):
                for sub in range(SUBS):
                    gb = 2 * (sub % 2)
                    cidx = par * SUBS + sub
                    em.pre("s", s, tmm[sub], st["gsl_rd"][sub % 2],
                           st["outb"][sub % 2])
                    em.do("s", s.activation(gsl[sub % 2][:], pb[gb][:, 0:c.NS], AF.Silu,
                                            scale=alg[:, cidx:cidx + 1]))
            block.scalar(b_silu)
            assert em.cnt["s"] == sfut + SUBS
            for sub in range(SUBS):
                pbrd[2 * (sub % 2)] = ("s", sfut + sub + 1)

            def b_hh(v, isl=isl):
                for sub in range(SUBS):
                    ub = 2 * (sub % 2) + 1
                    em.pre("v", v, ("s", sfut + sub + 1), st["hh_rd"])
                    em.do("v", v.tensor_tensor(
                        hh_v[:, sub, isl * c.NS:(isl + 1) * c.NS], gsl[sub % 2][:],
                        pb[ub][:, 0:c.NS], ALU.mult))
            block.vector(b_hh)
            assert em.cnt["v"] == vfut0 + SUBS
            for sub in range(SUBS):
                pbrd[2 * (sub % 2) + 1] = ("v", vfut0 + sub + 1)
                st["gsl_rd"][sub % 2] = ("v", vfut0 + sub + 1)
            st["hh_done"] = ("v", vfut0 + SUBS)

        npc = max(1, c.INNER // c.DM)
        wpc = c.INNER // npc
        ntp = wpc // 128

        def emit_D_sub(mg, sub):
            par = mg % 2
            cidx = par * SUBS + sub

            def b_hq(v, sub=sub, cidx=cidx):
                em.pre("v", v, st["hh_done"])
                em.do("v", v.reduce_max(tmp1[:, 0:1], hh_v[:, sub, :], axis=AX.X,
                                        apply_absolute_value=True))
                em.pre("v", v)
                em.do("v", v.tensor_scalar(tmp1[:, 1:2], tmp1[:, 0:1],
                                           alu_[:, cidx:cidx + 1], None, ALU.mult))
                em.pre("v", v)
                em.do("v", v.tensor_scalar(tmp1[:, 1:2], tmp1[:, 1:2], 1e-5, None, ALU.max))
                em.pre("v", v)
                em.do("v", v.reciprocal(tmp1[:, 2:3], tmp1[:, 1:2]))
                em.pre("v", v)
                em.do("v", v.tensor_scalar(tmp1[:, 3:4], tmp1[:, 2:3],
                                           alu_[:, cidx:cidx + 1], 127.0, ALU.mult, ALU.mult))
                em.pre("v", v)
                em.do("v", v.tensor_scalar(gam[:, sub:sub + 1], tmp1[:, 1:2],
                                           mv_m[:, 2:3], 1.0 / 127.0, ALU.mult, ALU.mult))
            block.vector(b_hq)

            for pc in range(npc):
                def b_hp(v, sub=sub, pc=pc):
                    em.pre("v", v, st["xqi_rd"])
                    em.do("v", v.tensor_scalar(qi_b[:, 0:wpc],
                                               hh_v[:, sub, pc * wpc:(pc + 1) * wpc],
                                               tmp1[:, 3:4], None, ALU.mult))
                    em.pre("v", v, st["zfb_rd"])
                    em.do("v", v.tensor_copy(zf_b[:, 0:wpc], qi_b[:, 0:wpc]))
                    em.pre("v", v)
                    em.do("v", v.scalar_tensor_tensor(
                        hh_v[:, sub, pc * wpc:(pc + 1) * wpc],
                        hh_v[:, sub, pc * wpc:(pc + 1) * wpc], tmp1[:, 3:4],
                        zf_b[:, 0:wpc], ALU.mult, ALU.subtract))
                    em.pre("v", v, st["qtm_rd"])
                    em.do("v", v.tensor_scalar(qtm_b[:, 0:wpc],
                                               hh_v[:, sub, pc * wpc:(pc + 1) * wpc],
                                               0.5, None, ALU.is_gt))
                    em.pre("v", v)
                    em.do("v", v.tensor_scalar(hh_v[:, sub, pc * wpc:(pc + 1) * wpc],
                                               hh_v[:, sub, pc * wpc:(pc + 1) * wpc],
                                               -0.5, None, ALU.is_lt))
                    em.pre("v", v)
                    em.do("v", v.tensor_tensor(zf_b[:, 0:wpc], zf_b[:, 0:wpc],
                                               qtm_b[:, 0:wpc], ALU.add))
                    em.pre("v", v)
                    em.do("v", v.tensor_tensor(qtm_b[:, 0:wpc], zf_b[:, 0:wpc],
                                               hh_v[:, sub, pc * wpc:(pc + 1) * wpc],
                                               ALU.subtract))
                block.vector(b_hp)
                st["xqi_rd"] = em.tok("v")
                st["zfb_rd"] = em.tok("v")
                t_hqc = em.tok("v")

                def b_hqtp(t):
                    em.wait("t", t, t_hqc, st["ptp_rd"])
                    for j in range(ntp):
                        ins = t.transpose(ptp[:, j * 128:(j + 1) * 128],
                                          qtm_b[:, j * 128:(j + 1) * 128], ident[:])
                    em.do("t", ins)
                block.tensor(b_hqtp)
                st["qtm_rd"] = em.tok("t")
                t_htp = em.tok("t")

                def b_hqev(v, sub=sub, pc=pc):
                    em.pre("v", v, t_htp,
                           st["hqfm_rd"] if (sub == 0 and pc == 0) else None)
                    em.do("v", v.tensor_copy(
                        hq_fv[:, pc * ntp:(pc + 1) * ntp, sub * 128:(sub + 1) * 128],
                        ptp[:, 0:ntp * 128].rearrange("p (k t) -> p k t", k=ntp)))
                block.vector(b_hqev)
                st["ptp_rd"] = em.tok("v")
            st["hq_rdy"][sub] = em.tok("v")
            if sub == SUBS - 1:
                st["hh_rd"] = em.tok("v")

        def emit_E_mm(mg, on, sub):
            sl = sub % 2
            dsl = (mg * NON + on) % 2

            def b_d_mm(t, on=on, sub=sub, sl=sl, dsl=dsl):
                em.wait("t", t, st["wd_tok"][dsl], st["hq_rdy"][sub], pbrd[2 + sl])
                for ic in range(c.IC):
                    ins = t.matmul(pb[2 + sl][:, 0:c.NS],
                                   hq_fv[:, ic, sub * 128:(sub + 1) * 128],
                                   wdsl_v[dsl][:, ic, :], start=(ic == 0),
                                   stop=(ic == c.IC - 1))
                tk = em.do("t", ins)
                if sub == SUBS - 1:
                    st["wd_use"][dsl] = tk
                st["hqfm_rd"] = tk
            block.tensor(b_d_mm)
            return em.tok("t")

        def make_ev(mg, on, sub, t_dm):
            sl = sub % 2
            par = mg % 2

            def run():
                def b_d_ev(v, on=on, sub=sub, sl=sl, par=par):
                    em.pre("v", v, t_dm, st["outb"][sl], st["gsl_rd"][sl])
                    em.do("v", v.scalar_tensor_tensor(
                        gsl[sl][:], pb[2 + sl][:, 0:c.NS], gam[:, sub:sub + 1],
                        res1v[par][:, sub, on * c.NS:(on + 1) * c.NS],
                        ALU.mult, ALU.add))
                block.vector(b_d_ev)
                pbrd[2 + sl] = em.tok("v")
                if on == NON - 1 and sub == SUBS - 1:
                    st["res1_rd"][par] = em.tok("v")
                t_de = em.tok("v")

                def b_out(sy, on=on, sub=sub, sl=sl):
                    em.wait("qd", sy, t_de)
                    t0 = mg * c.MEGA + sub * 128
                    st["outb"][sl] = em.dma_on(f"qo{sl}", sy.dma_start(
                        out=out[t0:t0 + 128, on * c.NS:(on + 1) * c.NS],
                        in_=gsl[sl][:]))
                block.sync(b_out)
            return run

        emit_wgu_load()
        emit_wgu_load()
        emit_wd_load()
        emit_wd_load()
        for sub in range(SUBS):
            emit_A_load(0, sub)
            emit_A_main1(0, sub)
            emit_A_main2(0, sub)
            emit_A_tp(0, sub)
        st["xq_rdy"][0] = em.tok("v")

        for mg in range(c.NMEGA):
            nxt = mg + 1

            for isl in range(NW):
                emit_BC_isl(mg, isl)
                if nxt < c.NMEGA:
                    if isl == 0:
                        emit_A_load(nxt, 0)
                    else:
                        s, r = divmod(isl - 1, 3)
                        if s < SUBS:
                            if r == 0:
                                emit_A_main1(nxt, s)
                            elif r == 1:
                                emit_A_main2(nxt, s)
                            else:
                                emit_A_tp(nxt, s)
                                if s + 1 < SUBS:
                                    emit_A_load(nxt, s + 1)
            if nxt < c.NMEGA:
                st["xq_rdy"][nxt % 2] = em.tok("v")

            for on in range(NON):
                prev = None
                for sub in range(SUBS):
                    if on == 0:
                        emit_D_sub(mg, sub)
                    t_dm = emit_E_mm(mg, on, sub)
                    if prev is not None:
                        prev()
                    prev = make_ev(mg, on, sub, t_dm)
                prev()
                emit_wd_load()

        if dbg:
            def b_dbg(sy):
                em.wait("qd", sy, *em.all_tokens())
                em.dma("qd", sy.dma_start(out=d_sv[:, 0:3], in_=sv_m[:]))
                em.dma("qd", sy.dma_start(out=d_sv[:, 3:6], in_=mv_m[:]))
                em.dma("qd", sy.dma_start(out=d_wgt, in_=wgt_s[:, 0:512]))
                em.dma("qd", sy.dma_start(out=d_wdt, in_=wdt_s[0:512, :]))
                em.dma("qd", sy.dma_start(out=d_hh, in_=hh[:, 0:c.INNER]))
                em.dma("qd", sy.dma_start(out=d_r1, in_=res1b[0][:, 0:c.DM]))
                em.dma("qd", sy.dma_start(out=d_al[:, 0:c.MEGA // 128], in_=alg[:, 0:c.MEGA // 128]))
                em.dma("qd", sy.dma_start(out=d_al[:, c.MEGA // 128:2 * (c.MEGA // 128)], in_=alu_[:, 0:c.MEGA // 128]))
                em.dma("qd", sy.dma_start(out=d_al[:, 2 * (c.MEGA // 128):], in_=gam[:]))
                em.dma("qd", sy.dma_start(out=d_lnx, in_=lnx_s))
                for h in range(c.NHL):
                    em.dma("qd", sy.dma_start(out=d_qcs[h * 64:(h + 1) * 64, :],
                                              in_=qs_s[h * 128:h * 128 + 64, :]))
                    em.dma("qd", sy.dma_start(out=d_qrs[h * 32:(h + 1) * 32, :],
                                              in_=qs_s[h * 128 + 64:h * 128 + 96, :]))
                for j in range(NRS):
                    em.dma("qd", sy.dma_start(
                        out=d_aop.rearrange("h o (j t) -> h o j t", t=c.SLOT)[:, :, j, :],
                        in_=aop_p[j]))
                    em.dma("qd", sy.dma_start(
                        out=d_aog[:, j * c.SLOT:(j + 1) * c.SLOT], in_=aog_p[j]))
                em.dma("qd", sy.dma_start(out=d_cq, in_=cq_s[:, 0:512]))
            block.sync(b_dbg)

            def b_dbg2(g):
                em.wait("gd", g, *em.all_tokens())
                em.dma("gd", g.dma_start(out=d_maxk, in_=maxk[:]))
            block.gpsimd(b_dbg2)

        barrier(em, block)
        p3_es.close()

    return nc


def _shard_inputs(c, inputs):
    x = np.ascontiguousarray(inputs["L0_attn_x"])
    W = {k: np.ascontiguousarray(v) for k, v in inputs.items()}
    maps = []
    for core in range(2 * c.n_pairs):
        p, hc = core // 2, core % 2
        m = {
            "x_seq": x[p],
            "x_own": x[p, hc * c.OWN:(hc + 1) * c.OWN],
            "wdq": W["L0_attn_W_DQ"],
            "wuq": W["L0_attn_W_UQ"][:, hc * c.YD:(hc + 1) * c.YD],
            "wqr": W["L0_attn_W_QR"][:, hc * c.NHL * 32:(hc + 1) * c.NHL * 32],
            "wdkv": W["L0_attn_W_DKV"],
            "wuk": W["L0_attn_W_UK"][:, hc * c.YD:(hc + 1) * c.YD],
            "wuv": W["L0_attn_W_UV"][:, hc * c.YD:(hc + 1) * c.YD],
            "wkr": W["L0_attn_W_KR"],
            "wo": W["L0_attn_W_O"][hc * c.YD:(hc + 1) * c.YD, :],
            "wg": W["L0_ffn_w_gate"],
            "wu": W["L0_ffn_w_up"],
            "wd": W["L0_ffn_w_down"],
        }
        maps.append({k: np.ascontiguousarray(v) for k, v in m.items()})
    return maps


def kernel(**inputs):
    c = FULL
    nc = build_nc(c)
    in_maps = _shard_inputs(c, inputs)
    res = run_bass_kernel_spmd(nc, in_maps, core_ids=list(range(2 * c.n_pairs)))
    B = c.n_pairs
    outp = np.empty((B, c.T, c.DM), dtype=np.float32)
    for core in range(2 * B):
        p, hc = core // 2, core % 2
        outp[p, hc * c.OWN:(hc + 1) * c.OWN] = res.results[core]["out"]
    return outp
